# version 12
# baseline (speedup 1.0000x reference)
import numpy as np
import ml_dtypes
from contextlib import ExitStack

import concourse.bass as bass
import concourse.mybir as mybir
from concourse.bass_utils import run_bass_kernel_spmd

F32 = mybir.dt.float32
BF16 = mybir.dt.bfloat16
AF = mybir.ActivationFunctionType
ALU = mybir.AluOpType
AX = mybir.AxisListType

N = 4096
D = 1024
T = N // 128
NH = 16
HD = 64
EPS = 1e-5
BF = ml_dtypes.bfloat16

ENGINES = ("tensor", "vector", "scalar", "gpsimd", "sync")


def _l0_attn_rope_tables():
    GRID = 16
    TOKF = GRID * GRID
    ids = np.arange(N)
    f_ids = ids // TOKF
    rem = ids - TOKF * f_ids
    h_ids = rem // GRID
    w_ids = rem - GRID * h_ids
    d_dim = 20
    half = d_dim // 2
    omega = np.arange(half, dtype=np.float64) / (d_dim / 2.0)
    inv = 1.0 / 10000.0 ** omega
    cos = np.ones((N, 64), dtype=np.float64)
    sin = np.zeros((N, 64), dtype=np.float64)
    for c, pos in enumerate((f_ids, h_ids, w_ids)):
        freq = pos[:, None].astype(np.float64) * inv[None, :]
        cs = np.concatenate([np.cos(freq), np.cos(freq)], axis=-1)
        sn = np.concatenate([np.sin(freq), np.sin(freq)], axis=-1)
        cos[:, c * 20:(c + 1) * 20] = cs
        sin[:, c * 20:(c + 1) * 20] = sn
    return cos.astype(np.float32), sin.astype(np.float32)


def _l0_ffn_rope_tables():
    inv_freq = 1.0 / 10000.0 ** (np.arange(0, HD, 2, dtype=np.float64) / HD)
    t = np.arange(N, dtype=np.float64)
    freqs = t[:, None] * inv_freq[None, :]
    emb = np.concatenate([freqs, freqs], axis=-1)
    return np.cos(emb).astype(np.float32), np.sin(emb).astype(np.float32)


def _l1f_rotary_tables():
    DIM_HEAD, WINDOW = 64, 256
    half = np.arange(0, DIM_HEAD, 2, dtype=np.float64)
    inv_freq = 1.0 / 10000.0 ** (half / DIM_HEAD)
    t = np.arange(N, dtype=np.float64)
    freqs = t[:, None] * inv_freq[None, :]
    freqs = np.concatenate([freqs, freqs], axis=-1)
    base_scale = (half + 0.4 * DIM_HEAD) / (1.4 * DIM_HEAD)
    power = (t - N // 2) / (WINDOW // 2)
    scale = base_scale[None, :] ** power[:, None]
    scale = np.concatenate([scale, scale], axis=-1)
    cos, sin = np.cos(freqs), np.sin(freqs)
    return (
        (cos * scale).astype(np.float32), (sin * scale).astype(np.float32),
        (cos / scale).astype(np.float32), (sin / scale).astype(np.float32),
    )


def _l1f_masks():
    i = np.arange(128)[:, None]
    j = np.arange(128)[None, :]
    NEG = np.float32(-1e30)
    m_le = np.where(j > i, NEG, 0.0).astype(np.float32)
    m_ge = np.where(j < i, NEG, 0.0).astype(np.float32)
    return m_le.astype(BF), m_ge.astype(BF)


def host_consts():
    c = {}
    c["identity_bf"] = np.eye(128, dtype=BF)
    c["identity_f32"] = np.eye(128, dtype=np.float32)
    c["ones_col_f32"] = np.ones((1, 128), dtype=np.float32)
    cq, sq = _l0_attn_rope_tables()
    c["l0a_cos"], c["l0a_sin"] = cq, sq
    cf, sf = _l0_ffn_rope_tables()
    c["l0f_cos"], c["l0f_sin"] = cf, sf
    tcs, tss, tci, tsi = _l1f_rotary_tables()
    c["l1f_tcs"], c["l1f_tss"], c["l1f_tci"], c["l1f_tsi"] = tcs, tss, tci, tsi
    mle, mge = _l1f_masks()
    c["l1f_mle"], c["l1f_mge"] = mle, mge
    return c


class Op:
    __slots__ = ("eng", "emit", "reads", "writes", "dma_key", "idx",
                 "inc_sem", "inc_val", "tag", "pe_chain", "block")

    def __init__(self, eng, emit, reads, writes, dma_key, idx, tag, pe_chain):
        self.eng = eng
        self.emit = emit
        self.reads = reads
        self.writes = writes
        self.dma_key = dma_key
        self.idx = idx
        self.inc_sem = None
        self.inc_val = None
        self.tag = tag
        self.pe_chain = pe_chain
        self.block = None


_PROG = [None]


def _nm(x):
    if isinstance(x, str):
        return x
    n = getattr(x, "name", None)
    if n is None:
        raise ValueError(f"can't get name of {x!r}")
    n = n if isinstance(n, str) else str(n)
    if n == "ARENA" and _PROG[0] is not None:
        dt_size = 2 if x.dtype == BF16 else 4
        f32_off = (x.offset * dt_size // 4) % ARENA_F32
        for (s, e2, bn) in _PROG[0]._arena_map:
            if s <= f32_off < e2:
                return bn
        raise ValueError(f"ARENA ap at f32 offset {f32_off} not in any live abuf")
    return n


class Buf:

    def __init__(self, ap, name):
        self.ap = ap
        self.name = name
        self.shape = ap.shape
        self.dtype = ap.dtype

    def __getitem__(self, sl):
        return self.ap[sl]


ARENA_F32 = 51712


class Prog:
    def __init__(self, nc):
        self.nc = nc
        self.es = ExitStack()
        self.ops = []
        self.pending = []
        self.last_w = {}
        self.readers = {}
        self.sb_tensors = {}
        self.dram_aps = {}
        self.sems = {}
        self.sem_counts = {}
        self.n_blocks = 0
        self._psum_pool = None
        self._arena = None
        self._arena_off = 0
        self.abufs = {}
        self._fenced = {}
        self._arena_map = []
        self._key_bind = {}
        self._dsem_rr = 0
        _PROG[0] = self

    def _arena_t(self):
        if self._arena is None:
            self._arena = self.es.enter_context(
                self.nc.sbuf_tensor("ARENA", [128, ARENA_F32], F32))
        return self._arena

    def abuf(self, name, shape, dtype):
        if name in self.abufs:
            b = self.abufs[name]
            assert tuple(b.shape) == tuple(shape) and b.dtype == dtype, name
            return b
        assert len(shape) == 2
        par, n = shape
        f32len = (n * (2 if dtype == BF16 else 4) + 3) // 4
        f32len = (f32len + 7) & ~7
        a = self._arena_t()
        assert self._arena_off + f32len <= ARENA_F32, \
            f"arena full allocating {name}: {self._arena_off}+{f32len}"
        sl = a[:, self._arena_off:self._arena_off + f32len]
        self._arena_map.append((self._arena_off, self._arena_off + f32len, name))
        self._arena_off += f32len
        ap = sl.bitcast(dtype) if dtype != F32 else sl
        ap = ap[0:par, 0:n]
        b = Buf(ap, name)
        self.abufs[name] = b
        return b

    def arena_mark(self):
        return (self._arena_off, set(self.abufs.keys()))

    def arena_release(self, mark):
        off, keep = mark
        self.dma_fence()
        self.flush("arena_release")
        for k in list(self.abufs.keys()):
            if k not in keep:
                del self.abufs[k]
                self.last_w.pop(k, None)
                self.readers.pop(k, None)
        self._arena_map = [m for m in self._arena_map if m[2] in self.abufs]
        self._arena_off = off

    def dma_fence(self):
        targets = [(s, c) for s, c in self.sem_counts.items()
                   if s.startswith("dk_") and c > self._fenced.get(s, 0)]
        if not targets:
            return

        def emit(e):
            for s, c in targets:
                e.wait_ge(self.sems[s], c)
            return None
        self.op("sync", emit, tag="dma_fence")
        for s, c in targets:
            self._fenced[s] = c
        self._key_bind.clear()

    def sem(self, name):
        if name not in self.sems:
            self.sems[name] = self.es.enter_context(self.nc.semaphore(name))
            self.sem_counts[name] = 0
        return self.sems[name]

    def sbuf(self, name, shape, dtype):
        if name not in self.sb_tensors:
            self.sb_tensors[name] = self.es.enter_context(
                self.nc.sbuf_tensor(name, list(shape), dtype))
        return self.sb_tensors[name]

    def psum_bank(self, i):
        if self._psum_pool is None:
            self._pp = [
                self.es.enter_context(self.nc.psum_tensor(f"pp{j}", [128, 1024], F32))
                for j in range(4)]
            self._psum_pool = [
                Buf(self._pp[j // 2][:, (j % 2) * 512:(j % 2 + 1) * 512], f"psb{j}")
                for j in range(8)]
        return self._psum_pool[i]

    def psum_pair(self, j):
        self.psum_bank(0)
        return Buf(self._pp[j][:], f"ppair{j}")

    def constcol(self, value):
        key = f"cc_{value!r}"
        if key not in self.sb_tensors:
            t = self.sbuf(key, [128, 1], F32)
            self.op("vector", lambda e: e.memset(t[:], value), writes=[t], tag=key)
        return self.sb_tensors[key]

    def din(self, name, shape, dtype=F32):
        if name not in self.dram_aps:
            self.dram_aps[name] = self.nc.dram_tensor(
                name, list(shape), dtype, kind="ExternalInput").ap()
        return self.dram_aps[name]

    def dout(self, name, shape, dtype=F32):
        if name not in self.dram_aps:
            self.dram_aps[name] = self.nc.dram_tensor(
                name, list(shape), dtype, kind="ExternalOutput").ap()
        return self.dram_aps[name]

    def dscratch(self, name, shape, dtype, io=None):
        if name not in self.dram_aps:
            kind = {None: "Internal", "in": "ExternalInput", "out": "ExternalOutput"}[io]
            self.dram_aps[name] = self.nc.dram_tensor(
                name, list(shape), dtype, kind=kind).ap()
        return self.dram_aps[name]

    def op(self, eng, emit, reads=(), writes=(), tag=None, pe_chain=False):
        o = Op(eng, emit, [_nm(r) for r in reads], [_nm(w) for w in writes],
               None, len(self.ops), tag, pe_chain)
        self.ops.append(o)
        self.pending.append(o)
        return o

    def _dma_sem(self, eng, key):
        kk = ("g" if eng == "gpsimd" else ("s" if eng == "sync" else "a"), key)
        if kk not in self._key_bind:
            assert len(self._key_bind) < 40, f"dma sem pool exhausted at {kk}"
            free = None
            used = set(self._key_bind.values())
            for i in range(40):
                cand = f"dk_{kk[0]}{(self._dsem_rr + i) % 40}"
                if cand not in used:
                    free = cand
                    break
            self._dsem_rr += 1
            self._key_bind[kk] = free
            self.sem(free)
        return self._key_bind[kk]

    def dma(self, eng, out_ap, in_ap, key, tag=None, extra_reads=(), **kw):
        sem_key = self._dma_sem(eng, key)

        def emit(e):
            return e.dma_start(out_ap, in_ap, **kw)

        o = Op(eng, emit, [_nm(in_ap)] + [_nm(r) for r in extra_reads],
               [_nm(out_ap)], sem_key, len(self.ops), tag, False)
        self.ops.append(o)
        self.pending.append(o)
        return o

    def mm(self, emit, reads=(), writes=(), tag=None):
        return self.op("tensor", emit, reads, writes, tag, pe_chain=True)

    def _lane(self, o):
        return o.dma_key if o.dma_key is not None else o.eng

    def _dep_ops(self, o):
        deps = []
        for r in o.reads:
            deps.extend(self.last_w.get(r, {}).values())
        for wn in o.writes:
            deps.extend(self.last_w.get(wn, {}).values())
            deps.extend(self.readers.get(wn, {}).values())
        out = []
        seen = set()
        for d in deps:
            if d is o or d.idx in seen:
                continue
            seen.add(d.idx)
            if o.pe_chain and d.pe_chain:
                continue
            if o.dma_key is not None and d.dma_key == o.dma_key:
                raw = any(r in d.writes for r in o.reads)
                war = any(w in d.reads for w in o.writes)
                if not raw and not war:
                    continue
            out.append(d)
        return out

    def _update_state(self, o):
        lane = self._lane(o)
        for wn in o.writes:
            self.last_w.setdefault(wn, {})
            self.last_w[wn] = {lane: o}
            self.readers[wn] = {}
        for r in o.reads:
            if r in self.last_w or True:
                self.readers.setdefault(r, {})[lane] = o

    def flush(self, name=None):
        if not self.pending:
            return
        ops = self.pending
        self.pending = []
        blkno = self.n_blocks
        self.n_blocks += 1
        for o in ops:
            o.block = blkno

        deps_of = {}
        for o in ops:
            deps = self._dep_ops(o)
            kept = []
            for d in deps:
                if d.block is not None and d.block < blkno and d.dma_key is None:
                    continue
                kept.append(d)
            deps_of[o.idx] = kept
            for d in kept:
                if d.inc_sem is None:
                    if d.dma_key is not None:
                        d.inc_sem = d.dma_key
                        self.sem(d.dma_key)
                    else:
                        d.inc_sem = "se_" + d.eng
                        self.sem(d.inc_sem)
            self._update_state(o)

        for o in ops:
            if o.dma_key is not None:
                o.inc_sem = o.dma_key
                self.sem(o.dma_key)

        lanes = {}
        for o in ops:
            if o.inc_sem is not None:
                lanes.setdefault(o.inc_sem, []).append(o)
        for sname, lst in lanes.items():
            lst.sort(key=lambda o: o.idx)
            step = 16 if sname.startswith("dk_") else 1
            base = self.sem_counts[sname]
            for i, o in enumerate(lst):
                o.inc_val = base + step * (i + 1)
            self.sem_counts[sname] = base + step * len(lst)

        per_eng = {e: [o for o in ops if o.eng == e] for e in ENGINES}
        prog = self

        with self.nc.Block() as blk:
            def make(eng_name):
                def body(e):
                    waited = {}
                    for o in per_eng[eng_name]:
                        for d in deps_of[o.idx]:
                            sem, val = d.inc_sem, d.inc_val
                            if d.eng == eng_name and d.dma_key is None and o.dma_key is None and d.pe_chain and o.pe_chain:
                                continue
                            if waited.get(sem, 0) >= val:
                                continue
                            e.wait_ge(prog.sems[sem], val)
                            waited[sem] = val
                        instr = o.emit(e)
                        if o.inc_sem is not None:
                            if instr is None:
                                raise RuntimeError(f"op {o.tag} needs inc, emitted None")
                            step = 16 if o.dma_key is not None else 1
                            instr.then_inc(prog.sems[o.inc_sem], step)
                return body
            for e in ENGINES:
                if per_eng[e]:
                    getattr(blk, e)(make(e))


def ln_tile(p, xt_ap, out_ap, scr, gamma_rep=None, beta_rep=None, out2_ap=None):
    sq, ssq, nsum, nmean, m2, var, sqv, rstd, nmr = (
        scr["sq"], scr["ssq"], scr["nsum"], scr["nmean"], scr["m2"], scr["var"],
        scr["sqv"], scr["rstd"], scr["nmr"])
    xn_name = _nm(xt_ap)
    p.op("scalar", lambda e: e.activation(sq[:], xt_ap, AF.Square, accum_out=ssq[:]),
         reads=[xn_name], writes=[sq, ssq], tag="ln.sq")
    p.op("vector", lambda e: e.tensor_reduce(nsum[:], xt_ap, AX.X, ALU.add, negate=True),
         reads=[xn_name], writes=[nsum], tag="ln.sum")
    p.op("vector", lambda e: e.tensor_scalar(nmean[:], nsum[:], 1.0 / D, None, ALU.mult),
         reads=[nsum], writes=[nmean], tag="ln.mean")
    p.op("vector", lambda e: e.tensor_tensor(m2[:], nmean[:], nmean[:], ALU.mult),
         reads=[nmean], writes=[m2], tag="ln.m2")
    p.op("vector", lambda e: e.tensor_scalar(var[:], ssq[:], 1.0 / D, None, ALU.mult),
         reads=[ssq], writes=[var], tag="ln.var0")
    p.op("vector", lambda e: e.tensor_tensor(var[:], var[:], m2[:], ALU.subtract),
         reads=[var, m2], writes=[var], tag="ln.var")
    epsc = p.constcol(EPS)
    p.op("scalar", lambda e: e.activation(sqv[:], var[:], AF.Sqrt, bias=epsc[:]),
         reads=[var, epsc], writes=[sqv], tag="ln.sqrt")
    p.op("vector", lambda e: e.reciprocal(rstd[:], sqv[:]),
         reads=[sqv], writes=[rstd], tag="ln.rstd")
    p.op("vector", lambda e: e.tensor_tensor(nmr[:], nmean[:], rstd[:], ALU.mult),
         reads=[nmean, rstd], writes=[nmr], tag="ln.nmr")
    if gamma_rep is None:
        p.op("vector", lambda e: e.tensor_scalar(out_ap, xt_ap, rstd[:], nmr[:],
                                                 ALU.mult, ALU.add),
             reads=[xn_name, nmr, rstd], writes=[_nm(out_ap)], tag="ln.out")
        if out2_ap is not None:
            p.op("vector", lambda e: e.tensor_scalar(out2_ap, xt_ap, rstd[:], nmr[:],
                                                     ALU.mult, ALU.add),
                 reads=[xn_name, nmr, rstd], writes=[_nm(out2_ap)], tag="ln.out2")
    else:
        xn_t = scr["xn"]
        p.op("vector", lambda e: e.tensor_scalar(xn_t[:], xt_ap, rstd[:], nmr[:],
                                                 ALU.mult, ALU.add),
             reads=[xn_name, nmr, rstd], writes=[xn_t], tag="ln.norm")
        p.op("vector", lambda e: e.tensor_tensor(xn_t[:], xn_t[:], gamma_rep[:], ALU.mult),
             reads=[xn_t, gamma_rep], writes=[xn_t], tag="ln.g")
        p.op("vector", lambda e: e.tensor_tensor(out_ap, xn_t[:], beta_rep[:], ALU.add),
             reads=[xn_t, beta_rep], writes=[_nm(out_ap)], tag="ln.b")
        if out2_ap is not None:
            p.op("vector", lambda e: e.tensor_copy(out2_ap, out_ap),
                 reads=[_nm(out_ap)], writes=[_nm(out2_ap)], tag="ln.out2")


def transpose128(p, src_ap, dst_ap, ps, idb, tag="tr"):
    p.mm(lambda e: e.transpose(ps[:src_ap.shape[1], :src_ap.shape[0]], src_ap, idb[:]),
         reads=[_nm(src_ap), idb], writes=[ps], tag=tag + ".t")
    p.op("scalar", lambda e: e.copy(dst_ap, ps[:src_ap.shape[1], :src_ap.shape[0]]),
         reads=[ps], writes=[_nm(dst_ap)], tag=tag + ".ev")


def load_weight_bf16(p, dram_ap, sb_name, kdim, ndim):
    kt = (kdim + 127) // 128
    sb = p.abuf(sb_name, [128, kt * ndim], BF16)
    src = dram_ap.rearrange("(a p) n -> p a n", p=128) if kdim >= 128 else None
    if kdim >= 128:
        assert kdim % 128 == 0
        p.dma("gpsimd", sb[:].rearrange("p (a n) -> p a n", n=ndim), src,
              key=sb_name, tag="ldw." + sb_name)
    else:
        p.dma("gpsimd", sb[:kdim, :], dram_ap, key=sb_name, tag="ldw." + sb_name)
    return sb


def bcast_vec(p, dram_vec, sb_name, width, eng="sync"):
    sb = p.abuf(sb_name, [128, width], F32)
    src = dram_vec.rearrange("(o n) -> o n", o=1).broadcast_to((128, width))
    p.dma(eng, sb[:], src, key=sb_name, tag="bc." + sb_name)
    return sb


def emit_full_attn(p, pfx, x_src, resid_src, out_dram, W, bvecs, rope_cos, rope_sin,
                   rope_style="interleaved", xln_store=None, tail_ln=None):
    _amark = p.arena_mark()
    idb = p.sbuf("identity_bf", [128, 128], BF16)
    idf = p.sbuf("identity_f32", [128, 128], F32)
    onec = p.sbuf("ones_col", [1, 128], F32)
    onecb = p.sbuf("ones_col_bf", [1, 128], BF16)

    KT = p.abuf(pfx + "KT", [128, 8 * N], BF16)
    Vp = p.abuf(pfx + "Vp", [128, T * 1040], BF16)
    maxq2 = p.abuf(pfx + "maxq2", [128, 16], F32)
    maxk2 = p.abuf(pfx + "maxk2", [128, 16], F32)
    negC = p.abuf(pfx + "negC", [128, 1], F32)
    _mloc = p.arena_mark()
    qT_hbm = p.dscratch(pfx + "qT", [T * 8 * 128, 128], BF16)
    xnT_hbm = p.dscratch(pfx + "xnT", [T * 8 * 128, 128], BF16)

    scr_bf = p.sbuf("scr_bf", [128, D], BF16)
    ln_scr = {k: p.sbuf("ln_" + k, [128, 1], F32)
              for k in ["ssq", "nsum", "nmean", "m2", "var", "sqv", "rstd", "nmr"]}
    ln_scr["sq"] = scr_bf

    ps_tr = [p.psum_bank(6), p.psum_bank(7)]

    def tr128(src_ap, dst_ap, par, tag, evac_eng="scalar", ps_list=None):
        ps = (ps_list or ps_tr)[par]
        psv = ps[:].bitcast(BF16)
        p.mm(lambda e: e.transpose(psv[:, 0:128], src_ap, idb[:]),
             reads=[_nm(src_ap), idb], writes=[ps], tag=tag + ".t")
        p.op(evac_eng, lambda e: e.copy(dst_ap, psv[:, 0:128]) if evac_eng == "scalar"
             else e.tensor_copy(dst_ap, psv[:, 0:128]),
             reads=[ps], writes=[_nm(dst_ap)], tag=tag + ".ev")

    xt = [p.abuf(pfx + f"xt{i}", [128, D], F32) for i in range(2)]
    xnb = [p.abuf(pfx + f"xnb{i}", [128, D], BF16) for i in range(2)]
    xnTt = [p.abuf(pfx + f"xnTt{i}", [128, D], BF16) for i in range(2)]
    xnf = [p.abuf(pfx + f"xnf{i}", [128, D], F32) for i in range(2)] if xln_store is not None else None
    p.op("vector", lambda e: e.memset(Vp[:], 1.0), writes=[Vp], tag="vp.ones")
    p.op("vector", lambda e: e.memset(maxq2[:], 0.0), writes=[maxq2], tag="mq0")
    p.op("vector", lambda e: e.memset(maxk2[:], 0.0), writes=[maxk2], tag="mk0")
    for t in range(T):
        par = t % 2
        p.dma("sync", xt[par][:], x_src[t * 128:(t + 1) * 128, :], key=pfx + f"xt{par}",
              tag=f"ld.x{t}")
        ln_tile(p, xt[par][:], xnb[par][:], ln_scr)
        if xln_store is not None:
            p.op("vector", lambda e, par=par: e.tensor_scalar(
                xnf[par][:], xt[par][:], ln_scr["rstd"][:], ln_scr["nmr"][:], ALU.mult, ALU.add),
                 reads=[xt[par], ln_scr["nmr"], ln_scr["rstd"]], writes=[xnf[par]], tag="ln.f32")
            p.dma("scalar", xln_store[t * 128:(t + 1) * 128, :], xnf[par][:],
                  key=pfx + f"xlnst{par}", tag=f"st.xln{t}")
        for j in range(8):
            tr128(xnb[par][:, j * 128:(j + 1) * 128], xnTt[par][:, j * 128:(j + 1) * 128],
                  j % 2, f"xnT{t}.{j}")
        p.dma("sync", xnT_hbm[t * 1024:(t + 1) * 1024, :].rearrange("(j q) b -> q j b", j=8),
              xnTt[par][:].rearrange("p (j b) -> p j b", b=128),
              key=pfx + f"xnTst{par}", tag=f"st.xnT{t}")
    p.flush(pfx + "pass1a")

    xnTl = [p.abuf(pfx + f"xnTl{i}", [128, D], BF16) for i in range(2)]
    qf = p.abuf(pfx + "qf", [128, D], BF16)
    rot = p.abuf(pfx + "rot", [128, D], BF16)
    t1b = p.abuf(pfx + "t1b", [128, D], BF16)
    qb = p.abuf(pfx + "qb", [128, D], BF16)
    nr2 = p.abuf(pfx + "nr2", [128, 16], F32)
    qTt = p.abuf(pfx + "qTt", [128, 8 * 128], BF16)
    ps_p = [p.psum_bank(0), p.psum_bank(1)]

    def rope_apply(src_bf, toff, out_bf, cs, snt):
        cs_v = cs[:, 0:64].unsqueeze(1).broadcast_to((128, 16, 64))
        sn_v = snt[:, 0:64].unsqueeze(1).broadcast_to((128, 16, 64))
        if rope_style == "interleaved":
            s3 = src_bf[:].rearrange("p (x two) -> p x two", two=2)
            r3 = rot[:].rearrange("p (x two) -> p x two", two=2)
            p.op("vector", lambda e: e.tensor_scalar(r3[:, :, 0:1], s3[:, :, 1:2], -1.0, None, ALU.mult),
                 reads=[src_bf], writes=[rot], tag="rope.r0")
            p.op("vector", lambda e: e.tensor_copy(r3[:, :, 1:2], s3[:, :, 0:1]),
                 reads=[src_bf], writes=[rot], tag="rope.r1")
        else:
            s3 = src_bf[:].rearrange("p (h x) -> p h x", x=64)
            r3 = rot[:].rearrange("p (h x) -> p h x", x=64)
            p.op("vector", lambda e: e.tensor_scalar(r3[:, :, 0:32], s3[:, :, 32:64], -1.0, None, ALU.mult),
                 reads=[src_bf], writes=[rot], tag="rope.r0")
            p.op("vector", lambda e: e.tensor_copy(r3[:, :, 32:64], s3[:, :, 0:32]),
                 reads=[src_bf], writes=[rot], tag="rope.r1")
        p.op("vector", lambda e: e.tensor_tensor(
            t1b[:].rearrange("p (h d) -> p h d", d=64),
            src_bf[:].rearrange("p (h d) -> p h d", d=64), cs_v, ALU.mult),
             reads=[src_bf, cs], writes=[t1b], tag="rope.c")
        p.op("vector", lambda e: e.tensor_tensor(
            rot[:].rearrange("p (h d) -> p h d", d=64),
            rot[:].rearrange("p (h d) -> p h d", d=64), sn_v, ALU.mult),
             reads=[rot, snt], writes=[rot], tag="rope.s")
        p.op("vector", lambda e: e.tensor_tensor(out_bf[:], t1b[:], rot[:], ALU.add),
             reads=[t1b, rot], writes=[out_bf], tag="rope.add")

    for kind in ("k", "v", "q"):
        Wsb = load_weight_bf16(p, W[kind], pfx + "Wp", D, D)
        brow = p.abuf(pfx + "brow", [1, D], BF16)
        p.dma("gpsimd", brow[:], bvecs[kind].rearrange("(o n) -> o n", o=1), key=pfx + "brow",
              tag="ld.brow")
        csl = [p.abuf(pfx + f"csl{i}", [128, 64], F32) for i in range(2)]
        snl = [p.abuf(pfx + f"snl{i}", [128, 64], F32) for i in range(2)]
        p.flush(pfx + f"ldw.{kind}")
        for t in range(T):
            par = t % 2
            p.dma("sync", xnTl[par][:].rearrange("p (j b) -> p j b", b=128),
                  xnT_hbm[t * 1024:(t + 1) * 1024, :].rearrange("(j q) b -> q j b", j=8),
                  key=pfx + f"xnTl{par}", tag=f"ld.xnT{t}")
            if kind != "v":
                p.dma("sync", csl[par][:], rope_cos[t * 128:(t + 1) * 128, :],
                      key=pfx + f"csl{par}", tag=f"ld.cs{t}")
                p.dma("sync", snl[par][:], rope_sin[t * 128:(t + 1) * 128, :],
                      key=pfx + f"snl{par}", tag=f"ld.sn{t}")
            for c in range(2):
                for j in range(8):
                    p.mm(lambda e, c=c, j=j, par=par, Wsb=Wsb: e.matmul(
                        ps_p[c][:],
                        xnTl[par][:, j * 128:(j + 1) * 128],
                        Wsb[:, j * D + c * 512: j * D + (c + 1) * 512],
                        start=(j == 0), stop=False),
                        reads=[xnTl[par], Wsb], writes=[ps_p[c]], tag=f"prj.{kind}{t}.{c}")
                p.mm(lambda e, c=c, brow=brow: e.matmul(
                    ps_p[c][:], onecb[:], brow[:, c * 512:(c + 1) * 512],
                    start=False, stop=True),
                    reads=[onecb, brow], writes=[ps_p[c]], tag=f"prjb.{kind}{t}.{c}")
            if kind == "v":
                vdst = Vp[:, t * 1040: (t + 1) * 1040].rearrange("p (h c) -> p h c", c=65)
                for c in range(2):
                    p.op("scalar", lambda e, c=c, vdst=vdst: e.copy(
                        vdst[:, c * 8:(c + 1) * 8, 0:64],
                        ps_p[c][:].rearrange("p (h d) -> p h d", d=64)),
                         reads=[ps_p[c]], writes=[Vp], tag=f"vev{t}.{c}")
            else:
                for c in range(2):
                    p.op("scalar", lambda e, c=c: e.copy(qf[:, c * 512:(c + 1) * 512], ps_p[c][:]),
                         reads=[ps_p[c]], writes=[qf], tag=f"pev{t}.{c}")
                rope_apply(qf, t, qb, csl[par], snl[par])
                acc = maxq2 if kind == "q" else maxk2
                p.op("vector", lambda e: e.tensor_tensor(scr_bf[:], qb[:], qb[:], ALU.mult),
                     reads=[qb], writes=[scr_bf], tag="nrm.sq")
                p.op("vector", lambda e: e.tensor_reduce(
                    nr2[:], scr_bf[:].rearrange("p (h d) -> p h d", d=64), AX.X, ALU.add),
                     reads=[scr_bf], writes=[nr2], tag="nrm.red")
                p.op("vector", lambda e, acc=acc: e.tensor_tensor(acc[:], acc[:], nr2[:], ALU.max),
                     reads=[acc, nr2], writes=[acc], tag="nrm.max")
                if kind == "q":
                    for j in range(8):
                        tr128(qb[:, j * 128:(j + 1) * 128], qTt[:, j * 128:(j + 1) * 128],
                              j % 2, f"qT{t}.{j}")
                    p.dma("sync", qT_hbm[t * 1024:(t + 1) * 1024, :].rearrange("(j q) b -> q j b", j=8),
                          qTt[:].rearrange("p (j b) -> p j b", b=128),
                          key=pfx + "qTst", tag=f"st.qT{t}")
                else:
                    for j in range(8):
                        tr128(qb[:, j * 128:(j + 1) * 128],
                              KT[:, j * N + t * 128: j * N + (t + 1) * 128], j % 2, f"kT{t}.{j}")
        p.flush(pfx + f"proj.{kind}")

    ps_s = p.psum_bank(7)
    gq = p.abuf(pfx + "gq", [128, 1], F32)
    gk = p.abuf(pfx + "gk", [128, 1], F32)
    gqk = p.abuf(pfx + "gqk", [128, 2], F32)
    g1 = p.abuf(pfx + "g1", [1, 128], F32)
    gsc = p.abuf(pfx + "gsc", [1, 1], F32)
    p.op("vector", lambda e: e.tensor_reduce(gq[:], maxq2[:], AX.X, ALU.max),
         reads=[maxq2], writes=[gq], tag="gC.rq")
    p.op("vector", lambda e: e.tensor_reduce(gk[:], maxk2[:], AX.X, ALU.max),
         reads=[maxk2], writes=[gk], tag="gC.rk")
    p.op("vector", lambda e: e.tensor_copy(gqk[:, 0:1], gq[:]), reads=[gq], writes=[gqk], tag="gC.c0")
    p.op("vector", lambda e: e.tensor_copy(gqk[:, 1:2], gk[:]), reads=[gk], writes=[gqk], tag="gC.c1")
    gqkb = p.abuf(pfx + "gqkb", [128, 2], BF16)
    p.op("vector", lambda e: e.tensor_copy(gqkb[:], gqk[:]), reads=[gqk], writes=[gqkb], tag="gC.cb")
    psv = ps_s[:].bitcast(BF16)
    p.mm(lambda e: e.transpose(psv[0:2, 0:128], gqkb[:], idb[:]),
         reads=[gqkb, idb], writes=[ps_s], tag="gC.t")
    gq_r = p.abuf(pfx + "gq_r", [2, 128], F32)
    gk_r = p.abuf(pfx + "gk_r", [1, 128], F32)
    p.op("scalar", lambda e: e.copy(gq_r[:], psv[0:2, 0:128]), reads=[ps_s], writes=[gq_r], tag="gC.e0")
    p.dma("sync", gk_r[0:1, :], gq_r[1:2, :], key="gkr", tag="gC.mv")
    p.op("vector", lambda e: e.tensor_reduce(g1[0:1, 0:1], gq_r[0:1, :], AX.X, ALU.max),
         reads=[gq_r], writes=[g1], tag="gC.m0")
    p.op("vector", lambda e: e.tensor_reduce(g1[0:1, 1:2], gk_r[0:1, :], AX.X, ALU.max),
         reads=[gk_r], writes=[g1], tag="gC.m1")
    p.op("scalar", lambda e: e.activation(g1[:, 2:4], g1[:, 0:2], AF.Sqrt),
         reads=[g1], writes=[g1], tag="gC.sqrt")
    p.op("vector", lambda e: e.tensor_tensor(gsc[:], g1[:, 2:3], g1[:, 3:4], ALU.mult),
         reads=[g1], writes=[gsc], tag="gC.mul")
    p.op("vector", lambda e: e.tensor_scalar(gsc[:], gsc[:], -0.125, None, ALU.mult),
         reads=[gsc], writes=[gsc], tag="gC.neg")
    p.mm(lambda e: e.matmul(ps_s[:, 0:1], onec[:], gsc[:], start=True, stop=True),
         reads=[onec, gsc], writes=[ps_s], tag="gC.bc")
    p.op("scalar", lambda e: e.copy(negC[:], ps_s[:, 0:1]), reads=[ps_s], writes=[negC], tag="gC.ev")
    p.flush(pfx + "negC")
    p.arena_release(_mloc)

    Wo = load_weight_bf16(p, W["o"], pfx + "Wo", D, D)
    borow = p.abuf(pfx + "borow", [1, D], BF16)
    p.dma("gpsimd", borow[:], bvecs["o"].rearrange("(o n) -> o n", o=1), key=pfx + "borow",
          tag="ld.borow")
    qt_s = [p.abuf(pfx + "qt_s", [128, 8 * 512], BF16)] * 2
    pT = [p.abuf(pfx + f"pT{i}", [128, 1024], BF16) for i in range(2)]
    ps_Sp = [p.psum_pair(0), p.psum_pair(1)]
    ps_O = [p.psum_bank(4), p.psum_bank(5)]
    ps_rbc = p.psum_bank(6)
    ps_W = [p.psum_bank(7), p.psum_bank(7)]
    r_row = p.abuf(pfx + "r_row", [1, 512], F32)
    rbc = p.abuf(pfx + "rbc", [64, 512], F32)
    ctxT = p.abuf(pfx + "ctxT", [128, 8 * 512], BF16)
    xr = [p.abuf(pfx + f"xr{i}", [128, D], F32) for i in range(2)]
    xo = [p.abuf(pfx + f"xo{i}", [128, D], F32) for i in range(2)]
    if tail_ln is not None:
        tail_ln["g_rep"] = bcast_vec(p, tail_ln["g"], pfx + "grep", D)
        tail_ln["b_rep"] = bcast_vec(p, tail_ln["b"], pfx + "brep", D)
        tail_ln["zb"] = p.sbuf("scr_bf", [128, D], BF16)
        zTt = p.abuf(pfx + "zTt", [128, 8 * 128], BF16)
        ln_scr2 = dict(ln_scr)
        ln_scr2["xn"] = p.abuf(pfx + "ln_xn", [128, D], F32)

    for s in range(8):
        spar = s % 2
        for tt in range(4):
            tg = s * 4 + tt
            p.dma("sync",
                  qt_s[spar][:].rearrange("p (j b) -> p j b", b=512)[:, :, tt * 128:(tt + 1) * 128],
                  qT_hbm[tg * 1024:(tg + 1) * 1024, :].rearrange("(j q) b -> q j b", j=8),
                  key=pfx + f"qt_s{spar}", tag=f"ld.qT{s}.{tt}")
        for h in range(NH):
            j, off, hop = h // 2, (h % 2) * 64, h % 2

            def S_mm(m):
                mp = m % 2
                for u in (0, 1):
                    kt = 2 * m + u
                    p.mm(lambda e, j=j, off=off, kt=kt, u=u, mp=mp, spar=spar: e.matmul(
                        ps_Sp[mp][:, u * 512:(u + 1) * 512],
                        KT[off:off + 64, j * N + kt * 128: j * N + (kt + 1) * 128],
                        qt_s[spar][off:off + 64, j * 512:(j + 1) * 512],
                        start=True, stop=True),
                        reads=[KT, qt_s[spar]], writes=[ps_Sp[mp]], tag=f"S.{s}.{h}.{kt}")

            def EXP(m):
                mp = m % 2
                p.op("scalar", lambda e, mp=mp: e.activation(
                    pT[mp][:], ps_Sp[mp][:], AF.Exp, bias=negC[:], scale=0.125),
                    reads=[ps_Sp[mp], negC], writes=[pT[mp]], tag=f"exp.{s}.{h}.{m}")

            def PV(kt):
                mp = (kt // 2) % 2
                u = kt % 2
                p.mm(lambda e, kt=kt, mp=mp, u=u, h=h, hop=hop: e.matmul(
                    ps_O[hop][0:65, :],
                    Vp[:, kt * 1040 + h * 65: kt * 1040 + (h + 1) * 65],
                    pT[mp][:, u * 512:(u + 1) * 512],
                    start=(kt == 0), stop=(kt == T - 1)),
                    reads=[Vp, pT[mp]], writes=[ps_O[hop]], tag=f"PV.{s}.{h}.{kt}")

            S_mm(0); EXP(0); S_mm(1); EXP(1)
            for m in range(T // 2):
                PV(2 * m)
                PV(2 * m + 1)
                if m + 2 < T // 2:
                    S_mm(m + 2)
                    EXP(m + 2)
            p.op("vector", lambda e, hop=hop: e.reciprocal(r_row[:], ps_O[hop][64:65, :]),
                 reads=[ps_O[hop]], writes=[r_row], tag=f"rcp.{s}.{h}")
            p.mm(lambda e: e.matmul(ps_rbc[0:64, :], onec[:, 0:64], r_row[:], start=True, stop=True),
                 reads=[onec, r_row], writes=[ps_rbc], tag=f"rbc.{s}.{h}")
            p.op("scalar", lambda e: e.copy(rbc[:], ps_rbc[0:64, :]),
                 reads=[ps_rbc], writes=[rbc], tag=f"rbce.{s}.{h}")
            p.op("vector", lambda e, j=j, off=off, hop=hop: e.tensor_tensor(
                ctxT[off:off + 64, j * 512:(j + 1) * 512], ps_O[hop][0:64, :], rbc[:], ALU.mult),
                 reads=[ps_O[hop], rbc], writes=[ctxT], tag=f"ctx.{s}.{h}")
        for tt in range(4):
            tg = s * 4 + tt
            tpar = tt % 2
            p.dma("sync", xr[tpar][:], resid_src[tg * 128:(tg + 1) * 128, :],
                  key=pfx + f"xr{tpar}", tag=f"ld.res{tg}")
            for c in range(2):
                for j in range(8):
                    p.mm(lambda e, c=c, j=j, tt=tt, tpar=tpar: e.matmul(
                        ps_W[tpar][:],
                        ctxT[:, j * 512 + tt * 128: j * 512 + tt * 128 + 128],
                        Wo[:, j * D + c * 512: j * D + (c + 1) * 512],
                        start=(j == 0), stop=False),
                        reads=[ctxT, Wo], writes=[ps_W[tpar]], tag=f"wo.{tg}.{c}")
                p.mm(lambda e, c=c, tpar=tpar: e.matmul(
                    ps_W[tpar][:], onecb[:], borow[:, c * 512:(c + 1) * 512],
                    start=False, stop=True),
                    reads=[onecb, borow], writes=[ps_W[tpar]], tag=f"wob.{tg}.{c}")
                p.op("vector", lambda e, c=c, tpar=tpar: e.tensor_tensor(
                    xo[tpar][:, c * 512:(c + 1) * 512], ps_W[tpar][:],
                    xr[tpar][:, c * 512:(c + 1) * 512], ALU.add),
                     reads=[ps_W[tpar], xr[tpar]], writes=[xo[tpar]], tag=f"woe.{tg}.{c}")
            if tail_ln is None:
                p.dma("sync", out_dram[tg * 128:(tg + 1) * 128, :], xo[tpar][:],
                      key=pfx + f"xo{tpar}", tag=f"st.out{tg}")
            else:
                zb = tail_ln["zb"]
                ln_tile(p, xo[tpar][:], xo[tpar][:], ln_scr2,
                        gamma_rep=tail_ln["g_rep"], beta_rep=tail_ln["b_rep"],
                        out2_ap=zb[:])
                p.dma("sync", out_dram[tg * 128:(tg + 1) * 128, :], xo[tpar][:],
                      key=pfx + f"xo{tpar}", tag=f"st.out{tg}")
                for j in range(8):
                    tr128(zb[:, j * 128:(j + 1) * 128], zTt[:, j * 128:(j + 1) * 128],
                          j % 2, f"zT{tg}.{j}", ps_list=(ps_O[0], ps_O[1]))
                p.dma("scalar",
                      tail_ln["outT_hbm"][tg * 1024:(tg + 1) * 1024, :].rearrange("(j q) b -> q j b", j=8),
                      zTt[:].rearrange("p (j b) -> p j b", b=128),
                      key=pfx + "zTst", tag=f"st.zT{tg}")
        p.flush(pfx + f"attn_s{s}")
    p.arena_release(_amark)


def emit_moe(p, pfx, x1_hbm, x1T_hbm, resid_src, out_dram, Wg_d, bg_d, W1_d, b1_d,
             W2_d, b2_d, g2_d, be2_d):
    E, FF = 8, 4096
    _amark = p.arena_mark()
    idb = p.sbuf("identity_bf", [128, 128], BF16)
    onecb = p.sbuf("ones_col_bf", [1, 128], BF16)
    onec = p.sbuf("ones_col", [1, 128], F32)
    ln_scr = {k: p.sbuf("ln_" + k, [128, 1], F32)
              for k in ["ssq", "nsum", "nmean", "m2", "var", "sqv", "rstd", "nmr"]}
    ln_scr["sq"] = p.sbuf("scr_bf", [128, D], BF16)
    moe_hbm = p.dscratch(pfx + "moe", [N, D], F32)

    comb = p.abuf(pfx + "comb", [128, T * 8], F32)
    Wgsb = load_weight_bf16(p, Wg_d, pfx + "Wg", D, 8)
    bgrow = p.abuf(pfx + "bgrow", [1, 8], BF16)
    p.dma("gpsimd", bgrow[:], bg_d.rearrange("(o n) -> o n", o=1), key=pfx + "bgrow")
    x1Tl = [p.abuf(pfx + f"x1Tl{i}", [128, D], BF16) for i in range(2)]
    ps_g = p.psum_bank(0)
    probs = p.abuf(pfx + "probs", [128, 8], F32)
    negm = p.abuf(pfx + "negm", [128, 1], F32)
    lsum = p.abuf(pfx + "lsum", [128, 1], F32)
    rden = p.abuf(pfx + "rden", [128, 1], F32)
    m1 = p.abuf(pfx + "m1", [128, 1], F32)
    m2c = p.abuf(pfx + "m2c", [128, 1], F32)
    msk = p.abuf(pfx + "msk", [128, 8], F32)
    pmk = p.abuf(pfx + "pmk", [128, 8], F32)
    p.flush(pfx + "gate_ld")
    for t in range(T):
        par = t % 2
        p.dma("sync", x1Tl[par][:].rearrange("p (j b) -> p j b", b=128),
              x1T_hbm[t * 1024:(t + 1) * 1024, :].rearrange("(j q) b -> q j b", j=8),
              key=pfx + f"x1Tl{par}", tag=f"g.ldT{t}")
        for j in range(8):
            p.mm(lambda e, j=j, par=par: e.matmul(
                ps_g[:, 0:8], x1Tl[par][:, j * 128:(j + 1) * 128],
                Wgsb[:, j * 8:(j + 1) * 8], start=(j == 0), stop=False),
                reads=[x1Tl[par], Wgsb], writes=[ps_g], tag=f"g.mm{t}")
        p.mm(lambda e: e.matmul(ps_g[:, 0:8], onecb[:], bgrow[:], start=False, stop=True),
             reads=[onecb, bgrow], writes=[ps_g], tag=f"g.b{t}")
        p.op("vector", lambda e: e.tensor_reduce(negm[:], ps_g[:, 0:8], AX.X, ALU.max, negate=True),
             reads=[ps_g], writes=[negm], tag=f"g.max{t}")
        p.op("scalar", lambda e: e.activation(probs[:], ps_g[:, 0:8], AF.Exp, bias=negm[:],
                                              accum_out=lsum[:]),
             reads=[ps_g, negm], writes=[probs, lsum], tag=f"g.exp{t}")
        p.op("vector", lambda e: e.reciprocal(rden[:], lsum[:]),
             reads=[lsum], writes=[rden], tag=f"g.rc{t}")
        p.op("vector", lambda e: e.tensor_scalar(probs[:], probs[:], rden[:], None, ALU.mult),
             reads=[probs, rden], writes=[probs], tag=f"g.n{t}")
        p.op("vector", lambda e: e.tensor_reduce(m1[:], probs[:], AX.X, ALU.max),
             reads=[probs], writes=[m1], tag=f"g.m1{t}")
        p.op("vector", lambda e: e.tensor_scalar(msk[:], probs[:], m1[:], None, ALU.is_ge),
             reads=[probs, m1], writes=[msk], tag=f"g.k1{t}")
        p.op("vector", lambda e: e.tensor_scalar(pmk[:], msk[:], -1e30, None, ALU.mult),
             reads=[msk], writes=[pmk], tag=f"g.pm{t}")
        p.op("vector", lambda e: e.tensor_tensor(pmk[:], pmk[:], probs[:], ALU.add),
             reads=[pmk, probs], writes=[pmk], tag=f"g.pm2{t}")
        p.op("vector", lambda e: e.tensor_reduce(m2c[:], pmk[:], AX.X, ALU.max),
             reads=[pmk], writes=[m2c], tag=f"g.m2{t}")
        p.op("vector", lambda e: e.tensor_scalar(msk[:], probs[:], m2c[:], None, ALU.is_ge),
             reads=[probs, m2c], writes=[msk], tag=f"g.k2{t}")
        p.op("vector", lambda e: e.tensor_tensor(m1[:], m1[:], m2c[:], ALU.add),
             reads=[m1, m2c], writes=[m1], tag=f"g.den{t}")
        p.op("vector", lambda e: e.reciprocal(rden[:], m1[:]),
             reads=[m1], writes=[rden], tag=f"g.rd{t}")
        p.op("vector", lambda e, t=t: e.scalar_tensor_tensor(
            comb[:, t * 8:(t + 1) * 8], probs[:], rden[:], msk[:], ALU.mult, ALU.mult),
             reads=[probs, rden, msk], writes=[comb], tag=f"g.c{t}")
    p.flush(pfx + "gate")

    _m2 = p.arena_mark()
    idf = p.sbuf("identity_f32", [128, 128], F32)
    b1r = p.abuf(pfx + "b1r", [32, 128], F32)
    W1sb = p.abuf(pfx + "W1sb", [128, 8 * FF], BF16)
    W2sb = p.abuf(pfx + "W2sb", [128, 32 * D], BF16)
    b1c = p.abuf(pfx + "b1c", [128, 32], F32)
    b2row = p.abuf(pfx + "b2row", [1, D], BF16)
    x1Ts = [p.abuf(pfx + f"x1Ts{i}", [128, 8 * 512], BF16) for i in range(2)]
    hT = p.abuf(pfx + "hT", [128, 32 * 512], BF16)
    mo = [p.abuf(pfx + f"mo{i}", [128, D], F32) for i in range(2)]
    ps_h = [p.psum_bank(0), p.psum_bank(1)]
    ps_y = [p.psum_bank(2), p.psum_bank(3)]

    for e in range(E):
        p.dma("gpsimd", W1sb[:].rearrange("p (a n) -> p a n", n=FF),
              W1_d[e].rearrange("(a p) n -> p a n", p=128), key=pfx + "W1sb",
              tag=f"ldW1.{e}")
        p.dma("gpsimd", W2sb[:].rearrange("p (a n) -> p a n", n=D),
              W2_d[e].rearrange("(a p) n -> p a n", p=128), key=pfx + "W2sb",
              tag=f"ldW2.{e}")
        p.dma("sync", b1r[:], b1_d[e].rearrange("(a p) -> a p", p=128), key=pfx + "b1r",
              tag=f"ldb1.{e}")
        p.mm(lambda e_: e_.transpose(ps_h[0][0:128, 0:32], b1r[:], idf[0:32, 0:32]),
             reads=[b1r, idf], writes=[ps_h[0]], tag=f"trb1.{e}")
        p.op("scalar", lambda e_: e_.copy(b1c[:], ps_h[0][0:128, 0:32]),
             reads=[ps_h[0]], writes=[b1c], tag=f"evb1.{e}")
        p.dma("gpsimd", b2row[:], b2_d[e].rearrange("(o n) -> o n", o=1), key=pfx + "b2row",
              tag=f"ldb2.{e}")
        for s in range(8):
            spar = s % 2
            for tt in range(4):
                tg = s * 4 + tt
                p.dma("sync",
                      x1Ts[spar][:].rearrange("p (j b) -> p j b", b=512)[:, :, tt * 128:(tt + 1) * 128],
                      x1T_hbm[tg * 1024:(tg + 1) * 1024, :].rearrange("(j q) b -> q j b", j=8),
                      key=pfx + f"x1Ts{spar}", tag=f"m.ldT{e}.{s}.{tt}")
            for fb in range(32):
                fpar = fb % 2
                for j in range(8):
                    p.mm(lambda e_, j=j, fb=fb, fpar=fpar, spar=spar: e_.matmul(
                        ps_h[fpar][:],
                        W1sb[:, j * FF + fb * 128: j * FF + (fb + 1) * 128],
                        x1Ts[spar][:, j * 512:(j + 1) * 512],
                        start=(j == 0), stop=(j == 7)),
                        reads=[W1sb, x1Ts[spar]], writes=[ps_h[fpar]], tag=f"h.{e}.{s}.{fb}")
                p.op("scalar", lambda e_, fb=fb, fpar=fpar: e_.activation(
                    hT[:, fb * 512:(fb + 1) * 512], ps_h[fpar][:], AF.Gelu,
                    bias=b1c[:, fb:fb + 1]),
                    reads=[ps_h[fpar], b1c], writes=[hT], tag=f"gelu.{e}.{s}.{fb}")
            for tt in range(4):
                ypar = tt % 2
                tg = s * 4 + tt
                if e > 0:
                    p.dma("sync", mo[ypar][:], moe_hbm[tg * 128:(tg + 1) * 128, :],
                          key=pfx + f"mo{ypar}", tag=f"m.ldmo{e}.{tg}")
                for c in range(2):
                    for fb in range(32):
                        p.mm(lambda e_, c=c, fb=fb, tt=tt, ypar=ypar: e_.matmul(
                            ps_y[ypar][:],
                            hT[:, fb * 512 + tt * 128: fb * 512 + tt * 128 + 128],
                            W2sb[:, fb * D + c * 512: fb * D + (c + 1) * 512],
                            start=(fb == 0), stop=False),
                            reads=[hT, W2sb], writes=[ps_y[ypar]], tag=f"y.{e}.{tg}.{c}")
                    p.mm(lambda e_, c=c, ypar=ypar: e_.matmul(
                        ps_y[ypar][:], onecb[:], b2row[:, c * 512:(c + 1) * 512],
                        start=False, stop=True),
                        reads=[onecb, b2row], writes=[ps_y[ypar]], tag=f"yb.{e}.{tg}.{c}")
                    cc = comb[:, tg * 8 + e: tg * 8 + e + 1]
                    if e == 0:
                        p.op("vector", lambda e_, c=c, ypar=ypar, cc=cc: e_.tensor_scalar(
                            mo[ypar][:, c * 512:(c + 1) * 512], ps_y[ypar][:], cc, None, ALU.mult),
                             reads=[ps_y[ypar], comb], writes=[mo[ypar]], tag=f"ya.{e}.{tg}.{c}")
                    else:
                        p.op("vector", lambda e_, c=c, ypar=ypar, cc=cc: e_.scalar_tensor_tensor(
                            mo[ypar][:, c * 512:(c + 1) * 512], ps_y[ypar][:], cc,
                            mo[ypar][:, c * 512:(c + 1) * 512], ALU.mult, ALU.add),
                             reads=[ps_y[ypar], comb, mo[ypar]], writes=[mo[ypar]], tag=f"ya.{e}.{tg}.{c}")
                p.dma("sync", moe_hbm[tg * 128:(tg + 1) * 128, :], mo[ypar][:],
                      key=pfx + f"most{ypar}", tag=f"m.stmo{e}.{tg}")
            p.flush(pfx + f"moe.{e}.{s}")

    p.arena_release(_m2)

    g2r = bcast_vec(p, g2_d, pfx + "g2r", D)
    b2r = bcast_vec(p, be2_d, pfx + "be2r", D)
    x1l = [p.abuf(pfx + f"x1l{i}", [128, D], F32) for i in range(2)]
    mol = [p.abuf(pfx + f"mol{i}", [128, D], F32) for i in range(2)]
    xrl = [p.abuf(pfx + f"xrl{i}", [128, D], F32) for i in range(2)]
    zf = [p.abuf(pfx + f"zf{i}", [128, D], F32) for i in range(2)]
    lnx = p.abuf(pfx + "lnxn", [128, D], F32)
    scr2 = dict(ln_scr)
    scr2["xn"] = lnx
    for t in range(T):
        par = t % 2
        p.dma("sync", x1l[par][:], x1_hbm[t * 128:(t + 1) * 128, :], key=pfx + f"x1l{par}",
              tag=f"f.ldx1{t}")
        p.dma("sync", mol[par][:], moe_hbm[t * 128:(t + 1) * 128, :], key=pfx + f"mol{par}",
              tag=f"f.ldmo{t}")
        p.dma("scalar", xrl[par][:], resid_src[t * 128:(t + 1) * 128, :], key=pfx + f"xrl{par}",
              tag=f"f.ldxr{t}")
        p.op("vector", lambda e, par=par: e.tensor_tensor(x1l[par][:], x1l[par][:], mol[par][:], ALU.add),
             reads=[x1l[par], mol[par]], writes=[x1l[par]], tag=f"f.add{t}")
        ln_tile(p, x1l[par][:], zf[par][:], scr2, gamma_rep=g2r, beta_rep=b2r)
        p.op("vector", lambda e, par=par: e.tensor_tensor(zf[par][:], zf[par][:], xrl[par][:], ALU.add),
             reads=[zf[par], xrl[par]], writes=[zf[par]], tag=f"f.res{t}")
        p.dma("sync", out_dram[t * 128:(t + 1) * 128, :], zf[par][:], key=pfx + f"zf{par}",
              tag=f"f.st{t}")
    p.flush(pfx + "moefin")
    p.arena_release(_amark)


def build_l0_ffn():
    nc = bass.Bass("TRN2", target_bir_lowering=False)
    p = Prog(nc)
    xin = p.din("x", [N, D], F32)
    out = p.dout("y", [N, D], F32)
    _load_shared_consts(p)
    W = {k[0]: p.din(f"L0_ffn_W{k}", [D, D], F32) for k in ("q", "k", "v", "o")}
    bvec = {k[0]: p.din(f"L0_ffn_b{k}", [D], F32) for k in ("q", "k", "v", "o")}
    cosd = p.din("l0f_cos", [N, 64], F32)
    sind = p.din("l0f_sin", [N, 64], F32)
    g1 = p.din("L0_ffn_g1", [D], F32)
    be1 = p.din("L0_ffn_be1", [D], F32)
    xln = p.dscratch("F_xln", [N, D], F32)
    x1 = p.dscratch("F_x1", [N, D], F32)
    x1T = p.dscratch("F_x1T", [T * 8 * 128, 128], BF16)
    emit_full_attn(p, "F_", xin, xln, x1, W, bvec, cosd, sind, rope_style="half",
                   xln_store=xln, tail_ln=dict(g=g1, b=be1, outT_hbm=x1T))
    emit_moe(p, "M_", x1, x1T, xin, out,
             p.din("L0_ffn_Wg", [D, 8], F32), p.din("L0_ffn_bg", [8], F32),
             p.din("L0_ffn_W1", [8, D, 4096], F32), p.din("L0_ffn_b1", [8, 4096], F32),
             p.din("L0_ffn_W2", [8, 4096, D], F32), p.din("L0_ffn_b2", [8, D], F32),
             p.din("L0_ffn_g2", [D], F32), p.din("L0_ffn_be2", [D], F32))
    p.flush("tail")
    nc.finalize()
    return nc


L1A_OFFS = [0, 1, 2, 3, 4, 6, 8, 12, 16, 24, 32, 48, 64, 96, 128, 192, 256, 384,
            512, 768, 1024, 1536, 2048, 3072]


def emit_l1_attn(p, pfx, x_src, out_dram, Wqkv_d, bqkv_d, Wout_d, bout_d,
                 Wgate_d, bgate_d, posb_d):
    NO = len(L1A_OFFS)
    _amark = p.arena_mark()
    idb = p.sbuf("identity_bf", [128, 128], BF16)
    onecb = p.sbuf("ones_col_bf", [1, 128], BF16)
    scr_bf = p.sbuf("scr_bf", [128, D], BF16)
    ln_scr = {k: p.sbuf("ln_" + k, [128, 1], F32)
              for k in ["ssq", "nsum", "nmean", "m2", "var", "sqv", "rstd", "nmr"]}
    ln_scr["sq"] = scr_bf
    xnT_hbm = p.dscratch(pfx + "xnT", [T * 8 * 128, 128], BF16)
    q_hbm = p.dscratch(pfx + "q", [N, D], BF16)
    gate_hbm = p.dscratch(pfx + "gate", [N, D], BF16)
    kv = p.abuf(pfx + "kv", [128, T * 2048], BF16)
    zkv = p.abuf(pfx + "zkv", [128, 2048], BF16)
    p.op("vector", lambda e: e.memset(zkv[:], 0.0), writes=[zkv], tag="zkv")
    ps_tr = [p.psum_bank(6), p.psum_bank(7)]

    def tr128(src_ap, dst_ap, par, tag):
        ps = ps_tr[par]
        psv = ps[:].bitcast(BF16)
        p.mm(lambda e: e.transpose(psv[:, 0:128], src_ap, idb[:]),
             reads=[_nm(src_ap), idb], writes=[ps], tag=tag + ".t")
        p.op("scalar", lambda e: e.copy(dst_ap, psv[:, 0:128]),
             reads=[ps], writes=[_nm(dst_ap)], tag=tag + ".ev")

    _m1 = p.arena_mark()
    Wkv = p.abuf(pfx + "Wkv", [128, 8 * 2048], BF16)
    p.dma("gpsimd", Wkv[:].rearrange("p (a n) -> p a n", n=2048),
          Wqkv_d[:, D:3 * D].rearrange("(a p) n -> p a n", p=128), key=pfx + "Wkv",
          tag="ldWkv")
    bkvrow = p.abuf(pfx + "bkvrow", [1, 2048], BF16)
    p.dma("gpsimd", bkvrow[:], bqkv_d[D:3 * D].rearrange("(o n) -> o n", o=1),
          key=pfx + "bkvrow", tag="ldbkv")
    xt = [p.abuf(pfx + f"xt{i}", [128, D], F32) for i in range(2)]
    xnb = [p.abuf(pfx + f"xnb{i}", [128, D], BF16) for i in range(2)]
    xnTt = [p.abuf(pfx + f"xnTt{i}", [128, D], BF16) for i in range(2)]
    ps_p = [p.psum_bank(0), p.psum_bank(1), p.psum_bank(2), p.psum_bank(3)]
    p.flush(pfx + "b1ld")
    for t in range(T):
        par = t % 2
        p.dma("sync", xt[par][:], x_src[t * 128:(t + 1) * 128, :], key=pfx + f"xt{par}",
              tag=f"ld.x{t}")
        ln_tile(p, xt[par][:], xnb[par][:], ln_scr)
        for j in range(8):
            tr128(xnb[par][:, j * 128:(j + 1) * 128], xnTt[par][:, j * 128:(j + 1) * 128],
                  j % 2, f"xnT{t}.{j}")
        p.dma("sync", xnT_hbm[t * 1024:(t + 1) * 1024, :].rearrange("(j q) b -> q j b", j=8),
              xnTt[par][:].rearrange("p (j b) -> p j b", b=128),
              key=pfx + f"xnTst{par}", tag=f"st.xnT{t}")
        for c in range(4):
            for j in range(8):
                p.mm(lambda e, c=c, j=j, par=par: e.matmul(
                    ps_p[c][:], xnTt[par][:, j * 128:(j + 1) * 128],
                    Wkv[:, j * 2048 + c * 512: j * 2048 + (c + 1) * 512],
                    start=(j == 0), stop=False),
                    reads=[xnTt[par], Wkv], writes=[ps_p[c]], tag=f"kv.{t}.{c}")
            p.mm(lambda e, c=c: e.matmul(
                ps_p[c][:], onecb[:], bkvrow[:, c * 512:(c + 1) * 512],
                start=False, stop=True),
                reads=[onecb, bkvrow], writes=[ps_p[c]], tag=f"kvb.{t}.{c}")
            p.op("scalar", lambda e, c=c, t=t: e.copy(
                kv[:, t * 2048 + c * 512: t * 2048 + (c + 1) * 512], ps_p[c][:]),
                 reads=[ps_p[c]], writes=[kv], tag=f"kvev.{t}.{c}")
    p.flush(pfx + "b1")
    p.arena_release(_m1)

    _m2 = p.arena_mark()
    Wqg = p.abuf(pfx + "Wqg", [128, 8 * 2048], BF16)
    p.dma("gpsimd", Wqg[:].rearrange("p (a n) -> p a n", n=2048)[:, :, 0:1024],
          Wqkv_d[:, 0:D].rearrange("(a p) n -> p a n", p=128), key=pfx + "Wqg", tag="ldWq")
    p.dma("gpsimd", Wqg[:].rearrange("p (a n) -> p a n", n=2048)[:, :, 1024:2048],
          Wgate_d.rearrange("(a p) n -> p a n", p=128), key=pfx + "Wqg", tag="ldWg")
    bqgrow = p.abuf(pfx + "bqgrow", [1, 2048], BF16)
    p.dma("gpsimd", bqgrow[:, 0:1024], bqkv_d[0:D].rearrange("(o n) -> o n", o=1),
          key=pfx + "bqgrow", tag="ldbq")
    p.dma("gpsimd", bqgrow[:, 1024:2048], bgate_d.rearrange("(o n) -> o n", o=1),
          key=pfx + "bqgrow", tag="ldbg")
    xnTl = [p.abuf(pfx + f"xnTl{i}", [128, D], BF16) for i in range(2)]
    qg = p.abuf(pfx + "qg", [128, 2048], BF16)
    p.flush(pfx + "b2ald")
    for t in range(T):
        par = t % 2
        p.dma("sync", xnTl[par][:].rearrange("p (j b) -> p j b", b=128),
              xnT_hbm[t * 1024:(t + 1) * 1024, :].rearrange("(j q) b -> q j b", j=8),
              key=pfx + f"xnTl{par}", tag=f"ld.xnT{t}")
        for c in range(4):
            for j in range(8):
                p.mm(lambda e, c=c, j=j, par=par: e.matmul(
                    ps_p[c][:], xnTl[par][:, j * 128:(j + 1) * 128],
                    Wqg[:, j * 2048 + c * 512: j * 2048 + (c + 1) * 512],
                    start=(j == 0), stop=False),
                    reads=[xnTl[par], Wqg], writes=[ps_p[c]], tag=f"qg.{t}.{c}")
            p.mm(lambda e, c=c: e.matmul(
                ps_p[c][:], onecb[:], bqgrow[:, c * 512:(c + 1) * 512],
                start=False, stop=True),
                reads=[onecb, bqgrow], writes=[ps_p[c]], tag=f"qgb.{t}.{c}")
            if c < 2:
                p.op("scalar", lambda e, c=c: e.copy(qg[:, c * 512:(c + 1) * 512], ps_p[c][:]),
                     reads=[ps_p[c]], writes=[qg], tag=f"qev.{t}.{c}")
            else:
                p.op("scalar", lambda e, c=c: e.activation(
                    qg[:, c * 512:(c + 1) * 512], ps_p[c][:], AF.Sigmoid),
                     reads=[ps_p[c]], writes=[qg], tag=f"gev.{t}.{c}")
        p.dma("sync", q_hbm[t * 128:(t + 1) * 128, :], qg[:, 0:1024], key=pfx + "qst",
              tag=f"st.q{t}")
        p.dma("sync", gate_hbm[t * 128:(t + 1) * 128, :], qg[:, 1024:2048], key=pfx + "gst",
              tag=f"st.g{t}")
    p.flush(pfx + "b2a")
    p.arena_release(_m2)

    Wosb = load_weight_bf16(p, Wout_d, pfx + "Wosb", D, D)
    borow = p.abuf(pfx + "borow", [1, D], BF16)
    p.dma("gpsimd", borow[:], bout_d.rearrange("(o n) -> o n", o=1), key=pfx + "borow",
          tag="ldbo")
    ps_w = [p.psum_bank(0), p.psum_bank(1)]
    pbrep = p.abuf(pfx + "pbrep", [128, NO * 16], F32)
    pb24 = p.abuf(pfx + "pb24", [24, 16], F32)
    pbT = p.abuf(pfx + "pbT", [16, NO], F32)
    pbrow = p.abuf(pfx + "pbrow", [1, NO * 16], F32)
    idf = p.sbuf("identity_f32", [128, 128], F32)
    onec = p.sbuf("ones_col", [1, 128], F32)
    p.dma("sync", pb24[:], posb_d, key=pfx + "pb24", tag="ldpb")
    p.mm(lambda e: e.transpose(ps_w[0][0:16, 0:NO], pb24[:], idf[0:24, 0:24]),
         reads=[pb24, idf], writes=[ps_w[0]], tag="pb.t")
    p.op("scalar", lambda e: e.copy(pbT[:], ps_w[0][0:16, 0:NO]),
         reads=[ps_w[0]], writes=[pbT], tag="pb.ev")
    pb_scr = p.dscratch(pfx + "pbscr", [16, NO], F32)
    p.dma("sync", pb_scr, pbT[:], key=pfx + "pbrow", tag="pb.flat0")
    p.dma("sync", pbrow[:], pb_scr.rearrange("h o -> (h o)").rearrange("(z n) -> z n", z=1),
          key=pfx + "pbrow", tag="pb.flat1")
    p.mm(lambda e: e.matmul(ps_w[1][:, 0:NO * 16], onec[:], pbrow[:], start=True, stop=True),
         reads=[onec, pbrow], writes=[ps_w[1]], tag="pb.bc")
    p.op("scalar", lambda e: e.copy(pbrep[:], ps_w[1][:, 0:NO * 16]),
         reads=[ps_w[1]], writes=[pbrep], tag="pb.bce")
    ql = [p.abuf(pfx + f"ql{i}", [128, D], BF16) for i in range(2)]
    gl = [p.abuf(pfx + f"gl{i}", [128, D], BF16) for i in range(2)]
    xrl = [p.abuf(pfx + f"xrl{i}", [128, D], F32) for i in range(2)]
    kvsh = [p.abuf(pfx + f"kvsh{i}", [128, 2048], BF16) for i in range(3)]
    KVQ = ["sync", "scalar", "sync"]
    sbf = p.abuf(pfx + "sbf", [128, NO * 16], F32)
    esb = p.abuf(pfx + "esb", [128, NO * 16], F32)
    negm = p.abuf(pfx + "negm", [128, 16], F32)
    lsum = p.abuf(pfx + "lsum", [128, 16], F32)
    rsum = p.abuf(pfx + "rsum", [128, 16], F32)
    oacc = p.abuf(pfx + "oacc", [128, D], F32)
    otmp2 = [p.abuf(pfx + f"otmp{i}", [128, D], F32) for i in range(2)]
    ogT = p.abuf(pfx + "ogT", [128, D], BF16)
    s3 = sbf[:].rearrange("p (h o) -> p h o", o=NO)
    e3 = esb[:].rearrange("p (h o) -> p h o", o=NO)

    for t in range(T):
        par = t % 2
        p.dma("sync", ql[par][:], q_hbm[t * 128:(t + 1) * 128, :], key=pfx + f"ql{par}",
              tag=f"ld.q{t}")
        p.dma("sync", gl[par][:], gate_hbm[t * 128:(t + 1) * 128, :], key=pfx + f"gl{par}",
              tag=f"ld.g{t}")
        p.dma("scalar", xrl[par][:], x_src[t * 128:(t + 1) * 128, :], key=pfx + f"xrl{par}",
              tag=f"ld.xr{t}")
        for oi, dlt in enumerate(L1A_OFFS):
            a, r = dlt // 128, dlt % 128
            opar = oi % 3
            if r == 0:
                src_t = t - a
                kv_ap = kv[:, src_t * 2048:(src_t + 1) * 2048] if src_t >= 0 else zkv[:]
                kv_names = [kv] if src_t >= 0 else [zkv]
            else:
                hi_t, lo_t = t - a, t - a - 1
                hi_src = kv[:, hi_t * 2048:(hi_t + 1) * 2048][0:128 - r, :] if hi_t >= 0 else zkv[0:128 - r, :]
                lo_src = kv[:, lo_t * 2048:(lo_t + 1) * 2048][128 - r:128, :] if lo_t >= 0 else zkv[128 - r:128, :]
                q_ = KVQ[opar]
                p.dma(q_, kvsh[opar][r:128, :], hi_src, key=pfx + f"kvsh{opar}",
                      tag=f"sh.{t}.{oi}.h")
                p.dma(q_, kvsh[opar][0:r, :], lo_src, key=pfx + f"kvsh{opar}",
                      tag=f"sh.{t}.{oi}.l")
                kv_ap = kvsh[opar][:]
                kv_names = [kvsh[opar]]
            k3 = kv_ap[:, 0:1024].rearrange("p (h d) -> p h d", d=64)
            v3 = kv_ap[:, 1024:2048].rearrange("p (h d) -> p h d", d=64)
            q3 = ql[par][:].rearrange("p (h d) -> p h d", d=64)
            p.op("vector", lambda e, q3=q3, k3=k3: e.tensor_tensor(
                scr_bf[:].rearrange("p (h d) -> p h d", d=64), q3, k3, ALU.mult),
                 reads=[ql[par]] + kv_names, writes=[scr_bf], tag=f"sc.{t}.{oi}")
            p.op("vector", lambda e, oi=oi: e.tensor_reduce(
                s3[:, :, oi:oi + 1], scr_bf[:].rearrange("p (h d) -> p h d", d=64),
                AX.X, ALU.add),
                 reads=[scr_bf], writes=[sbf], tag=f"sr.{t}.{oi}")
        p.op("vector", lambda e: e.tensor_scalar(sbf[:], sbf[:], 0.125, None, ALU.mult),
             reads=[sbf], writes=[sbf], tag=f"ssc.{t}")
        p.op("vector", lambda e: e.tensor_tensor(sbf[:], sbf[:], pbrep[:], ALU.add),
             reads=[sbf, pbrep], writes=[sbf], tag=f"spb.{t}")
        p.op("vector", lambda e: e.tensor_reduce(negm[:], s3, AX.X, ALU.max, negate=True),
             reads=[sbf], writes=[negm], tag=f"sm.{t}")
        p.op("vector", lambda e: e.tensor_tensor(
            s3, s3, negm[:].unsqueeze(2).broadcast_to((128, 16, NO)), ALU.add),
             reads=[sbf, negm], writes=[sbf], tag=f"ssub.{t}")
        p.op("scalar", lambda e: e.activation(esb[:], sbf[:], AF.Exp),
             reads=[sbf], writes=[esb], tag=f"sexp.{t}")
        p.op("vector", lambda e: e.tensor_reduce(lsum[:], e3, AX.X, ALU.add),
             reads=[esb], writes=[lsum], tag=f"ssum.{t}")
        p.op("vector", lambda e: e.reciprocal(rsum[:], lsum[:]),
             reads=[lsum], writes=[rsum], tag=f"srec.{t}")
        p.op("vector", lambda e: e.tensor_tensor(
            e3, e3, rsum[:].unsqueeze(2).broadcast_to((128, 16, NO)), ALU.mult),
             reads=[esb, rsum], writes=[esb], tag=f"snrm.{t}")
        for oi, dlt in enumerate(L1A_OFFS):
            a, r = dlt // 128, dlt % 128
            opar = oi % 3
            if r == 0:
                src_t = t - a
                kv_ap = kv[:, src_t * 2048:(src_t + 1) * 2048] if src_t >= 0 else zkv[:]
                kv_names = [kv] if src_t >= 0 else [zkv]
            else:
                hi_t, lo_t = t - a, t - a - 1
                hi_src = kv[:, hi_t * 2048:(hi_t + 1) * 2048][0:128 - r, :] if hi_t >= 0 else zkv[0:128 - r, :]
                lo_src = kv[:, lo_t * 2048:(lo_t + 1) * 2048][128 - r:128, :] if lo_t >= 0 else zkv[128 - r:128, :]
                q_ = KVQ[opar]
                p.dma(q_, kvsh[opar][r:128, :], hi_src, key=pfx + f"kvsh{opar}",
                      tag=f"sh2.{t}.{oi}.h")
                p.dma(q_, kvsh[opar][0:r, :], lo_src, key=pfx + f"kvsh{opar}",
                      tag=f"sh2.{t}.{oi}.l")
                kv_ap = kvsh[opar][:]
                kv_names = [kvsh[opar]]
            v3 = kv_ap[:, 1024:2048].rearrange("p (h d) -> p h d", d=64)
            wv = e3[:, :, oi:oi + 1].broadcast_to((128, 16, 64))
            if oi == 0:
                p.op("vector", lambda e, v3=v3, wv=wv: e.tensor_tensor(
                    oacc[:].rearrange("p (h d) -> p h d", d=64), v3, wv, ALU.mult),
                     reads=[esb] + kv_names, writes=[oacc], tag=f"oa.{t}.{oi}")
            else:
                ot = otmp2[oi % 2]
                p.op("gpsimd", lambda e, v3=v3, wv=wv, ot=ot: e.tensor_tensor(
                    ot[:].rearrange("p (h d) -> p h d", d=64), v3, wv, ALU.mult),
                     reads=[esb] + kv_names, writes=[ot], tag=f"om.{t}.{oi}")
                p.op("vector", lambda e, ot=ot: e.tensor_tensor(oacc[:], oacc[:], ot[:], ALU.add),
                     reads=[oacc, ot], writes=[oacc], tag=f"oa.{t}.{oi}")
        p.op("vector", lambda e, par=par: e.tensor_tensor(oacc[:], oacc[:], gl[par][:], ALU.mult),
             reads=[oacc, gl[par]], writes=[oacc], tag=f"og.{t}")
        p.op("vector", lambda e: e.tensor_copy(scr_bf[:], oacc[:]),
             reads=[oacc], writes=[scr_bf], tag=f"ogb.{t}")
        for j in range(8):
            tr128(scr_bf[:, j * 128:(j + 1) * 128], ogT[:, j * 128:(j + 1) * 128],
                  j % 2, f"ogT{t}.{j}")
        wpar = t % 2
        for c in range(2):
            for j in range(8):
                p.mm(lambda e, c=c, j=j, wpar=wpar: e.matmul(
                    ps_w[wpar][:], ogT[:, j * 128:(j + 1) * 128],
                    Wosb[:, j * D + c * 512: j * D + (c + 1) * 512],
                    start=(j == 0), stop=False),
                    reads=[ogT, Wosb], writes=[ps_w[wpar]], tag=f"wo.{t}.{c}")
            p.mm(lambda e, c=c, wpar=wpar: e.matmul(
                ps_w[wpar][:], onecb[:], borow[:, c * 512:(c + 1) * 512],
                start=False, stop=True),
                reads=[onecb, borow], writes=[ps_w[wpar]], tag=f"wob.{t}.{c}")
            p.op("vector", lambda e, c=c, par=par, wpar=wpar: e.tensor_tensor(
                xrl[par][:, c * 512:(c + 1) * 512], ps_w[wpar][:],
                xrl[par][:, c * 512:(c + 1) * 512], ALU.add),
                 reads=[ps_w[wpar], xrl[par]], writes=[xrl[par]], tag=f"woe.{t}.{c}")
        p.dma("sync", out_dram[t * 128:(t + 1) * 128, :], xrl[par][:], key=pfx + f"ost{par}",
              tag=f"st.{t}")
        if t % 4 == 3:
            p.flush(pfx + f"b2b.{t}")
    p.flush(pfx + "b2b")
    p.arena_release(_amark)


def build_l1_attn():
    nc = bass.Bass("TRN2", target_bir_lowering=False)
    p = Prog(nc)
    xin = p.din("x", [N, D], F32)
    out = p.dout("y", [N, D], F32)
    _load_shared_consts(p)
    emit_l1_attn(p, "G_", xin, out,
                 p.din("L1_attn_Wqkv", [D, 3 * D], F32), p.din("L1_attn_bqkv", [3 * D], F32),
                 p.din("L1_attn_Wout", [D, D], F32), p.din("L1_attn_bout", [D], F32),
                 p.din("L1_attn_Wgate", [D, D], F32), p.din("L1_attn_bgate", [D], F32),
                 p.din("L1_attn_pos_bias", [24, 16], F32))
    p.flush("tail")
    nc.finalize()
    return nc


def emit_l1_ffn(p, pfx, x_src, out_dram, Wqkv_d, qs_d, ks_d, Wo_d, lng_d, lnb_d,
                W1_d, b1a_d, b1g_d, W2_d, b2_d, depth=6, do_attn=True, do_ffn=True):
    DEPTH, W, NW = depth, 256, 16
    FF2, FFH = 5460, 2730
    AT, GT = 22, 22
    _amark = p.arena_mark()
    idb = p.sbuf("identity_bf", [128, 128], BF16)
    idf = p.sbuf("identity_f32", [128, 128], F32)
    onecb = p.sbuf("ones_col_bf", [1, 128], BF16)
    scr_bf = p.sbuf("scr_bf", [128, D], BF16)
    ln_scr = {k: p.sbuf("ln_" + k, [128, 1], F32)
              for k in ["ssq", "nsum", "nmean", "m2", "var", "sqv", "rstd", "nmr"]}
    ln_scr["sq"] = scr_bf
    hbufA = p.dscratch(pfx + "hbufA", [N, D], F32)
    hbufB = p.dscratch(pfx + "hbufB", [N, D], F32)
    negscr = p.dscratch(pfx + "negscr", [16, 128], BF16)
    mle = p.sbuf("mle_sb", [128, 128], BF16)
    mge = p.sbuf("mge_sb", [128, 128], BF16)
    p.dma("sync", mle[:], p.din("l1f_mle", [128, 128], BF16), key="mle")
    p.dma("sync", mge[:], p.din("l1f_mge", [128, 128], BF16), key="mge")

    _mc0 = p.arena_mark()
    xt = [p.abuf(pfx + f"c0x{i}", [128, D], F32) for i in range(2)]
    zt = [p.abuf(pfx + f"c0z{i}", [128, D], F32) for i in range(2)]
    for t in range(T):
        par = t % 2
        p.dma("sync", xt[par][:], x_src[t * 128:(t + 1) * 128, :], key=pfx + f"c0x{par}",
              tag=f"c0.ld{t}")
        ln_tile(p, xt[par][:], zt[par][:], ln_scr)
        p.dma("sync", hbufA[t * 128:(t + 1) * 128, :], zt[par][:], key=pfx + f"c0z{par}",
              tag=f"c0.st{t}")
    p.flush(pfx + "c0")
    p.arena_release(_mc0)

    hcur = [0]

    def hrd():
        return hbufA if hcur[0] == 0 else hbufB

    def hwr():
        return hbufB if hcur[0] == 0 else hbufA

    for L in range(DEPTH):
      if do_attn:
        _ml = p.arena_mark()
        Wqkv = p.abuf(pfx + "Wqkv", [128, 8 * 3072], BF16)
        p.dma("gpsimd", Wqkv[:].rearrange("p (a n) -> p a n", n=3072),
              Wqkv_d[L].rearrange("(a p) n -> p a n", p=128), key=pfx + "Wqkv",
              tag=f"a{L}.ldW")
        Wo = p.abuf(pfx + "Wo", [128, 8 * D], BF16)
        p.dma("gpsimd", Wo[:].rearrange("p (a n) -> p a n", n=D),
              Wo_d[L].rearrange("(a p) n -> p a n", p=128), key=pfx + "Wo",
              tag=f"a{L}.ldWo")
        qs_rep = p.abuf(pfx + "qs_rep", [128, 64], F32)
        ks_rep = p.abuf(pfx + "ks_rep", [128, 64], F32)
        p.dma("sync", qs_rep[:], qs_d[L].rearrange("(o n) -> o n", o=1).broadcast_to((128, 64)),
              key=pfx + "qs_rep", tag=f"a{L}.ldqs")
        p.dma("sync", ks_rep[:], ks_d[L].rearrange("(o n) -> o n", o=1).broadcast_to((128, 64)),
              key=pfx + "ks_rep", tag=f"a{L}.ldks")
        scr2 = p.abuf(pfx + "scr2", [128, 2048], F32)
        krhs = p.abuf(pfx + "krhs", [128, 16 * 768], BF16)
        qrhs2 = [p.abuf(pfx + f"qrhs{i}", [128, 16 * 256], BF16) for i in range(2)]
        vw = p.abuf(pfx + "vw", [128, 6 * 1040], BF16)
        p.op("vector", lambda e: e.memset(krhs[64:65, :], 1.0),
             writes=[krhs] + [f"krhs_s{i}" for i in range(6)], tag=f"a{L}.ones")
        p.op("vector", lambda e: e.memset(vw[:], 1.0),
             writes=[vw] + [f"vw_s{i}" for i in range(6)], tag=f"a{L}.vones")
        xsb = [p.abuf(pfx + f"xsb{i}", [128, D], BF16) for i in range(2)]
        xsT = [p.abuf(pfx + f"xsT{i}", [128, D], BF16) for i in range(2)]
        qkb = p.abuf(pfx + "qkb", [128, 2048], F32)
        qkbb2 = [p.abuf(pfx + f"qkbb{i}", [128, 2048], BF16) for i in range(2)]
        rotb = p.abuf(pfx + "rotb", [128, 2048], F32)
        t1b = p.abuf(pfx + "t1b", [128, 2048], F32)
        nrm = p.abuf(pfx + "nrm", [128, 32], F32)
        rnb = p.abuf(pfx + "rnb", [128, 32], F32)
        tb = {nm: [p.abuf(pfx + f"{nm}{i}", [128, 64], F32) for i in range(2)]
              for nm in ("tcs", "tss", "tci", "tsi")}
        tbl_d = {nm: p.din("l1f_" + nm, [N, 64], F32) for nm in ("tcs", "tss", "tci", "tsi")}
        negm0 = p.abuf(pfx + "negm0", [128, 16], F32)
        negm1 = p.abuf(pfx + "negm1", [128, 16], F32)
        negb = p.abuf(pfx + "negb", [128, 16], BF16)
        negT = p.abuf(pfx + "negT", [16, 128], BF16)
        pT2 = [p.abuf(pfx + f"pT{i}", [128, 512], BF16) for i in range(2)]
        ctx = [p.abuf(pfx + f"ctx{i}", [128, D], BF16) for i in range(2)]
        ctxT = p.abuf(pfx + "ctxT", [128, D], BF16)
        hl = [p.abuf(pfx + f"hl{i}", [128, D], F32) for i in range(2)]
        ps_tr = [p.psum_bank(6), p.psum_bank(7)]
        ps_qkv = [p.psum_bank(0), p.psum_bank(1)]
        ps_X = [p.psum_bank(2), p.psum_bank(3)]
        ps_ST = [p.psum_bank(4), p.psum_bank(5)]
        ps_PV = [p.psum_bank(2), p.psum_bank(3)]
        ps_W = [p.psum_bank(0), p.psum_bank(1)]

        def tr(src_ap, dst_ap, par, tag, dst_name=None):
            ps = ps_tr[par]
            psv = ps[:].bitcast(BF16)
            w_ = src_ap.shape[1]
            p.mm(lambda e: e.transpose(psv[0:w_, 0:128], src_ap, idb[:]),
                 reads=[_nm(src_ap), idb], writes=[ps], tag=tag + ".t")
            p.op("scalar", lambda e: e.copy(dst_ap, psv[0:w_, 0:128]),
                 reads=[ps], writes=[dst_name or _nm(dst_ap)], tag=tag + ".ev")

        def projw_a(w):
            qrhs = qrhs2[w % 2]
            for tt in range(2):
                t = 2 * w + tt
                par = t % 2
                slot = t % 6
                p.dma("gpsimd", xsb[par][:, 0:512], hrd()[t * 128:(t + 1) * 128, 0:512],
                      key=pfx + f"xsb{par}", tag=f"a{L}.sh0.{t}")
                if t == 0:
                    p.op("vector", lambda e, par=par: e.memset(xsb[par][0:1, 512:1024], 0.0),
                         writes=[xsb[par]], tag=f"a{L}.shz")
                else:
                    p.dma("gpsimd", xsb[par][0:1, 512:1024],
                          hrd()[t * 128 - 1: t * 128, 512:1024],
                          key=pfx + f"xsb{par}", tag=f"a{L}.shb.{t}")
                p.dma("gpsimd", xsb[par][1:128, 512:1024],
                      hrd()[t * 128: t * 128 + 127, 512:1024],
                      key=pfx + f"xsb{par}", tag=f"a{L}.sh1.{t}")
                for nm in ("tcs", "tss", "tci", "tsi"):
                    p.dma("sync", tb[nm][par][:], tbl_d[nm][t * 128:(t + 1) * 128, :],
                          key=pfx + f"{nm}{par}", tag=f"a{L}.tb.{t}")
                for j in range(8):
                    tr(xsb[par][:, j * 128:(j + 1) * 128], xsT[par][:, j * 128:(j + 1) * 128],
                       j % 2, f"a{L}.xT{t}.{j}")
                for c in range(6):
                    cpar = c % 2
                    for j in range(8):
                        p.mm(lambda e, c=c, j=j, par=par, cpar=cpar: e.matmul(
                            ps_qkv[cpar][:], xsT[par][:, j * 128:(j + 1) * 128],
                            Wqkv[:, j * 3072 + c * 512: j * 3072 + (c + 1) * 512],
                            start=(j == 0), stop=(j == 7)),
                            reads=[xsT[par], Wqkv], writes=[ps_qkv[cpar]], tag=f"a{L}.qkv{t}.{c}")
                    if c < 4:
                        p.op("scalar", lambda e, c=c, cpar=cpar: e.copy(
                            qkb[:, c * 512:(c + 1) * 512], ps_qkv[cpar][:]),
                             reads=[ps_qkv[cpar]], writes=[qkb], tag=f"a{L}.qkev{t}.{c}")
                    else:
                        vdst = vw[:, slot * 1040:(slot + 1) * 1040].rearrange(
                            "p (h c2) -> p h c2", c2=65)
                        p.op("scalar", lambda e, c=c, cpar=cpar, vdst=vdst: e.copy(
                            vdst[:, (c - 4) * 8:(c - 3) * 8, 0:64],
                            ps_qkv[cpar][:].rearrange("p (h d) -> p h d", d=64)),
                             reads=[ps_qkv[cpar]], writes=[f"vw_s{slot}"], tag=f"a{L}.vev{t}.{c}")
                qk3 = qkb[:].rearrange("p (s d) -> p s d", d=64)
                r3 = rotb[:].rearrange("p (s d) -> p s d", d=64)
                p.op("vector", lambda e: e.tensor_tensor(scr2[:], qkb[:], qkb[:], ALU.mult),
                     reads=[qkb], writes=[scr2], tag=f"a{L}.sq{t}")
                p.op("vector", lambda e: e.tensor_reduce(
                    nrm[:], scr2[:].rearrange("p (s d) -> p s d", d=64), AX.X, ALU.add),
                     reads=[scr2], writes=[nrm], tag=f"a{L}.ss{t}")
                p.op("scalar", lambda e: e.activation(nrm[:], nrm[:], AF.Sqrt),
                     reads=[nrm], writes=[nrm], tag=f"a{L}.nsq{t}")
                p.op("vector", lambda e: e.tensor_scalar(nrm[:], nrm[:], 1e-12, None, ALU.max),
                     reads=[nrm], writes=[nrm], tag=f"a{L}.ncl{t}")
                p.op("vector", lambda e: e.reciprocal(rnb[:], nrm[:]),
                     reads=[nrm], writes=[rnb], tag=f"a{L}.nrc{t}")
                p.op("vector", lambda e: e.tensor_tensor(
                    qk3, qk3, rnb[:].unsqueeze(2).broadcast_to((128, 32, 64)), ALU.mult),
                     reads=[qkb, rnb], writes=[qkb], tag=f"a{L}.nn{t}")
                p.op("vector", lambda e: e.tensor_tensor(
                    qkb[:, 0:1024].rearrange("p (h d) -> p h d", d=64),
                    qkb[:, 0:1024].rearrange("p (h d) -> p h d", d=64),
                    qs_rep[:].unsqueeze(1).broadcast_to((128, 16, 64)), ALU.mult),
                     reads=[qkb, qs_rep], writes=[qkb], tag=f"a{L}.qs{t}")
                p.op("vector", lambda e: e.tensor_tensor(
                    qkb[:, 1024:2048].rearrange("p (h d) -> p h d", d=64),
                    qkb[:, 1024:2048].rearrange("p (h d) -> p h d", d=64),
                    ks_rep[:].unsqueeze(1).broadcast_to((128, 16, 64)), ALU.mult),
                     reads=[qkb, ks_rep], writes=[qkb], tag=f"a{L}.ks{t}")
                p.op("vector", lambda e: e.tensor_scalar(
                    r3[:, :, 0:32], qk3[:, :, 32:64], -1.0, None, ALU.mult),
                     reads=[qkb], writes=[rotb], tag=f"a{L}.r0{t}")
                p.op("vector", lambda e: e.tensor_copy(r3[:, :, 32:64], qk3[:, :, 0:32]),
                     reads=[qkb], writes=[rotb], tag=f"a{L}.r1{t}")
                for half, (ct, st_) in (( (0, 1024), ("tcs", "tss")), ((1024, 2048), ("tci", "tsi"))):
                    h0, h1 = half
                    p.op("vector", lambda e, h0=h0, h1=h1, ct=ct, par=par: e.tensor_tensor(
                        t1b[:, h0:h1].rearrange("p (h d) -> p h d", d=64),
                        qkb[:, h0:h1].rearrange("p (h d) -> p h d", d=64),
                        tb[ct][par][:].unsqueeze(1).broadcast_to((128, 16, 64)), ALU.mult),
                         reads=[qkb, tb[ct][par]], writes=[t1b], tag=f"a{L}.c{t}.{h0}")
                    p.op("vector", lambda e, h0=h0, h1=h1, st_=st_, par=par: e.tensor_tensor(
                        rotb[:, h0:h1].rearrange("p (h d) -> p h d", d=64),
                        rotb[:, h0:h1].rearrange("p (h d) -> p h d", d=64),
                        tb[st_][par][:].unsqueeze(1).broadcast_to((128, 16, 64)), ALU.mult),
                         reads=[rotb, tb[st_][par]], writes=[rotb], tag=f"a{L}.s{t}.{h0}")
                p.op("vector", lambda e: e.tensor_tensor(qkb[:], t1b[:], rotb[:], ALU.add),
                     reads=[t1b, rotb], writes=[qkb], tag=f"a{L}.ra{t}")
                p.op("vector", lambda e, par=par: e.tensor_copy(qkbb2[par][:], qkb[:]),
                     reads=[qkb], writes=[qkbb2[par]], tag=f"a{L}.rb{t}")
        def projw_b(w):
            qrhs = qrhs2[w % 2]
            for tt in range(2):
                t = 2 * w + tt
                par = t % 2
                slot = t % 6
                qkbb_t = qkbb2[par]
                for h in range(16):
                    tr(qkbb_t[:, h * 64:(h + 1) * 64], qrhs[0:64, h * 256 + tt * 128: h * 256 + (tt + 1) * 128],
                       h % 2, f"a{L}.qT{t}.{h}")
                    tr(qkbb_t[:, 1024 + h * 64: 1024 + (h + 1) * 64],
                       krhs[0:64, h * 768 + slot * 128: h * 768 + (slot + 1) * 128],
                       h % 2, f"a{L}.kT{t}.{h}", dst_name=f"krhs_s{slot}")

        def attnw_x(w):
            qrhs = qrhs2[w % 2]
            def kslices(tq):
                ks = [(tq - 2, mge), (tq - 1, None), (tq, mle)]
                return [(kt, mk) for kt, mk in ks if kt >= 0]
            ksl_q = [kslices(2 * w), kslices(2 * w + 1)]
            for qsub in range(2):
                ksl = ksl_q[qsub]
                negm = (negm0, negm1)[qsub]
                for h in range(16):
                    psx = ps_X[h % 2]
                    for ki, (kt, mk) in enumerate(ksl):
                        p.mm(lambda e, h=h, qsub=qsub, kt=kt, ki=ki, mk=mk, psx=psx, qrhs=qrhs: e.matmul(
                            psx[:, ki * 128:(ki + 1) * 128],
                            qrhs[0:64, h * 256 + qsub * 128: h * 256 + (qsub + 1) * 128],
                            krhs[0:64, h * 768 + (kt % 6) * 128: h * 768 + (kt % 6 + 1) * 128],
                            start=True, stop=(mk is None)),
                            reads=[qrhs, f"krhs_s{kt % 6}"], writes=[psx], tag=f"a{L}.X.{w}.{h}")
                        if mk is not None:
                            p.mm(lambda e, mk=mk, ki=ki, psx=psx: e.matmul(
                                psx[:, ki * 128:(ki + 1) * 128],
                                idb[:], mk[:], start=False, stop=True),
                                reads=[idb, mk], writes=[psx], tag=f"a{L}.Xm.{w}.{h}")
                    p.op("vector", lambda e, h=h, nks=len(ksl), negm=negm, psx=psx: e.tensor_reduce(
                        negm[:, h:h + 1], psx[:, 0: nks * 128],
                        AX.X, ALU.max, negate=True),
                         reads=[psx], writes=[negm], tag=f"a{L}.Xr.{w}.{h}")
                p.op("vector", lambda e, negm=negm: e.tensor_copy(negb[:], negm[:]),
                     reads=[negm], writes=[negb], tag=f"a{L}.nb.{w}.{qsub}")
                psv = ps_tr[0][:].bitcast(BF16)
                p.mm(lambda e, psv=psv: e.transpose(psv[0:16, 0:128], negb[:], idb[:]),
                     reads=[negb, idb], writes=[ps_tr[0]], tag=f"a{L}.nt.{w}.{qsub}")
                p.op("scalar", lambda e, psv=psv: e.copy(negT[:], psv[0:16, 0:128]),
                     reads=[ps_tr[0]], writes=[negT], tag=f"a{L}.ne.{w}.{qsub}")
                p.dma("sync", negscr, negT[:], key=pfx + "negscr", tag=f"a{L}.nd1.{w}.{qsub}")
                p.dma("sync",
                      qrhs[64:65, :].rearrange("z (h b) -> z h b", b=256)[:, :, qsub * 128:(qsub + 1) * 128],
                      negscr.unsqueeze(0),
                      key=pfx + "negscr", tag=f"a{L}.nd2.{w}.{qsub}")

        def attnw_z(w):
            qrhs = qrhs2[w % 2]
            def kslices(tq):
                ks = [(tq - 2, mge), (tq - 1, None), (tq, mle)]
                return [(kt, mk) for kt, mk in ks if kt >= 0]
            ksl_q = [kslices(2 * w), kslices(2 * w + 1)]

            zitems = []
            for h in range(16):
                for qsub in range(2):
                    zitems.append((h, qsub, ksl_q[qsub]))

            def z_st(i):
                h, qsub, ksl = zitems[i]
                pst = ps_ST[i % 2]
                for ki, (kt, mk) in enumerate(ksl):
                    mkT = mle if mk is mge else (mge if mk is mle else None)
                    p.mm(lambda e, h=h, qsub=qsub, kt=kt, ki=ki, mkT=mkT, pst=pst, qrhs=qrhs: e.matmul(
                        pst[:, ki * 128:(ki + 1) * 128],
                        krhs[0:65, h * 768 + (kt % 6) * 128: h * 768 + (kt % 6 + 1) * 128],
                        qrhs[0:65, h * 256 + qsub * 128: h * 256 + (qsub + 1) * 128],
                        start=True, stop=(mkT is None)),
                        reads=[f"krhs_s{kt % 6}", qrhs], writes=[pst], tag=f"a{L}.Z.{w}.{h}")
                    if mkT is not None:
                        p.mm(lambda e, mkT=mkT, ki=ki, pst=pst: e.matmul(
                            pst[:, ki * 128:(ki + 1) * 128],
                            idb[:], mkT[:], start=False, stop=True),
                            reads=[idb, mkT], writes=[pst], tag=f"a{L}.Zm.{w}.{h}")

            def z_exp(i):
                h, qsub, ksl = zitems[i]
                pst = ps_ST[i % 2]
                nks = len(ksl)
                p.op("scalar", lambda e, nks=nks, pst=pst, i=i: e.activation(
                    pT2[i % 2][:, 0: nks * 128], pst[:, 0: nks * 128], AF.Exp, scale=8.0),
                    reads=[pst], writes=[pT2[i % 2]], tag=f"a{L}.Ze.{w}.{h}")

            def z_pv(i):
                h, qsub, ksl = zitems[i]
                ppv = ps_PV[i % 2]
                nks = len(ksl)
                for ki, (kt, mk) in enumerate(ksl):
                    p.mm(lambda e, ki=ki, kt=kt, h=h, nks=nks, ppv=ppv, i=i: e.matmul(
                        ppv[:, 0:65],
                        pT2[i % 2][:, ki * 128:(ki + 1) * 128],
                        vw[:, (kt % 6) * 1040 + h * 65: (kt % 6) * 1040 + (h + 1) * 65],
                        start=(ki == 0), stop=(ki == nks - 1)),
                        reads=[pT2[i % 2], f"vw_s{kt % 6}"], writes=[ppv], tag=f"a{L}.PV.{w}.{h}")
                p.op("vector", lambda e, qsub=qsub, ppv=ppv: e.reciprocal(
                    rnb[:, qsub:qsub + 1], ppv[:, 64:65]),
                     reads=[ppv], writes=[rnb], tag=f"a{L}.Zr.{w}.{h}")
                p.op("vector", lambda e, qsub=qsub, h=h, ppv=ppv: e.tensor_scalar(
                    ctx[qsub][:, h * 64:(h + 1) * 64], ppv[:, 0:64],
                    rnb[:, qsub:qsub + 1], None, ALU.mult),
                     reads=[ppv, rnb], writes=[ctx[qsub]], tag=f"a{L}.Zc.{w}.{h}")

            z_st(0); z_exp(0); z_st(1); z_exp(1)
            for i in range(32):
                z_pv(i)
                if i + 2 < 32:
                    z_st(i + 2)
                    z_exp(i + 2)
            for tt in range(2):
                t = 2 * w + tt
                par = t % 2
                p.dma("scalar", hl[par][:], hrd()[t * 128:(t + 1) * 128, :], key=pfx + f"hl{par}",
                      tag=f"a{L}.hres{t}")
                for j in range(8):
                    tr(ctx[tt][:, j * 128:(j + 1) * 128], ctxT[:, j * 128:(j + 1) * 128],
                       j % 2, f"a{L}.cT{t}.{j}")
                for c in range(2):
                    psw = ps_W[(2 * tt + c) % 2]
                    for j in range(8):
                        p.mm(lambda e, c=c, j=j, psw=psw: e.matmul(
                            psw[:], ctxT[:, j * 128:(j + 1) * 128],
                            Wo[:, j * D + c * 512: j * D + (c + 1) * 512],
                            start=(j == 0), stop=(j == 7)),
                            reads=[ctxT, Wo], writes=[psw], tag=f"a{L}.wo{t}.{c}")
                    p.op("vector", lambda e, c=c, par=par, psw=psw: e.tensor_tensor(
                        hl[par][:, c * 512:(c + 1) * 512], psw[:],
                        hl[par][:, c * 512:(c + 1) * 512], ALU.add),
                         reads=[psw, hl[par]], writes=[hl[par]], tag=f"a{L}.we{t}.{c}")
                p.dma("sync", hwr()[t * 128:(t + 1) * 128, :], hl[par][:], key=pfx + f"hst{par}",
                      tag=f"a{L}.hst{t}")

        projw_a(0)
        projw_b(0)
        for w in range(NW):
            if w + 1 < NW:
                projw_a(w + 1)
            attnw_x(w)
            if w + 1 < NW:
                projw_b(w + 1)
            attnw_z(w)
            p.flush(pfx + f"a{L}.w{w}")

        hcur[0] ^= 1
        p.arena_release(_ml)

      if do_ffn:
        _ml = p.arena_mark()
        W1 = p.abuf(pfx + "W1", [128, 8 * FF2], BF16)
        p.dma("gpsimd", W1[:].rearrange("p (a n) -> p a n", n=FF2),
              W1_d[L].rearrange("(a p) n -> p a n", p=128), key=pfx + "W1", tag=f"f{L}.ldW1")
        W2 = p.abuf(pfx + "W2", [128, 22 * D], BF16)
        p.dma("gpsimd", W2[:].rearrange("p (a n) -> p a n", n=D)[:, 0:21, :],
              W2_d[L][0:2688].rearrange("(a p) n -> p a n", p=128), key=pfx + "W2",
              tag=f"f{L}.ldW2a")
        p.dma("gpsimd", W2[0:42, :].rearrange("p (a n) -> p a n", n=D)[:, 21:22, :],
              W2_d[L][2688:2730].rearrange("(a o) n -> a o n", o=1), key=pfx + "W2",
              tag=f"f{L}.ldW2b")
        b2row = p.abuf(pfx + "b2row", [1, D], BF16)
        p.dma("gpsimd", b2row[:], b2_d[L].rearrange("(o n) -> o n", o=1), key=pfx + "b2row",
              tag=f"f{L}.ldb2")
        b1ar = p.abuf(pfx + "b1ar", [22, 128], F32)
        b1gr = p.abuf(pfx + "b1gr", [22, 128], F32)
        b1ac = p.abuf(pfx + "b1ac", [128, 22], F32)
        b1gc = p.abuf(pfx + "b1gc", [128, 22], F32)
        p.dma("sync", b1ar[:], b1a_d[L].rearrange("(a pp) -> a pp", pp=128), key=pfx + "b1ar",
              tag=f"f{L}.ldb1a")
        p.dma("sync", b1gr[:], b1g_d[L].rearrange("(a pp) -> a pp", pp=128), key=pfx + "b1gr",
              tag=f"f{L}.ldb1g")
        gr = bcast_vec(p, lng_d[L], pfx + "lngr", D)
        br = bcast_vec(p, lnb_d[L], pfx + "lnbr", D)
        ps_tr = [p.psum_bank(6), p.psum_bank(7)]
        for (bsrc, bdst) in ((b1ar, b1ac), (b1gr, b1gc)):
            psv = ps_tr[0]
            p.mm(lambda e, bsrc=bsrc, psv=psv: e.transpose(psv[0:128, 0:22], bsrc[:], idf[0:22, 0:22]),
                 reads=[bsrc, idf], writes=[psv], tag=f"f{L}.b1t")
            p.op("scalar", lambda e, bdst=bdst, psv=psv: e.copy(bdst[:], psv[0:128, 0:22]),
                 reads=[psv], writes=[bdst], tag=f"f{L}.b1e")
        xsf = [p.abuf(pfx + "xsf", [128, D], F32)] * 2
        hl = [p.abuf(pfx + f"fhl{i}", [128, D], F32) for i in range(4)]
        xnb2 = p.abuf(pfx + "xnb2", [128, D], BF16)
        ln_scr_f = dict(ln_scr)
        ln_scr_f["xn"] = p.abuf(pfx + "flnxn", [128, D], F32)
        xnTs = p.abuf(pfx + "xnTs", [128, 8 * 512], BF16)
        ag = p.abuf(pfx + "ag", [128, 22 * 512], BF16)
        asb = p.abuf(pfx + "asb", [128, 512], F32)
        gsb = p.abuf(pfx + "gsb", [128, 512], BF16)
        ps_a = [p.psum_bank(0), p.psum_bank(1)]
        ps_g2 = [p.psum_bank(2), p.psum_bank(3)]
        ps_y = [p.psum_bank(4), p.psum_bank(5)]

        for s in range(8):
            for tt in range(4):
                t = s * 4 + tt
                par = t % 2
                p.dma("sync", hl[tt][:], hrd()[t * 128:(t + 1) * 128, :], key=pfx + f"fhl{tt}",
                      tag=f"f{L}.ldh{t}")
                p.op("vector", lambda e, par=par, tt=tt: e.tensor_copy(xsf[par][:, 0:512], hl[tt][:, 0:512]),
                     reads=[hl[tt]], writes=[xsf[par]], tag=f"f{L}.sh0.{t}")
                if t == 0:
                    p.op("vector", lambda e, par=par: e.memset(xsf[par][0:1, 512:1024], 0.0),
                         writes=[xsf[par]], tag=f"f{L}.shz")
                else:
                    p.dma("gpsimd", xsf[par][0:1, 512:1024],
                          hrd()[t * 128 - 1: t * 128, 512:1024], key=pfx + f"xsf{par}",
                          tag=f"f{L}.shb.{t}")
                p.dma("gpsimd", xsf[par][1:128, 512:1024], hrd()[t * 128: t * 128 + 127, 512:1024],
                      key=pfx + f"xsf{par}", tag=f"f{L}.sh1.{t}")
                ln_tile(p, xsf[par][:], xnb2[:], ln_scr_f, gamma_rep=gr, beta_rep=br)
                for j in range(8):
                    tr128_ffn = tr if False else None
                    ps = ps_tr[j % 2]
                    psv = ps[:].bitcast(BF16)
                    p.mm(lambda e, j=j, psv=psv: e.transpose(
                        psv[0:128, 0:128], xnb2[:, j * 128:(j + 1) * 128], idb[:]),
                        reads=[xnb2, idb], writes=[_nm(ps)], tag=f"f{L}.xT{t}.{j}.t")
                    p.op("scalar", lambda e, j=j, tt=tt, psv=psv: e.copy(
                        xnTs[:, j * 512 + tt * 128: j * 512 + (tt + 1) * 128], psv[0:128, 0:128]),
                        reads=[_nm(ps)], writes=[xnTs], tag=f"f{L}.xT{t}.{j}.e")
            for m in range(AT):
                wdt = 128 if m < 21 else 42
                mpar = m % 2
                for j in range(8):
                    p.mm(lambda e, m=m, j=j, wdt=wdt, mpar=mpar: e.matmul(
                        ps_a[mpar][0:wdt, :],
                        W1[:, j * FF2 + m * 128: j * FF2 + m * 128 + wdt],
                        xnTs[:, j * 512:(j + 1) * 512],
                        start=(j == 0), stop=(j == 7)),
                        reads=[W1, xnTs], writes=[ps_a[mpar]], tag=f"f{L}.A.{s}.{m}")
                    p.mm(lambda e, m=m, j=j, wdt=wdt, mpar=mpar: e.matmul(
                        ps_g2[mpar][0:wdt, :],
                        W1[:, j * FF2 + FFH + m * 128: j * FF2 + FFH + m * 128 + wdt],
                        xnTs[:, j * 512:(j + 1) * 512],
                        start=(j == 0), stop=(j == 7)),
                        reads=[W1, xnTs], writes=[ps_g2[mpar]], tag=f"f{L}.G.{s}.{m}")
                p.op("vector", lambda e, m=m, wdt=wdt, mpar=mpar: e.tensor_scalar(
                    asb[0:wdt, :], ps_a[mpar][0:wdt, :], b1ac[0:wdt, m:m + 1], None, ALU.add),
                     reads=[ps_a[mpar], b1ac], writes=[asb], tag=f"f{L}.ab.{s}.{m}")
                p.op("scalar", lambda e, m=m, wdt=wdt, mpar=mpar: e.activation(
                    gsb[0:wdt, :], ps_g2[mpar][0:wdt, :], AF.Gelu, bias=b1gc[0:wdt, m:m + 1]),
                     reads=[ps_g2[mpar], b1gc], writes=[gsb], tag=f"f{L}.gg.{s}.{m}")
                p.op("vector", lambda e, m=m, wdt=wdt: e.tensor_tensor(
                    ag[0:wdt, m * 512:(m + 1) * 512], asb[0:wdt, :], gsb[0:wdt, :], ALU.mult),
                     reads=[asb, gsb], writes=[ag], tag=f"f{L}.ag.{s}.{m}")
            for tt in range(4):
                t = s * 4 + tt
                par = t % 2
                ypar = tt % 2
                for c in range(2):
                    for m in range(AT):
                        wdt = 128 if m < 21 else 42
                        p.mm(lambda e, c=c, m=m, tt=tt, wdt=wdt, ypar=ypar: e.matmul(
                            ps_y[ypar][:],
                            ag[0:wdt, m * 512 + tt * 128: m * 512 + (tt + 1) * 128],
                            W2[0:wdt, m * D + c * 512: m * D + (c + 1) * 512],
                            start=(m == 0), stop=False),
                            reads=[ag, W2], writes=[ps_y[ypar]], tag=f"f{L}.y.{t}.{c}")
                    p.mm(lambda e, c=c, ypar=ypar: e.matmul(
                        ps_y[ypar][:], onecb[:], b2row[:, c * 512:(c + 1) * 512],
                        start=False, stop=True),
                        reads=[onecb, b2row], writes=[ps_y[ypar]], tag=f"f{L}.yb.{t}.{c}")
                    p.op("vector", lambda e, c=c, tt=tt, ypar=ypar: e.tensor_tensor(
                        hl[tt][:, c * 512:(c + 1) * 512], ps_y[ypar][:],
                        hl[tt][:, c * 512:(c + 1) * 512], ALU.add),
                         reads=[ps_y[ypar], hl[tt]], writes=[hl[tt]], tag=f"f{L}.ye.{t}.{c}")
                p.dma("sync", hwr()[t * 128:(t + 1) * 128, :], hl[tt][:], key=pfx + f"fhst{tt}",
                      tag=f"f{L}.hst{t}")
            p.flush(pfx + f"f{L}.s{s}")
        hcur[0] ^= 1
        p.arena_release(_ml)

    xa = [p.abuf(pfx + f"c1a{i}", [128, D], F32) for i in range(2)]
    xb = [p.abuf(pfx + f"c1b{i}", [128, D], F32) for i in range(2)]
    for t in range(T):
        par = t % 2
        p.dma("sync", xa[par][:], x_src[t * 128:(t + 1) * 128, :], key=pfx + f"c1a{par}",
              tag=f"c1.lda{t}")
        p.dma("scalar", xb[par][:], hrd()[t * 128:(t + 1) * 128, :], key=pfx + f"c1b{par}",
              tag=f"c1.ldb{t}")
        p.op("vector", lambda e, par=par: e.tensor_tensor(xa[par][:], xa[par][:], xb[par][:], ALU.add),
             reads=[xa[par], xb[par]], writes=[xa[par]], tag=f"c1.add{t}")
        p.dma("sync", out_dram[t * 128:(t + 1) * 128, :], xa[par][:], key=pfx + f"c1st{par}",
              tag=f"c1.st{t}")
    p.flush(pfx + "c1")
    p.arena_release(_amark)


def build_l1_ffn(depth=6, do_attn=True, do_ffn=True):
    nc = bass.Bass("TRN2", target_bir_lowering=False)
    p = Prog(nc)
    xin = p.din("x", [N, D], F32)
    out = p.dout("y", [N, D], F32)
    _load_shared_consts(p)
    emit_l1_ffn(p, "H_", xin, out,
                p.din("L1_ffn_Wqkv", [6, D, 3 * D], F32),
                p.din("L1_ffn_q_scale", [6, 64], F32), p.din("L1_ffn_k_scale", [6, 64], F32),
                p.din("L1_ffn_Wo", [6, D, D], F32),
                p.din("L1_ffn_ln_g", [6, D], F32), p.din("L1_ffn_ln_b", [6, D], F32),
                p.din("L1_ffn_W1", [6, D, 5460], F32),
                p.din("L1_ffn_b1a_pad", [6, 2816], F32), p.din("L1_ffn_b1g_pad", [6, 2816], F32),
                p.din("L1_ffn_W2", [6, 2730, D], F32), p.din("L1_ffn_b2", [6, D], F32),
                depth=depth, do_attn=do_attn, do_ffn=do_ffn)
    p.flush("tail")
    nc.finalize()
    return nc


def _load_shared_consts(p):
    idb = p.sbuf("identity_bf", [128, 128], BF16)
    idf = p.sbuf("identity_f32", [128, 128], F32)
    onec = p.sbuf("ones_col", [1, 128], F32)
    p.dma("sync", idb[:], p.din("identity_bf_in", [128, 128], BF16), key="idb")
    p.dma("sync", idf[:], p.din("identity_f32_in", [128, 128], F32), key="idf")
    p.dma("sync", onec[:], p.din("ones_col_in", [1, 128], F32), key="onec")
    onecb = p.sbuf("ones_col_bf", [1, 128], BF16)
    p.dma("gpsimd", onecb[:], p.din("ones_col_in", [1, 128], F32), key="onecb")


def const_inputs():
    c = host_consts()
    return {
        "identity_bf_in": c["identity_bf"],
        "identity_f32_in": c["identity_f32"],
        "ones_col_in": c["ones_col_f32"],
        "l0a_cos": c["l0a_cos"],
        "l0a_sin": c["l0a_sin"],
        "l0f_cos": c["l0f_cos"],
        "l0f_sin": c["l0f_sin"],
        "l1f_tcs": c["l1f_tcs"],
        "l1f_tss": c["l1f_tss"],
        "l1f_tci": c["l1f_tci"],
        "l1f_tsi": c["l1f_tsi"],
        "l1f_mle": c["l1f_mle"],
        "l1f_mge": c["l1f_mge"],
    }


def build_l0_attn():
    nc = bass.Bass("TRN2", target_bir_lowering=False)
    p = Prog(nc)
    x = p.din("x", [N, D], F32)
    out = p.dout("y", [N, D], F32)
    _load_shared_consts(p)
    W = {k: p.din(f"L0_attn_W{k}", [D, D], F32) for k in "qkvo"}
    bvec = {k: p.din(f"L0_attn_b{k}", [D], F32) for k in "qkvo"}
    cosd = p.din("l0a_cos", [N, 64], F32)
    sind = p.din("l0a_sin", [N, 64], F32)
    emit_full_attn(p, "A_", x, x, out, W, bvec, cosd, sind, rope_style="interleaved")
    p.flush("tail")
    nc.finalize()
    return nc


def const_inputs_for(nc):
    import concourse.mybir as _mb
    names = {a.memorylocations[0].name for a in nc.m.functions[0].allocations
             if isinstance(a, _mb.MemoryLocationSet) and a.kind == "ExternalInput"}
    return {k: v for k, v in const_inputs().items() if k in names}


def _b1_pads(b1):
    pa = np.zeros((6, 2816), dtype=np.float32)
    pg = np.zeros((6, 2816), dtype=np.float32)
    pa[:, :2730] = b1[:, :2730]
    pg[:, :2730] = b1[:, 2730:]
    return pa, pg


KERNEL_MODE = "fused"


def _filter_inputs(nc, full):
    import concourse.mybir as _mb
    names = {a.memorylocations[0].name for a in nc.m.functions[0].allocations
             if isinstance(a, _mb.MemoryLocationSet) and a.kind == "ExternalInput"}
    return {k: v for k, v in full.items() if k in names}


def kernel(**inputs):
    consts = const_inputs()
    pa, pg = _b1_pads(inputs["L1_ffn_b1"])
    full = dict(inputs)
    full["L1_ffn_b1a_pad"] = pa
    full["L1_ffn_b1g_pad"] = pg
    full.update(consts)
    xs = np.ascontiguousarray(inputs["L0_attn_hidden_states"])

    if KERNEL_MODE == "unfused":
        cur = [np.ascontiguousarray(xs[i]) for i in range(8)]
        for build in (build_l0_attn, build_l0_ffn, build_l1_attn, build_l1_ffn):
            nc = build()
            nc.finalize()
            base = _filter_inputs(nc, full)
            in_maps = [{**base, "x": cur[i]} for i in range(8)]
            res = run_bass_kernel_spmd(nc, in_maps, core_ids=list(range(8)))
            cur = [np.asarray(res.results[i]["y"]) for i in range(8)]
        return np.stack(cur).astype(np.float32)

    nc = build_fused()
    nc.finalize()
    base = _filter_inputs(nc, full)
    in_maps = [{**base, "x": np.ascontiguousarray(xs[i])} for i in range(8)]
    res = run_bass_kernel_spmd(nc, in_maps, core_ids=list(range(8)))
    return np.stack([np.asarray(res.results[i]["y"]) for i in range(8)]).astype(np.float32)


def build_fused():
    nc = bass.Bass("TRN2", target_bir_lowering=False)
    p = Prog(nc)
    x = p.din("x", [N, D], F32)
    y = p.dout("y", [N, D], F32)
    _load_shared_consts(p)
    xr1 = p.dscratch("xr1", [N, D], F32)
    xr2 = p.dscratch("xr2", [N, D], F32)
    xr3 = p.dscratch("xr3", [N, D], F32)
    W = {k: p.din(f"L0_attn_W{k}", [D, D], F32) for k in "qkvo"}
    bvec = {k: p.din(f"L0_attn_b{k}", [D], F32) for k in "qkvo"}
    emit_full_attn(p, "A_", x, x, xr1, W, bvec,
                   p.din("l0a_cos", [N, 64], F32), p.din("l0a_sin", [N, 64], F32),
                   rope_style="interleaved")
    W = {k[0]: p.din(f"L0_ffn_W{k}", [D, D], F32) for k in ("q", "k", "v", "o")}
    bvec = {k[0]: p.din(f"L0_ffn_b{k}", [D], F32) for k in ("q", "k", "v", "o")}
    xln = p.dscratch("F_xln", [N, D], F32)
    x1 = p.dscratch("F_x1", [N, D], F32)
    x1T = p.dscratch("F_x1T", [T * 8 * 128, 128], BF16)
    emit_full_attn(p, "F_", xr1, xln, x1, W, bvec,
                   p.din("l0f_cos", [N, 64], F32), p.din("l0f_sin", [N, 64], F32),
                   rope_style="half", xln_store=xln,
                   tail_ln=dict(g=p.din("L0_ffn_g1", [D], F32),
                                b=p.din("L0_ffn_be1", [D], F32), outT_hbm=x1T))
    emit_moe(p, "M_", x1, x1T, xr1, xr2,
             p.din("L0_ffn_Wg", [D, 8], F32), p.din("L0_ffn_bg", [8], F32),
             p.din("L0_ffn_W1", [8, D, 4096], F32), p.din("L0_ffn_b1", [8, 4096], F32),
             p.din("L0_ffn_W2", [8, 4096, D], F32), p.din("L0_ffn_b2", [8, D], F32),
             p.din("L0_ffn_g2", [D], F32), p.din("L0_ffn_be2", [D], F32))
    emit_l1_attn(p, "G_", xr2, xr3,
                 p.din("L1_attn_Wqkv", [D, 3 * D], F32), p.din("L1_attn_bqkv", [3 * D], F32),
                 p.din("L1_attn_Wout", [D, D], F32), p.din("L1_attn_bout", [D], F32),
                 p.din("L1_attn_Wgate", [D, D], F32), p.din("L1_attn_bgate", [D], F32),
                 p.din("L1_attn_pos_bias", [24, 16], F32))
    emit_l1_ffn(p, "H_", xr3, y,
                p.din("L1_ffn_Wqkv", [6, D, 3 * D], F32),
                p.din("L1_ffn_q_scale", [6, 64], F32), p.din("L1_ffn_k_scale", [6, 64], F32),
                p.din("L1_ffn_Wo", [6, D, D], F32),
                p.din("L1_ffn_ln_g", [6, D], F32), p.din("L1_ffn_ln_b", [6, D], F32),
                p.din("L1_ffn_W1", [6, D, 5460], F32),
                p.din("L1_ffn_b1a_pad", [6, 2816], F32), p.din("L1_ffn_b1g_pad", [6, 2816], F32),
                p.din("L1_ffn_W2", [6, 2730, D], F32), p.din("L1_ffn_b2", [6, D], F32))
    p.flush("tail")
    return nc


# version 13 (final)
# speedup vs baseline: 1.0238x; 1.0238x over previous
import numpy as np
import ml_dtypes
from contextlib import ExitStack

import concourse.bass as bass
import concourse.mybir as mybir
from concourse.bass_utils import run_bass_kernel_spmd

F32 = mybir.dt.float32
BF16 = mybir.dt.bfloat16
AF = mybir.ActivationFunctionType
ALU = mybir.AluOpType
AX = mybir.AxisListType

N = 4096
D = 1024
T = N // 128
NH = 16
HD = 64
EPS = 1e-5
BF = ml_dtypes.bfloat16

ENGINES = ("tensor", "vector", "scalar", "gpsimd", "sync")


def _l0_attn_rope_tables():
    GRID = 16
    TOKF = GRID * GRID
    ids = np.arange(N)
    f_ids = ids // TOKF
    rem = ids - TOKF * f_ids
    h_ids = rem // GRID
    w_ids = rem - GRID * h_ids
    d_dim = 20
    half = d_dim // 2
    omega = np.arange(half, dtype=np.float64) / (d_dim / 2.0)
    inv = 1.0 / 10000.0 ** omega
    cos = np.ones((N, 64), dtype=np.float64)
    sin = np.zeros((N, 64), dtype=np.float64)
    for c, pos in enumerate((f_ids, h_ids, w_ids)):
        freq = pos[:, None].astype(np.float64) * inv[None, :]
        cs = np.concatenate([np.cos(freq), np.cos(freq)], axis=-1)
        sn = np.concatenate([np.sin(freq), np.sin(freq)], axis=-1)
        cos[:, c * 20:(c + 1) * 20] = cs
        sin[:, c * 20:(c + 1) * 20] = sn
    return cos.astype(np.float32), sin.astype(np.float32)


def _l0_ffn_rope_tables():
    inv_freq = 1.0 / 10000.0 ** (np.arange(0, HD, 2, dtype=np.float64) / HD)
    t = np.arange(N, dtype=np.float64)
    freqs = t[:, None] * inv_freq[None, :]
    emb = np.concatenate([freqs, freqs], axis=-1)
    return np.cos(emb).astype(np.float32), np.sin(emb).astype(np.float32)


def _l1f_rotary_tables():
    DIM_HEAD, WINDOW = 64, 256
    half = np.arange(0, DIM_HEAD, 2, dtype=np.float64)
    inv_freq = 1.0 / 10000.0 ** (half / DIM_HEAD)
    t = np.arange(N, dtype=np.float64)
    freqs = t[:, None] * inv_freq[None, :]
    freqs = np.concatenate([freqs, freqs], axis=-1)
    base_scale = (half + 0.4 * DIM_HEAD) / (1.4 * DIM_HEAD)
    power = (t - N // 2) / (WINDOW // 2)
    scale = base_scale[None, :] ** power[:, None]
    scale = np.concatenate([scale, scale], axis=-1)
    cos, sin = np.cos(freqs), np.sin(freqs)
    return (
        (cos * scale).astype(np.float32), (sin * scale).astype(np.float32),
        (cos / scale).astype(np.float32), (sin / scale).astype(np.float32),
    )


def _l1f_masks():
    i = np.arange(128)[:, None]
    j = np.arange(128)[None, :]
    NEG = np.float32(-1e30)
    m_le = np.where(j > i, NEG, 0.0).astype(np.float32)
    m_ge = np.where(j < i, NEG, 0.0).astype(np.float32)
    return m_le.astype(BF), m_ge.astype(BF)


def host_consts():
    c = {}
    c["identity_bf"] = np.eye(128, dtype=BF)
    c["identity_f32"] = np.eye(128, dtype=np.float32)
    c["ones_col_f32"] = np.ones((1, 128), dtype=np.float32)
    cq, sq = _l0_attn_rope_tables()
    c["l0a_cos"], c["l0a_sin"] = cq, sq
    cf, sf = _l0_ffn_rope_tables()
    c["l0f_cos"], c["l0f_sin"] = cf, sf
    tcs, tss, tci, tsi = _l1f_rotary_tables()
    c["l1f_tcs"], c["l1f_tss"], c["l1f_tci"], c["l1f_tsi"] = tcs, tss, tci, tsi
    mle, mge = _l1f_masks()
    c["l1f_mle"], c["l1f_mge"] = mle, mge
    return c


class Op:
    __slots__ = ("eng", "emit", "reads", "writes", "dma_key", "idx",
                 "inc_sem", "inc_val", "tag", "pe_chain", "block")

    def __init__(self, eng, emit, reads, writes, dma_key, idx, tag, pe_chain):
        self.eng = eng
        self.emit = emit
        self.reads = reads
        self.writes = writes
        self.dma_key = dma_key
        self.idx = idx
        self.inc_sem = None
        self.inc_val = None
        self.tag = tag
        self.pe_chain = pe_chain
        self.block = None


_PROG = [None]


def _nm(x):
    if isinstance(x, str):
        return x
    n = getattr(x, "name", None)
    if n is None:
        raise ValueError(f"can't get name of {x!r}")
    n = n if isinstance(n, str) else str(n)
    if n == "ARENA" and _PROG[0] is not None:
        dt_size = 2 if x.dtype == BF16 else 4
        f32_off = (x.offset * dt_size // 4) % ARENA_F32
        for (s, e2, bn) in _PROG[0]._arena_map:
            if s <= f32_off < e2:
                return bn
        raise ValueError(f"ARENA ap at f32 offset {f32_off} not in any live abuf")
    return n


class Buf:

    def __init__(self, ap, name):
        self.ap = ap
        self.name = name
        self.shape = ap.shape
        self.dtype = ap.dtype

    def __getitem__(self, sl):
        return self.ap[sl]


ARENA_F32 = 51712


class Prog:
    def __init__(self, nc):
        self.nc = nc
        self.es = ExitStack()
        self.ops = []
        self.pending = []
        self.last_w = {}
        self.readers = {}
        self.sb_tensors = {}
        self.dram_aps = {}
        self.sems = {}
        self.sem_counts = {}
        self.n_blocks = 0
        self._psum_pool = None
        self._arena = None
        self._arena_off = 0
        self.abufs = {}
        self._fenced = {}
        self._arena_map = []
        self._key_bind = {}
        self._dsem_rr = 0
        _PROG[0] = self

    def _arena_t(self):
        if self._arena is None:
            self._arena = self.es.enter_context(
                self.nc.sbuf_tensor("ARENA", [128, ARENA_F32], F32))
        return self._arena

    def abuf(self, name, shape, dtype):
        if name in self.abufs:
            b = self.abufs[name]
            assert tuple(b.shape) == tuple(shape) and b.dtype == dtype, name
            return b
        assert len(shape) == 2
        par, n = shape
        f32len = (n * (2 if dtype == BF16 else 4) + 3) // 4
        f32len = (f32len + 7) & ~7
        a = self._arena_t()
        assert self._arena_off + f32len <= ARENA_F32, \
            f"arena full allocating {name}: {self._arena_off}+{f32len}"
        sl = a[:, self._arena_off:self._arena_off + f32len]
        self._arena_map.append((self._arena_off, self._arena_off + f32len, name))
        self._arena_off += f32len
        ap = sl.bitcast(dtype) if dtype != F32 else sl
        ap = ap[0:par, 0:n]
        b = Buf(ap, name)
        self.abufs[name] = b
        return b

    def arena_mark(self):
        return (self._arena_off, set(self.abufs.keys()))

    def arena_release(self, mark):
        off, keep = mark
        self.dma_fence()
        self.flush("arena_release")
        for k in list(self.abufs.keys()):
            if k not in keep:
                del self.abufs[k]
                self.last_w.pop(k, None)
                self.readers.pop(k, None)
        self._arena_map = [m for m in self._arena_map if m[2] in self.abufs]
        self._arena_off = off

    def dma_fence(self):
        targets = [(s, c) for s, c in self.sem_counts.items()
                   if s.startswith("dk_") and c > self._fenced.get(s, 0)]
        if not targets:
            return

        def emit(e):
            for s, c in targets:
                e.wait_ge(self.sems[s], c)
            return None
        self.op("sync", emit, tag="dma_fence")
        for s, c in targets:
            self._fenced[s] = c
        self._key_bind.clear()

    def sem(self, name):
        if name not in self.sems:
            self.sems[name] = self.es.enter_context(self.nc.semaphore(name))
            self.sem_counts[name] = 0
        return self.sems[name]

    def sbuf(self, name, shape, dtype):
        if name not in self.sb_tensors:
            self.sb_tensors[name] = self.es.enter_context(
                self.nc.sbuf_tensor(name, list(shape), dtype))
        return self.sb_tensors[name]

    def psum_bank(self, i):
        if self._psum_pool is None:
            self._pp = [
                self.es.enter_context(self.nc.psum_tensor(f"pp{j}", [128, 1024], F32))
                for j in range(4)]
            self._psum_pool = [
                Buf(self._pp[j // 2][:, (j % 2) * 512:(j % 2 + 1) * 512], f"psb{j}")
                for j in range(8)]
        return self._psum_pool[i]

    def psum_pair(self, j):
        self.psum_bank(0)
        return Buf(self._pp[j][:], f"ppair{j}")

    def constcol(self, value):
        key = f"cc_{value!r}"
        if key not in self.sb_tensors:
            t = self.sbuf(key, [128, 1], F32)
            self.op("vector", lambda e: e.memset(t[:], value), writes=[t], tag=key)
        return self.sb_tensors[key]

    def din(self, name, shape, dtype=F32):
        if name not in self.dram_aps:
            self.dram_aps[name] = self.nc.dram_tensor(
                name, list(shape), dtype, kind="ExternalInput").ap()
        return self.dram_aps[name]

    def dout(self, name, shape, dtype=F32):
        if name not in self.dram_aps:
            self.dram_aps[name] = self.nc.dram_tensor(
                name, list(shape), dtype, kind="ExternalOutput").ap()
        return self.dram_aps[name]

    def dscratch(self, name, shape, dtype, io=None):
        if name not in self.dram_aps:
            kind = {None: "Internal", "in": "ExternalInput", "out": "ExternalOutput"}[io]
            self.dram_aps[name] = self.nc.dram_tensor(
                name, list(shape), dtype, kind=kind).ap()
        return self.dram_aps[name]

    def op(self, eng, emit, reads=(), writes=(), tag=None, pe_chain=False):
        o = Op(eng, emit, [_nm(r) for r in reads], [_nm(w) for w in writes],
               None, len(self.ops), tag, pe_chain)
        self.ops.append(o)
        self.pending.append(o)
        return o

    def _dma_sem(self, eng, key):
        kk = ("g" if eng == "gpsimd" else ("s" if eng == "sync" else "a"), key)
        if kk not in self._key_bind:
            assert len(self._key_bind) < 40, f"dma sem pool exhausted at {kk}"
            free = None
            used = set(self._key_bind.values())
            for i in range(40):
                cand = f"dk_{kk[0]}{(self._dsem_rr + i) % 40}"
                if cand not in used:
                    free = cand
                    break
            self._dsem_rr += 1
            self._key_bind[kk] = free
            self.sem(free)
        return self._key_bind[kk]

    def dma(self, eng, out_ap, in_ap, key, tag=None, extra_reads=(), **kw):
        sem_key = self._dma_sem(eng, key)

        def emit(e):
            return e.dma_start(out_ap, in_ap, **kw)

        o = Op(eng, emit, [_nm(in_ap)] + [_nm(r) for r in extra_reads],
               [_nm(out_ap)], sem_key, len(self.ops), tag, False)
        self.ops.append(o)
        self.pending.append(o)
        return o

    def mm(self, emit, reads=(), writes=(), tag=None):
        return self.op("tensor", emit, reads, writes, tag, pe_chain=True)

    def _lane(self, o):
        return o.dma_key if o.dma_key is not None else o.eng

    def _dep_ops(self, o):
        deps = []
        for r in o.reads:
            deps.extend(self.last_w.get(r, {}).values())
        for wn in o.writes:
            deps.extend(self.last_w.get(wn, {}).values())
            deps.extend(self.readers.get(wn, {}).values())
        out = []
        seen = set()
        for d in deps:
            if d is o or d.idx in seen:
                continue
            seen.add(d.idx)
            if o.pe_chain and d.pe_chain:
                continue
            if o.dma_key is not None and d.dma_key == o.dma_key:
                raw = any(r in d.writes for r in o.reads)
                war = any(w in d.reads for w in o.writes)
                if not raw and not war:
                    continue
            out.append(d)
        return out

    def _update_state(self, o):
        lane = self._lane(o)
        for wn in o.writes:
            self.last_w.setdefault(wn, {})
            self.last_w[wn] = {lane: o}
            self.readers[wn] = {}
        for r in o.reads:
            if r in self.last_w or True:
                self.readers.setdefault(r, {})[lane] = o

    def flush(self, name=None, force=True):
        if not self.pending:
            return
        if not force and len(self.pending) < 6000:
            return
        ops = self.pending
        self.pending = []
        blkno = self.n_blocks
        self.n_blocks += 1
        for o in ops:
            o.block = blkno

        deps_of = {}
        for o in ops:
            deps = self._dep_ops(o)
            kept = []
            for d in deps:
                if d.block is not None and d.block < blkno and d.dma_key is None:
                    continue
                kept.append(d)
            deps_of[o.idx] = kept
            for d in kept:
                if d.inc_sem is None:
                    if d.dma_key is not None:
                        d.inc_sem = d.dma_key
                        self.sem(d.dma_key)
                    else:
                        d.inc_sem = "se_" + d.eng
                        self.sem(d.inc_sem)
            self._update_state(o)

        for o in ops:
            if o.dma_key is not None:
                o.inc_sem = o.dma_key
                self.sem(o.dma_key)

        lanes = {}
        for o in ops:
            if o.inc_sem is not None:
                lanes.setdefault(o.inc_sem, []).append(o)
        for sname, lst in lanes.items():
            lst.sort(key=lambda o: o.idx)
            step = 16 if sname.startswith("dk_") else 1
            base = self.sem_counts[sname]
            for i, o in enumerate(lst):
                o.inc_val = base + step * (i + 1)
            self.sem_counts[sname] = base + step * len(lst)

        per_eng = {e: [o for o in ops if o.eng == e] for e in ENGINES}
        prog = self

        with self.nc.Block() as blk:
            def make(eng_name):
                def body(e):
                    waited = {}
                    for o in per_eng[eng_name]:
                        for d in deps_of[o.idx]:
                            sem, val = d.inc_sem, d.inc_val
                            if d.eng == eng_name and d.dma_key is None and o.dma_key is None and d.pe_chain and o.pe_chain:
                                continue
                            if waited.get(sem, 0) >= val:
                                continue
                            e.wait_ge(prog.sems[sem], val)
                            waited[sem] = val
                        instr = o.emit(e)
                        if o.inc_sem is not None:
                            if instr is None:
                                raise RuntimeError(f"op {o.tag} needs inc, emitted None")
                            step = 16 if o.dma_key is not None else 1
                            instr.then_inc(prog.sems[o.inc_sem], step)
                return body
            for e in ENGINES:
                if per_eng[e]:
                    getattr(blk, e)(make(e))


def ln_tile(p, xt_ap, out_ap, scr, gamma_rep=None, beta_rep=None, out2_ap=None):
    sq, ssq, nsum, nmean, m2, var, sqv, rstd, nmr = (
        scr["sq"], scr["ssq"], scr["nsum"], scr["nmean"], scr["m2"], scr["var"],
        scr["sqv"], scr["rstd"], scr["nmr"])
    xn_name = _nm(xt_ap)
    p.op("scalar", lambda e: e.activation(sq[:], xt_ap, AF.Square, accum_out=ssq[:]),
         reads=[xn_name], writes=[sq, ssq], tag="ln.sq")
    p.op("vector", lambda e: e.tensor_reduce(nsum[:], xt_ap, AX.X, ALU.add, negate=True),
         reads=[xn_name], writes=[nsum], tag="ln.sum")
    p.op("vector", lambda e: e.tensor_scalar(nmean[:], nsum[:], 1.0 / D, None, ALU.mult),
         reads=[nsum], writes=[nmean], tag="ln.mean")
    p.op("vector", lambda e: e.tensor_tensor(m2[:], nmean[:], nmean[:], ALU.mult),
         reads=[nmean], writes=[m2], tag="ln.m2")
    p.op("vector", lambda e: e.tensor_scalar(var[:], ssq[:], 1.0 / D, None, ALU.mult),
         reads=[ssq], writes=[var], tag="ln.var0")
    p.op("vector", lambda e: e.tensor_tensor(var[:], var[:], m2[:], ALU.subtract),
         reads=[var, m2], writes=[var], tag="ln.var")
    epsc = p.constcol(EPS)
    p.op("scalar", lambda e: e.activation(sqv[:], var[:], AF.Sqrt, bias=epsc[:]),
         reads=[var, epsc], writes=[sqv], tag="ln.sqrt")
    p.op("vector", lambda e: e.reciprocal(rstd[:], sqv[:]),
         reads=[sqv], writes=[rstd], tag="ln.rstd")
    p.op("vector", lambda e: e.tensor_tensor(nmr[:], nmean[:], rstd[:], ALU.mult),
         reads=[nmean, rstd], writes=[nmr], tag="ln.nmr")
    if gamma_rep is None:
        p.op("vector", lambda e: e.tensor_scalar(out_ap, xt_ap, rstd[:], nmr[:],
                                                 ALU.mult, ALU.add),
             reads=[xn_name, nmr, rstd], writes=[_nm(out_ap)], tag="ln.out")
        if out2_ap is not None:
            p.op("vector", lambda e: e.tensor_scalar(out2_ap, xt_ap, rstd[:], nmr[:],
                                                     ALU.mult, ALU.add),
                 reads=[xn_name, nmr, rstd], writes=[_nm(out2_ap)], tag="ln.out2")
    else:
        xn_t = scr["xn"]
        p.op("vector", lambda e: e.tensor_scalar(xn_t[:], xt_ap, rstd[:], nmr[:],
                                                 ALU.mult, ALU.add),
             reads=[xn_name, nmr, rstd], writes=[xn_t], tag="ln.norm")
        p.op("vector", lambda e: e.tensor_tensor(xn_t[:], xn_t[:], gamma_rep[:], ALU.mult),
             reads=[xn_t, gamma_rep], writes=[xn_t], tag="ln.g")
        p.op("vector", lambda e: e.tensor_tensor(out_ap, xn_t[:], beta_rep[:], ALU.add),
             reads=[xn_t, beta_rep], writes=[_nm(out_ap)], tag="ln.b")
        if out2_ap is not None:
            p.op("vector", lambda e: e.tensor_copy(out2_ap, out_ap),
                 reads=[_nm(out_ap)], writes=[_nm(out2_ap)], tag="ln.out2")


def transpose128(p, src_ap, dst_ap, ps, idb, tag="tr"):
    p.mm(lambda e: e.transpose(ps[:src_ap.shape[1], :src_ap.shape[0]], src_ap, idb[:]),
         reads=[_nm(src_ap), idb], writes=[ps], tag=tag + ".t")
    p.op("scalar", lambda e: e.copy(dst_ap, ps[:src_ap.shape[1], :src_ap.shape[0]]),
         reads=[ps], writes=[_nm(dst_ap)], tag=tag + ".ev")


def load_weight_bf16(p, dram_ap, sb_name, kdim, ndim):
    kt = (kdim + 127) // 128
    sb = p.abuf(sb_name, [128, kt * ndim], BF16)
    src = dram_ap.rearrange("(a p) n -> p a n", p=128) if kdim >= 128 else None
    if kdim >= 128:
        assert kdim % 128 == 0
        p.dma("gpsimd", sb[:].rearrange("p (a n) -> p a n", n=ndim), src,
              key=sb_name, tag="ldw." + sb_name)
    else:
        p.dma("gpsimd", sb[:kdim, :], dram_ap, key=sb_name, tag="ldw." + sb_name)
    return sb


def bcast_vec(p, dram_vec, sb_name, width, eng="sync"):
    sb = p.abuf(sb_name, [128, width], F32)
    src = dram_vec.rearrange("(o n) -> o n", o=1).broadcast_to((128, width))
    p.dma(eng, sb[:], src, key=sb_name, tag="bc." + sb_name)
    return sb


def emit_full_attn(p, pfx, x_src, resid_src, out_dram, W, bvecs, rope_cos, rope_sin,
                   rope_style="interleaved", xln_store=None, tail_ln=None):
    _amark = p.arena_mark()
    idb = p.sbuf("identity_bf", [128, 128], BF16)
    idf = p.sbuf("identity_f32", [128, 128], F32)
    onec = p.sbuf("ones_col", [1, 128], F32)
    onecb = p.sbuf("ones_col_bf", [1, 128], BF16)

    KT = p.abuf(pfx + "KT", [128, 8 * N], BF16)
    Vp = p.abuf(pfx + "Vp", [128, T * 1040], BF16)
    maxq2 = p.abuf(pfx + "maxq2", [128, 16], F32)
    maxk2 = p.abuf(pfx + "maxk2", [128, 16], F32)
    negC = p.abuf(pfx + "negC", [128, 1], F32)
    _mloc = p.arena_mark()
    qT_hbm = p.dscratch(pfx + "qT", [T * 8 * 128, 128], BF16)
    xnT_hbm = p.dscratch(pfx + "xnT", [T * 8 * 128, 128], BF16)

    scr_bf = p.sbuf("scr_bf", [128, D], BF16)
    ln_scr = {k: p.sbuf("ln_" + k, [128, 1], F32)
              for k in ["ssq", "nsum", "nmean", "m2", "var", "sqv", "rstd", "nmr"]}
    ln_scr["sq"] = scr_bf

    ps_tr = [p.psum_bank(6), p.psum_bank(7)]

    def tr128(src_ap, dst_ap, par, tag, evac_eng="scalar", ps_list=None):
        ps = (ps_list or ps_tr)[par]
        psv = ps[:].bitcast(BF16)
        p.mm(lambda e: e.transpose(psv[:, 0:128], src_ap, idb[:]),
             reads=[_nm(src_ap), idb], writes=[ps], tag=tag + ".t")
        p.op(evac_eng, lambda e: e.copy(dst_ap, psv[:, 0:128]) if evac_eng == "scalar"
             else e.tensor_copy(dst_ap, psv[:, 0:128]),
             reads=[ps], writes=[_nm(dst_ap)], tag=tag + ".ev")

    xt = [p.abuf(pfx + f"xt{i}", [128, D], F32) for i in range(2)]
    xnb = [p.abuf(pfx + f"xnb{i}", [128, D], BF16) for i in range(2)]
    xnTt = [p.abuf(pfx + f"xnTt{i}", [128, D], BF16) for i in range(2)]
    xnf = [p.abuf(pfx + f"xnf{i}", [128, D], F32) for i in range(2)] if xln_store is not None else None
    p.op("vector", lambda e: e.memset(Vp[:], 1.0), writes=[Vp], tag="vp.ones")
    p.op("vector", lambda e: e.memset(maxq2[:], 0.0), writes=[maxq2], tag="mq0")
    p.op("vector", lambda e: e.memset(maxk2[:], 0.0), writes=[maxk2], tag="mk0")
    for t in range(T):
        par = t % 2
        p.dma("sync", xt[par][:], x_src[t * 128:(t + 1) * 128, :], key=pfx + f"xt{par}",
              tag=f"ld.x{t}")
        ln_tile(p, xt[par][:], xnb[par][:], ln_scr)
        if xln_store is not None:
            p.op("vector", lambda e, par=par: e.tensor_scalar(
                xnf[par][:], xt[par][:], ln_scr["rstd"][:], ln_scr["nmr"][:], ALU.mult, ALU.add),
                 reads=[xt[par], ln_scr["nmr"], ln_scr["rstd"]], writes=[xnf[par]], tag="ln.f32")
            p.dma("scalar", xln_store[t * 128:(t + 1) * 128, :], xnf[par][:],
                  key=pfx + f"xlnst{par}", tag=f"st.xln{t}")
        for j in range(8):
            tr128(xnb[par][:, j * 128:(j + 1) * 128], xnTt[par][:, j * 128:(j + 1) * 128],
                  j % 2, f"xnT{t}.{j}")
        p.dma("sync", xnT_hbm[t * 1024:(t + 1) * 1024, :].rearrange("(j q) b -> q j b", j=8),
              xnTt[par][:].rearrange("p (j b) -> p j b", b=128),
              key=pfx + f"xnTst{par}", tag=f"st.xnT{t}")
    p.flush(pfx + "pass1a", force=False)

    xnTl = [p.abuf(pfx + f"xnTl{i}", [128, D], BF16) for i in range(2)]
    qf = p.abuf(pfx + "qf", [128, D], BF16)
    rot = p.abuf(pfx + "rot", [128, D], BF16)
    t1b = p.abuf(pfx + "t1b", [128, D], BF16)
    qb = p.abuf(pfx + "qb", [128, D], BF16)
    nr2 = p.abuf(pfx + "nr2", [128, 16], F32)
    qTt = p.abuf(pfx + "qTt", [128, 8 * 128], BF16)
    ps_p = [p.psum_bank(0), p.psum_bank(1)]

    def rope_apply(src_bf, toff, out_bf, cs, snt):
        cs_v = cs[:, 0:64].unsqueeze(1).broadcast_to((128, 16, 64))
        sn_v = snt[:, 0:64].unsqueeze(1).broadcast_to((128, 16, 64))
        if rope_style == "interleaved":
            s3 = src_bf[:].rearrange("p (x two) -> p x two", two=2)
            r3 = rot[:].rearrange("p (x two) -> p x two", two=2)
            p.op("vector", lambda e: e.tensor_scalar(r3[:, :, 0:1], s3[:, :, 1:2], -1.0, None, ALU.mult),
                 reads=[src_bf], writes=[rot], tag="rope.r0")
            p.op("vector", lambda e: e.tensor_copy(r3[:, :, 1:2], s3[:, :, 0:1]),
                 reads=[src_bf], writes=[rot], tag="rope.r1")
        else:
            s3 = src_bf[:].rearrange("p (h x) -> p h x", x=64)
            r3 = rot[:].rearrange("p (h x) -> p h x", x=64)
            p.op("vector", lambda e: e.tensor_scalar(r3[:, :, 0:32], s3[:, :, 32:64], -1.0, None, ALU.mult),
                 reads=[src_bf], writes=[rot], tag="rope.r0")
            p.op("vector", lambda e: e.tensor_copy(r3[:, :, 32:64], s3[:, :, 0:32]),
                 reads=[src_bf], writes=[rot], tag="rope.r1")
        p.op("vector", lambda e: e.tensor_tensor(
            t1b[:].rearrange("p (h d) -> p h d", d=64),
            src_bf[:].rearrange("p (h d) -> p h d", d=64), cs_v, ALU.mult),
             reads=[src_bf, cs], writes=[t1b], tag="rope.c")
        p.op("vector", lambda e: e.tensor_tensor(
            rot[:].rearrange("p (h d) -> p h d", d=64),
            rot[:].rearrange("p (h d) -> p h d", d=64), sn_v, ALU.mult),
             reads=[rot, snt], writes=[rot], tag="rope.s")
        p.op("vector", lambda e: e.tensor_tensor(out_bf[:], t1b[:], rot[:], ALU.add),
             reads=[t1b, rot], writes=[out_bf], tag="rope.add")

    for kind in ("k", "v", "q"):
        Wsb = load_weight_bf16(p, W[kind], pfx + "Wp", D, D)
        brow = p.abuf(pfx + "brow", [1, D], BF16)
        p.dma("gpsimd", brow[:], bvecs[kind].rearrange("(o n) -> o n", o=1), key=pfx + "brow",
              tag="ld.brow")
        csl = [p.abuf(pfx + f"csl{i}", [128, 64], F32) for i in range(2)]
        snl = [p.abuf(pfx + f"snl{i}", [128, 64], F32) for i in range(2)]
        p.flush(pfx + f"ldw.{kind}", force=False)
        for t in range(T):
            par = t % 2
            p.dma("sync", xnTl[par][:].rearrange("p (j b) -> p j b", b=128),
                  xnT_hbm[t * 1024:(t + 1) * 1024, :].rearrange("(j q) b -> q j b", j=8),
                  key=pfx + f"xnTl{par}", tag=f"ld.xnT{t}")
            if kind != "v":
                p.dma("sync", csl[par][:], rope_cos[t * 128:(t + 1) * 128, :],
                      key=pfx + f"csl{par}", tag=f"ld.cs{t}")
                p.dma("sync", snl[par][:], rope_sin[t * 128:(t + 1) * 128, :],
                      key=pfx + f"snl{par}", tag=f"ld.sn{t}")
            for c in range(2):
                for j in range(8):
                    p.mm(lambda e, c=c, j=j, par=par, Wsb=Wsb: e.matmul(
                        ps_p[c][:],
                        xnTl[par][:, j * 128:(j + 1) * 128],
                        Wsb[:, j * D + c * 512: j * D + (c + 1) * 512],
                        start=(j == 0), stop=False),
                        reads=[xnTl[par], Wsb], writes=[ps_p[c]], tag=f"prj.{kind}{t}.{c}")
                p.mm(lambda e, c=c, brow=brow: e.matmul(
                    ps_p[c][:], onecb[:], brow[:, c * 512:(c + 1) * 512],
                    start=False, stop=True),
                    reads=[onecb, brow], writes=[ps_p[c]], tag=f"prjb.{kind}{t}.{c}")
            if kind == "v":
                vdst = Vp[:, t * 1040: (t + 1) * 1040].rearrange("p (h c) -> p h c", c=65)
                for c in range(2):
                    p.op("scalar", lambda e, c=c, vdst=vdst: e.copy(
                        vdst[:, c * 8:(c + 1) * 8, 0:64],
                        ps_p[c][:].rearrange("p (h d) -> p h d", d=64)),
                         reads=[ps_p[c]], writes=[Vp], tag=f"vev{t}.{c}")
            else:
                for c in range(2):
                    p.op("scalar", lambda e, c=c: e.copy(qf[:, c * 512:(c + 1) * 512], ps_p[c][:]),
                         reads=[ps_p[c]], writes=[qf], tag=f"pev{t}.{c}")
                rope_apply(qf, t, qb, csl[par], snl[par])
                acc = maxq2 if kind == "q" else maxk2
                p.op("vector", lambda e: e.tensor_tensor(scr_bf[:], qb[:], qb[:], ALU.mult),
                     reads=[qb], writes=[scr_bf], tag="nrm.sq")
                p.op("vector", lambda e: e.tensor_reduce(
                    nr2[:], scr_bf[:].rearrange("p (h d) -> p h d", d=64), AX.X, ALU.add),
                     reads=[scr_bf], writes=[nr2], tag="nrm.red")
                p.op("vector", lambda e, acc=acc: e.tensor_tensor(acc[:], acc[:], nr2[:], ALU.max),
                     reads=[acc, nr2], writes=[acc], tag="nrm.max")
                if kind == "q":
                    for j in range(8):
                        tr128(qb[:, j * 128:(j + 1) * 128], qTt[:, j * 128:(j + 1) * 128],
                              j % 2, f"qT{t}.{j}")
                    p.dma("sync", qT_hbm[t * 1024:(t + 1) * 1024, :].rearrange("(j q) b -> q j b", j=8),
                          qTt[:].rearrange("p (j b) -> p j b", b=128),
                          key=pfx + "qTst", tag=f"st.qT{t}")
                else:
                    for j in range(8):
                        tr128(qb[:, j * 128:(j + 1) * 128],
                              KT[:, j * N + t * 128: j * N + (t + 1) * 128], j % 2, f"kT{t}.{j}")
        p.flush(pfx + f"proj.{kind}", force=False)

    ps_s = p.psum_bank(7)
    gq = p.abuf(pfx + "gq", [128, 1], F32)
    gk = p.abuf(pfx + "gk", [128, 1], F32)
    gqk = p.abuf(pfx + "gqk", [128, 2], F32)
    g1 = p.abuf(pfx + "g1", [1, 128], F32)
    gsc = p.abuf(pfx + "gsc", [1, 1], F32)
    p.op("vector", lambda e: e.tensor_reduce(gq[:], maxq2[:], AX.X, ALU.max),
         reads=[maxq2], writes=[gq], tag="gC.rq")
    p.op("vector", lambda e: e.tensor_reduce(gk[:], maxk2[:], AX.X, ALU.max),
         reads=[maxk2], writes=[gk], tag="gC.rk")
    p.op("vector", lambda e: e.tensor_copy(gqk[:, 0:1], gq[:]), reads=[gq], writes=[gqk], tag="gC.c0")
    p.op("vector", lambda e: e.tensor_copy(gqk[:, 1:2], gk[:]), reads=[gk], writes=[gqk], tag="gC.c1")
    gqkb = p.abuf(pfx + "gqkb", [128, 2], BF16)
    p.op("vector", lambda e: e.tensor_copy(gqkb[:], gqk[:]), reads=[gqk], writes=[gqkb], tag="gC.cb")
    psv = ps_s[:].bitcast(BF16)
    p.mm(lambda e: e.transpose(psv[0:2, 0:128], gqkb[:], idb[:]),
         reads=[gqkb, idb], writes=[ps_s], tag="gC.t")
    gq_r = p.abuf(pfx + "gq_r", [2, 128], F32)
    gk_r = p.abuf(pfx + "gk_r", [1, 128], F32)
    p.op("scalar", lambda e: e.copy(gq_r[:], psv[0:2, 0:128]), reads=[ps_s], writes=[gq_r], tag="gC.e0")
    p.dma("sync", gk_r[0:1, :], gq_r[1:2, :], key="gkr", tag="gC.mv")
    p.op("vector", lambda e: e.tensor_reduce(g1[0:1, 0:1], gq_r[0:1, :], AX.X, ALU.max),
         reads=[gq_r], writes=[g1], tag="gC.m0")
    p.op("vector", lambda e: e.tensor_reduce(g1[0:1, 1:2], gk_r[0:1, :], AX.X, ALU.max),
         reads=[gk_r], writes=[g1], tag="gC.m1")
    p.op("scalar", lambda e: e.activation(g1[:, 2:4], g1[:, 0:2], AF.Sqrt),
         reads=[g1], writes=[g1], tag="gC.sqrt")
    p.op("vector", lambda e: e.tensor_tensor(gsc[:], g1[:, 2:3], g1[:, 3:4], ALU.mult),
         reads=[g1], writes=[gsc], tag="gC.mul")
    p.op("vector", lambda e: e.tensor_scalar(gsc[:], gsc[:], -0.125, None, ALU.mult),
         reads=[gsc], writes=[gsc], tag="gC.neg")
    p.mm(lambda e: e.matmul(ps_s[:, 0:1], onec[:], gsc[:], start=True, stop=True),
         reads=[onec, gsc], writes=[ps_s], tag="gC.bc")
    p.op("scalar", lambda e: e.copy(negC[:], ps_s[:, 0:1]), reads=[ps_s], writes=[negC], tag="gC.ev")
    p.flush(pfx + "negC", force=False)
    p.arena_release(_mloc)

    Wo = load_weight_bf16(p, W["o"], pfx + "Wo", D, D)
    borow = p.abuf(pfx + "borow", [1, D], BF16)
    p.dma("gpsimd", borow[:], bvecs["o"].rearrange("(o n) -> o n", o=1), key=pfx + "borow",
          tag="ld.borow")
    qt_s = [p.abuf(pfx + "qt_s", [128, 8 * 512], BF16)] * 2
    pT = [p.abuf(pfx + f"pT{i}", [128, 1024], BF16) for i in range(2)]
    ps_Sp = [p.psum_pair(0), p.psum_pair(1)]
    ps_O = [p.psum_bank(4), p.psum_bank(5)]
    ps_rbc = p.psum_bank(6)
    ps_W = [p.psum_bank(7), p.psum_bank(7)]
    r_row = p.abuf(pfx + "r_row", [1, 512], F32)
    rbc = p.abuf(pfx + "rbc", [64, 512], F32)
    ctxT = p.abuf(pfx + "ctxT", [128, 8 * 512], BF16)
    xr = [p.abuf(pfx + f"xr{i}", [128, D], F32) for i in range(2)]
    xo = [p.abuf(pfx + f"xo{i}", [128, D], F32) for i in range(2)]
    if tail_ln is not None:
        tail_ln["g_rep"] = bcast_vec(p, tail_ln["g"], pfx + "grep", D)
        tail_ln["b_rep"] = bcast_vec(p, tail_ln["b"], pfx + "brep", D)
        tail_ln["zb"] = p.sbuf("scr_bf", [128, D], BF16)
        zTt = p.abuf(pfx + "zTt", [128, 8 * 128], BF16)
        ln_scr2 = dict(ln_scr)
        ln_scr2["xn"] = p.abuf(pfx + "ln_xn", [128, D], F32)

    for s in range(8):
        spar = s % 2
        for tt in range(4):
            tg = s * 4 + tt
            p.dma("sync",
                  qt_s[spar][:].rearrange("p (j b) -> p j b", b=512)[:, :, tt * 128:(tt + 1) * 128],
                  qT_hbm[tg * 1024:(tg + 1) * 1024, :].rearrange("(j q) b -> q j b", j=8),
                  key=pfx + f"qt_s{spar}", tag=f"ld.qT{s}.{tt}")
        for h in range(NH):
            j, off, hop = h // 2, (h % 2) * 64, h % 2

            def S_mm(m):
                mp = m % 2
                for u in (0, 1):
                    kt = 2 * m + u
                    p.mm(lambda e, j=j, off=off, kt=kt, u=u, mp=mp, spar=spar: e.matmul(
                        ps_Sp[mp][:, u * 512:(u + 1) * 512],
                        KT[off:off + 64, j * N + kt * 128: j * N + (kt + 1) * 128],
                        qt_s[spar][off:off + 64, j * 512:(j + 1) * 512],
                        start=True, stop=True),
                        reads=[KT, qt_s[spar]], writes=[ps_Sp[mp]], tag=f"S.{s}.{h}.{kt}")

            def EXP(m):
                mp = m % 2
                p.op("scalar", lambda e, mp=mp: e.activation(
                    pT[mp][:], ps_Sp[mp][:], AF.Exp, bias=negC[:], scale=0.125),
                    reads=[ps_Sp[mp], negC], writes=[pT[mp]], tag=f"exp.{s}.{h}.{m}")

            def PV(kt):
                mp = (kt // 2) % 2
                u = kt % 2
                p.mm(lambda e, kt=kt, mp=mp, u=u, h=h, hop=hop: e.matmul(
                    ps_O[hop][0:65, :],
                    Vp[:, kt * 1040 + h * 65: kt * 1040 + (h + 1) * 65],
                    pT[mp][:, u * 512:(u + 1) * 512],
                    start=(kt == 0), stop=(kt == T - 1)),
                    reads=[Vp, pT[mp]], writes=[ps_O[hop]], tag=f"PV.{s}.{h}.{kt}")

            S_mm(0); EXP(0); S_mm(1); EXP(1)
            for m in range(T // 2):
                PV(2 * m)
                PV(2 * m + 1)
                if m + 2 < T // 2:
                    S_mm(m + 2)
                    EXP(m + 2)
            p.op("vector", lambda e, hop=hop: e.reciprocal(r_row[:], ps_O[hop][64:65, :]),
                 reads=[ps_O[hop]], writes=[r_row], tag=f"rcp.{s}.{h}")
            p.mm(lambda e: e.matmul(ps_rbc[0:64, :], onec[:, 0:64], r_row[:], start=True, stop=True),
                 reads=[onec, r_row], writes=[ps_rbc], tag=f"rbc.{s}.{h}")
            p.op("scalar", lambda e: e.copy(rbc[:], ps_rbc[0:64, :]),
                 reads=[ps_rbc], writes=[rbc], tag=f"rbce.{s}.{h}")
            p.op("vector", lambda e, j=j, off=off, hop=hop: e.tensor_tensor(
                ctxT[off:off + 64, j * 512:(j + 1) * 512], ps_O[hop][0:64, :], rbc[:], ALU.mult),
                 reads=[ps_O[hop], rbc], writes=[ctxT], tag=f"ctx.{s}.{h}")
        for tt in range(4):
            tg = s * 4 + tt
            tpar = tt % 2
            p.dma("sync", xr[tpar][:], resid_src[tg * 128:(tg + 1) * 128, :],
                  key=pfx + f"xr{tpar}", tag=f"ld.res{tg}")
            for c in range(2):
                for j in range(8):
                    p.mm(lambda e, c=c, j=j, tt=tt, tpar=tpar: e.matmul(
                        ps_W[tpar][:],
                        ctxT[:, j * 512 + tt * 128: j * 512 + tt * 128 + 128],
                        Wo[:, j * D + c * 512: j * D + (c + 1) * 512],
                        start=(j == 0), stop=False),
                        reads=[ctxT, Wo], writes=[ps_W[tpar]], tag=f"wo.{tg}.{c}")
                p.mm(lambda e, c=c, tpar=tpar: e.matmul(
                    ps_W[tpar][:], onecb[:], borow[:, c * 512:(c + 1) * 512],
                    start=False, stop=True),
                    reads=[onecb, borow], writes=[ps_W[tpar]], tag=f"wob.{tg}.{c}")
                p.op("vector", lambda e, c=c, tpar=tpar: e.tensor_tensor(
                    xo[tpar][:, c * 512:(c + 1) * 512], ps_W[tpar][:],
                    xr[tpar][:, c * 512:(c + 1) * 512], ALU.add),
                     reads=[ps_W[tpar], xr[tpar]], writes=[xo[tpar]], tag=f"woe.{tg}.{c}")
            if tail_ln is None:
                p.dma("sync", out_dram[tg * 128:(tg + 1) * 128, :], xo[tpar][:],
                      key=pfx + f"xo{tpar}", tag=f"st.out{tg}")
            else:
                zb = tail_ln["zb"]
                ln_tile(p, xo[tpar][:], xo[tpar][:], ln_scr2,
                        gamma_rep=tail_ln["g_rep"], beta_rep=tail_ln["b_rep"],
                        out2_ap=zb[:])
                p.dma("sync", out_dram[tg * 128:(tg + 1) * 128, :], xo[tpar][:],
                      key=pfx + f"xo{tpar}", tag=f"st.out{tg}")
                for j in range(8):
                    tr128(zb[:, j * 128:(j + 1) * 128], zTt[:, j * 128:(j + 1) * 128],
                          j % 2, f"zT{tg}.{j}", ps_list=(ps_O[0], ps_O[1]))
                p.dma("scalar",
                      tail_ln["outT_hbm"][tg * 1024:(tg + 1) * 1024, :].rearrange("(j q) b -> q j b", j=8),
                      zTt[:].rearrange("p (j b) -> p j b", b=128),
                      key=pfx + "zTst", tag=f"st.zT{tg}")
        p.flush(pfx + f"attn_s{s}", force=False)
    p.arena_release(_amark)


def emit_moe(p, pfx, x1_hbm, x1T_hbm, resid_src, out_dram, Wg_d, bg_d, W1_d, b1_d,
             W2_d, b2_d, g2_d, be2_d):
    E, FF = 8, 4096
    _amark = p.arena_mark()
    idb = p.sbuf("identity_bf", [128, 128], BF16)
    onecb = p.sbuf("ones_col_bf", [1, 128], BF16)
    onec = p.sbuf("ones_col", [1, 128], F32)
    ln_scr = {k: p.sbuf("ln_" + k, [128, 1], F32)
              for k in ["ssq", "nsum", "nmean", "m2", "var", "sqv", "rstd", "nmr"]}
    ln_scr["sq"] = p.sbuf("scr_bf", [128, D], BF16)
    moe_hbm = p.dscratch(pfx + "moe", [N, D], F32)

    comb = p.abuf(pfx + "comb", [128, T * 8], F32)
    Wgsb = load_weight_bf16(p, Wg_d, pfx + "Wg", D, 8)
    bgrow = p.abuf(pfx + "bgrow", [1, 8], BF16)
    p.dma("gpsimd", bgrow[:], bg_d.rearrange("(o n) -> o n", o=1), key=pfx + "bgrow")
    x1Tl = [p.abuf(pfx + f"x1Tl{i}", [128, D], BF16) for i in range(2)]
    ps_g = p.psum_bank(0)
    probs = p.abuf(pfx + "probs", [128, 8], F32)
    negm = p.abuf(pfx + "negm", [128, 1], F32)
    lsum = p.abuf(pfx + "lsum", [128, 1], F32)
    rden = p.abuf(pfx + "rden", [128, 1], F32)
    m1 = p.abuf(pfx + "m1", [128, 1], F32)
    m2c = p.abuf(pfx + "m2c", [128, 1], F32)
    msk = p.abuf(pfx + "msk", [128, 8], F32)
    pmk = p.abuf(pfx + "pmk", [128, 8], F32)
    p.flush(pfx + "gate_ld", force=False)
    for t in range(T):
        par = t % 2
        p.dma("sync", x1Tl[par][:].rearrange("p (j b) -> p j b", b=128),
              x1T_hbm[t * 1024:(t + 1) * 1024, :].rearrange("(j q) b -> q j b", j=8),
              key=pfx + f"x1Tl{par}", tag=f"g.ldT{t}")
        for j in range(8):
            p.mm(lambda e, j=j, par=par: e.matmul(
                ps_g[:, 0:8], x1Tl[par][:, j * 128:(j + 1) * 128],
                Wgsb[:, j * 8:(j + 1) * 8], start=(j == 0), stop=False),
                reads=[x1Tl[par], Wgsb], writes=[ps_g], tag=f"g.mm{t}")
        p.mm(lambda e: e.matmul(ps_g[:, 0:8], onecb[:], bgrow[:], start=False, stop=True),
             reads=[onecb, bgrow], writes=[ps_g], tag=f"g.b{t}")
        p.op("vector", lambda e: e.tensor_reduce(negm[:], ps_g[:, 0:8], AX.X, ALU.max, negate=True),
             reads=[ps_g], writes=[negm], tag=f"g.max{t}")
        p.op("scalar", lambda e: e.activation(probs[:], ps_g[:, 0:8], AF.Exp, bias=negm[:],
                                              accum_out=lsum[:]),
             reads=[ps_g, negm], writes=[probs, lsum], tag=f"g.exp{t}")
        p.op("vector", lambda e: e.reciprocal(rden[:], lsum[:]),
             reads=[lsum], writes=[rden], tag=f"g.rc{t}")
        p.op("vector", lambda e: e.tensor_scalar(probs[:], probs[:], rden[:], None, ALU.mult),
             reads=[probs, rden], writes=[probs], tag=f"g.n{t}")
        p.op("vector", lambda e: e.tensor_reduce(m1[:], probs[:], AX.X, ALU.max),
             reads=[probs], writes=[m1], tag=f"g.m1{t}")
        p.op("vector", lambda e: e.tensor_scalar(msk[:], probs[:], m1[:], None, ALU.is_ge),
             reads=[probs, m1], writes=[msk], tag=f"g.k1{t}")
        p.op("vector", lambda e: e.tensor_scalar(pmk[:], msk[:], -1e30, None, ALU.mult),
             reads=[msk], writes=[pmk], tag=f"g.pm{t}")
        p.op("vector", lambda e: e.tensor_tensor(pmk[:], pmk[:], probs[:], ALU.add),
             reads=[pmk, probs], writes=[pmk], tag=f"g.pm2{t}")
        p.op("vector", lambda e: e.tensor_reduce(m2c[:], pmk[:], AX.X, ALU.max),
             reads=[pmk], writes=[m2c], tag=f"g.m2{t}")
        p.op("vector", lambda e: e.tensor_scalar(msk[:], probs[:], m2c[:], None, ALU.is_ge),
             reads=[probs, m2c], writes=[msk], tag=f"g.k2{t}")
        p.op("vector", lambda e: e.tensor_tensor(m1[:], m1[:], m2c[:], ALU.add),
             reads=[m1, m2c], writes=[m1], tag=f"g.den{t}")
        p.op("vector", lambda e: e.reciprocal(rden[:], m1[:]),
             reads=[m1], writes=[rden], tag=f"g.rd{t}")
        p.op("vector", lambda e, t=t: e.scalar_tensor_tensor(
            comb[:, t * 8:(t + 1) * 8], probs[:], rden[:], msk[:], ALU.mult, ALU.mult),
             reads=[probs, rden, msk], writes=[comb], tag=f"g.c{t}")
    p.flush(pfx + "gate", force=False)

    _m2 = p.arena_mark()
    idf = p.sbuf("identity_f32", [128, 128], F32)
    b1r = p.abuf(pfx + "b1r", [32, 128], F32)
    W1sb = p.abuf(pfx + "W1sb", [128, 8 * FF], BF16)
    W2sb = p.abuf(pfx + "W2sb", [128, 32 * D], BF16)
    b1c = p.abuf(pfx + "b1c", [128, 32], F32)
    b2row = p.abuf(pfx + "b2row", [1, D], BF16)
    x1Ts = [p.abuf(pfx + f"x1Ts{i}", [128, 8 * 512], BF16) for i in range(2)]
    hT = p.abuf(pfx + "hT", [128, 32 * 512], BF16)
    mo = [p.abuf(pfx + f"mo{i}", [128, D], F32) for i in range(2)]
    ps_h = [p.psum_bank(0), p.psum_bank(1)]
    ps_y = [p.psum_bank(2), p.psum_bank(3)]

    for e in range(E):
        p.dma("gpsimd", W1sb[:].rearrange("p (a n) -> p a n", n=FF),
              W1_d[e].rearrange("(a p) n -> p a n", p=128), key=pfx + "W1sb",
              tag=f"ldW1.{e}")
        p.dma("gpsimd", W2sb[:].rearrange("p (a n) -> p a n", n=D),
              W2_d[e].rearrange("(a p) n -> p a n", p=128), key=pfx + "W2sb",
              tag=f"ldW2.{e}")
        p.dma("sync", b1r[:], b1_d[e].rearrange("(a p) -> a p", p=128), key=pfx + "b1r",
              tag=f"ldb1.{e}")
        p.mm(lambda e_: e_.transpose(ps_h[0][0:128, 0:32], b1r[:], idf[0:32, 0:32]),
             reads=[b1r, idf], writes=[ps_h[0]], tag=f"trb1.{e}")
        p.op("scalar", lambda e_: e_.copy(b1c[:], ps_h[0][0:128, 0:32]),
             reads=[ps_h[0]], writes=[b1c], tag=f"evb1.{e}")
        p.dma("gpsimd", b2row[:], b2_d[e].rearrange("(o n) -> o n", o=1), key=pfx + "b2row",
              tag=f"ldb2.{e}")
        for s in range(8):
            spar = s % 2
            for tt in range(4):
                tg = s * 4 + tt
                p.dma("sync",
                      x1Ts[spar][:].rearrange("p (j b) -> p j b", b=512)[:, :, tt * 128:(tt + 1) * 128],
                      x1T_hbm[tg * 1024:(tg + 1) * 1024, :].rearrange("(j q) b -> q j b", j=8),
                      key=pfx + f"x1Ts{spar}", tag=f"m.ldT{e}.{s}.{tt}")
            for fb in range(32):
                fpar = fb % 2
                for j in range(8):
                    p.mm(lambda e_, j=j, fb=fb, fpar=fpar, spar=spar: e_.matmul(
                        ps_h[fpar][:],
                        W1sb[:, j * FF + fb * 128: j * FF + (fb + 1) * 128],
                        x1Ts[spar][:, j * 512:(j + 1) * 512],
                        start=(j == 0), stop=(j == 7)),
                        reads=[W1sb, x1Ts[spar]], writes=[ps_h[fpar]], tag=f"h.{e}.{s}.{fb}")
                p.op("scalar", lambda e_, fb=fb, fpar=fpar: e_.activation(
                    hT[:, fb * 512:(fb + 1) * 512], ps_h[fpar][:], AF.Gelu,
                    bias=b1c[:, fb:fb + 1]),
                    reads=[ps_h[fpar], b1c], writes=[hT], tag=f"gelu.{e}.{s}.{fb}")
            for tt in range(4):
                ypar = tt % 2
                tg = s * 4 + tt
                if e > 0:
                    p.dma("sync", mo[ypar][:], moe_hbm[tg * 128:(tg + 1) * 128, :],
                          key=pfx + f"mo{ypar}", tag=f"m.ldmo{e}.{tg}")
                for c in range(2):
                    for fb in range(32):
                        p.mm(lambda e_, c=c, fb=fb, tt=tt, ypar=ypar: e_.matmul(
                            ps_y[ypar][:],
                            hT[:, fb * 512 + tt * 128: fb * 512 + tt * 128 + 128],
                            W2sb[:, fb * D + c * 512: fb * D + (c + 1) * 512],
                            start=(fb == 0), stop=False),
                            reads=[hT, W2sb], writes=[ps_y[ypar]], tag=f"y.{e}.{tg}.{c}")
                    p.mm(lambda e_, c=c, ypar=ypar: e_.matmul(
                        ps_y[ypar][:], onecb[:], b2row[:, c * 512:(c + 1) * 512],
                        start=False, stop=True),
                        reads=[onecb, b2row], writes=[ps_y[ypar]], tag=f"yb.{e}.{tg}.{c}")
                    cc = comb[:, tg * 8 + e: tg * 8 + e + 1]
                    if e == 0:
                        p.op("vector", lambda e_, c=c, ypar=ypar, cc=cc: e_.tensor_scalar(
                            mo[ypar][:, c * 512:(c + 1) * 512], ps_y[ypar][:], cc, None, ALU.mult),
                             reads=[ps_y[ypar], comb], writes=[mo[ypar]], tag=f"ya.{e}.{tg}.{c}")
                    else:
                        p.op("vector", lambda e_, c=c, ypar=ypar, cc=cc: e_.scalar_tensor_tensor(
                            mo[ypar][:, c * 512:(c + 1) * 512], ps_y[ypar][:], cc,
                            mo[ypar][:, c * 512:(c + 1) * 512], ALU.mult, ALU.add),
                             reads=[ps_y[ypar], comb, mo[ypar]], writes=[mo[ypar]], tag=f"ya.{e}.{tg}.{c}")
                p.dma("sync", moe_hbm[tg * 128:(tg + 1) * 128, :], mo[ypar][:],
                      key=pfx + f"most{ypar}", tag=f"m.stmo{e}.{tg}")
            p.flush(pfx + f"moe.{e}.{s}", force=False)

    p.arena_release(_m2)

    g2r = bcast_vec(p, g2_d, pfx + "g2r", D)
    b2r = bcast_vec(p, be2_d, pfx + "be2r", D)
    x1l = [p.abuf(pfx + f"x1l{i}", [128, D], F32) for i in range(2)]
    mol = [p.abuf(pfx + f"mol{i}", [128, D], F32) for i in range(2)]
    xrl = [p.abuf(pfx + f"xrl{i}", [128, D], F32) for i in range(2)]
    zf = [p.abuf(pfx + f"zf{i}", [128, D], F32) for i in range(2)]
    lnx = p.abuf(pfx + "lnxn", [128, D], F32)
    scr2 = dict(ln_scr)
    scr2["xn"] = lnx
    for t in range(T):
        par = t % 2
        p.dma("sync", x1l[par][:], x1_hbm[t * 128:(t + 1) * 128, :], key=pfx + f"x1l{par}",
              tag=f"f.ldx1{t}")
        p.dma("sync", mol[par][:], moe_hbm[t * 128:(t + 1) * 128, :], key=pfx + f"mol{par}",
              tag=f"f.ldmo{t}")
        p.dma("scalar", xrl[par][:], resid_src[t * 128:(t + 1) * 128, :], key=pfx + f"xrl{par}",
              tag=f"f.ldxr{t}")
        p.op("vector", lambda e, par=par: e.tensor_tensor(x1l[par][:], x1l[par][:], mol[par][:], ALU.add),
             reads=[x1l[par], mol[par]], writes=[x1l[par]], tag=f"f.add{t}")
        ln_tile(p, x1l[par][:], zf[par][:], scr2, gamma_rep=g2r, beta_rep=b2r)
        p.op("vector", lambda e, par=par: e.tensor_tensor(zf[par][:], zf[par][:], xrl[par][:], ALU.add),
             reads=[zf[par], xrl[par]], writes=[zf[par]], tag=f"f.res{t}")
        p.dma("sync", out_dram[t * 128:(t + 1) * 128, :], zf[par][:], key=pfx + f"zf{par}",
              tag=f"f.st{t}")
    p.flush(pfx + "moefin", force=False)
    p.arena_release(_amark)


def build_l0_ffn():
    nc = bass.Bass("TRN2", target_bir_lowering=False)
    p = Prog(nc)
    xin = p.din("x", [N, D], F32)
    out = p.dout("y", [N, D], F32)
    _load_shared_consts(p)
    W = {k[0]: p.din(f"L0_ffn_W{k}", [D, D], F32) for k in ("q", "k", "v", "o")}
    bvec = {k[0]: p.din(f"L0_ffn_b{k}", [D], F32) for k in ("q", "k", "v", "o")}
    cosd = p.din("l0f_cos", [N, 64], F32)
    sind = p.din("l0f_sin", [N, 64], F32)
    g1 = p.din("L0_ffn_g1", [D], F32)
    be1 = p.din("L0_ffn_be1", [D], F32)
    xln = p.dscratch("F_xln", [N, D], F32)
    x1 = p.dscratch("F_x1", [N, D], F32)
    x1T = p.dscratch("F_x1T", [T * 8 * 128, 128], BF16)
    emit_full_attn(p, "F_", xin, xln, x1, W, bvec, cosd, sind, rope_style="half",
                   xln_store=xln, tail_ln=dict(g=g1, b=be1, outT_hbm=x1T))
    emit_moe(p, "M_", x1, x1T, xin, out,
             p.din("L0_ffn_Wg", [D, 8], F32), p.din("L0_ffn_bg", [8], F32),
             p.din("L0_ffn_W1", [8, D, 4096], F32), p.din("L0_ffn_b1", [8, 4096], F32),
             p.din("L0_ffn_W2", [8, 4096, D], F32), p.din("L0_ffn_b2", [8, D], F32),
             p.din("L0_ffn_g2", [D], F32), p.din("L0_ffn_be2", [D], F32))
    p.flush("tail", force=False)
    nc.finalize()
    return nc


L1A_OFFS = [0, 1, 2, 3, 4, 6, 8, 12, 16, 24, 32, 48, 64, 96, 128, 192, 256, 384,
            512, 768, 1024, 1536, 2048, 3072]


def emit_l1_attn(p, pfx, x_src, out_dram, Wqkv_d, bqkv_d, Wout_d, bout_d,
                 Wgate_d, bgate_d, posb_d):
    NO = len(L1A_OFFS)
    _amark = p.arena_mark()
    idb = p.sbuf("identity_bf", [128, 128], BF16)
    onecb = p.sbuf("ones_col_bf", [1, 128], BF16)
    scr_bf = p.sbuf("scr_bf", [128, D], BF16)
    ln_scr = {k: p.sbuf("ln_" + k, [128, 1], F32)
              for k in ["ssq", "nsum", "nmean", "m2", "var", "sqv", "rstd", "nmr"]}
    ln_scr["sq"] = scr_bf
    xnT_hbm = p.dscratch(pfx + "xnT", [T * 8 * 128, 128], BF16)
    q_hbm = p.dscratch(pfx + "q", [N, D], BF16)
    gate_hbm = p.dscratch(pfx + "gate", [N, D], BF16)
    kv = p.abuf(pfx + "kv", [128, T * 2048], BF16)
    zkv = p.abuf(pfx + "zkv", [128, 2048], BF16)
    p.op("vector", lambda e: e.memset(zkv[:], 0.0), writes=[zkv], tag="zkv")
    ps_tr = [p.psum_bank(6), p.psum_bank(7)]

    def tr128(src_ap, dst_ap, par, tag):
        ps = ps_tr[par]
        psv = ps[:].bitcast(BF16)
        p.mm(lambda e: e.transpose(psv[:, 0:128], src_ap, idb[:]),
             reads=[_nm(src_ap), idb], writes=[ps], tag=tag + ".t")
        p.op("scalar", lambda e: e.copy(dst_ap, psv[:, 0:128]),
             reads=[ps], writes=[_nm(dst_ap)], tag=tag + ".ev")

    _m1 = p.arena_mark()
    Wkv = p.abuf(pfx + "Wkv", [128, 8 * 2048], BF16)
    p.dma("gpsimd", Wkv[:].rearrange("p (a n) -> p a n", n=2048),
          Wqkv_d[:, D:3 * D].rearrange("(a p) n -> p a n", p=128), key=pfx + "Wkv",
          tag="ldWkv")
    bkvrow = p.abuf(pfx + "bkvrow", [1, 2048], BF16)
    p.dma("gpsimd", bkvrow[:], bqkv_d[D:3 * D].rearrange("(o n) -> o n", o=1),
          key=pfx + "bkvrow", tag="ldbkv")
    xt = [p.abuf(pfx + f"xt{i}", [128, D], F32) for i in range(2)]
    xnb = [p.abuf(pfx + f"xnb{i}", [128, D], BF16) for i in range(2)]
    xnTt = [p.abuf(pfx + f"xnTt{i}", [128, D], BF16) for i in range(2)]
    ps_p = [p.psum_bank(0), p.psum_bank(1), p.psum_bank(2), p.psum_bank(3)]
    p.flush(pfx + "b1ld", force=False)
    for t in range(T):
        par = t % 2
        p.dma("sync", xt[par][:], x_src[t * 128:(t + 1) * 128, :], key=pfx + f"xt{par}",
              tag=f"ld.x{t}")
        ln_tile(p, xt[par][:], xnb[par][:], ln_scr)
        for j in range(8):
            tr128(xnb[par][:, j * 128:(j + 1) * 128], xnTt[par][:, j * 128:(j + 1) * 128],
                  j % 2, f"xnT{t}.{j}")
        p.dma("sync", xnT_hbm[t * 1024:(t + 1) * 1024, :].rearrange("(j q) b -> q j b", j=8),
              xnTt[par][:].rearrange("p (j b) -> p j b", b=128),
              key=pfx + f"xnTst{par}", tag=f"st.xnT{t}")
        for c in range(4):
            for j in range(8):
                p.mm(lambda e, c=c, j=j, par=par: e.matmul(
                    ps_p[c][:], xnTt[par][:, j * 128:(j + 1) * 128],
                    Wkv[:, j * 2048 + c * 512: j * 2048 + (c + 1) * 512],
                    start=(j == 0), stop=False),
                    reads=[xnTt[par], Wkv], writes=[ps_p[c]], tag=f"kv.{t}.{c}")
            p.mm(lambda e, c=c: e.matmul(
                ps_p[c][:], onecb[:], bkvrow[:, c * 512:(c + 1) * 512],
                start=False, stop=True),
                reads=[onecb, bkvrow], writes=[ps_p[c]], tag=f"kvb.{t}.{c}")
            p.op("scalar", lambda e, c=c, t=t: e.copy(
                kv[:, t * 2048 + c * 512: t * 2048 + (c + 1) * 512], ps_p[c][:]),
                 reads=[ps_p[c]], writes=[kv], tag=f"kvev.{t}.{c}")
    p.flush(pfx + "b1", force=False)
    p.arena_release(_m1)

    _m2 = p.arena_mark()
    Wqg = p.abuf(pfx + "Wqg", [128, 8 * 2048], BF16)
    p.dma("gpsimd", Wqg[:].rearrange("p (a n) -> p a n", n=2048)[:, :, 0:1024],
          Wqkv_d[:, 0:D].rearrange("(a p) n -> p a n", p=128), key=pfx + "Wqg", tag="ldWq")
    p.dma("gpsimd", Wqg[:].rearrange("p (a n) -> p a n", n=2048)[:, :, 1024:2048],
          Wgate_d.rearrange("(a p) n -> p a n", p=128), key=pfx + "Wqg", tag="ldWg")
    bqgrow = p.abuf(pfx + "bqgrow", [1, 2048], BF16)
    p.dma("gpsimd", bqgrow[:, 0:1024], bqkv_d[0:D].rearrange("(o n) -> o n", o=1),
          key=pfx + "bqgrow", tag="ldbq")
    p.dma("gpsimd", bqgrow[:, 1024:2048], bgate_d.rearrange("(o n) -> o n", o=1),
          key=pfx + "bqgrow", tag="ldbg")
    xnTl = [p.abuf(pfx + f"xnTl{i}", [128, D], BF16) for i in range(2)]
    qg = p.abuf(pfx + "qg", [128, 2048], BF16)
    p.flush(pfx + "b2ald", force=False)
    for t in range(T):
        par = t % 2
        p.dma("sync", xnTl[par][:].rearrange("p (j b) -> p j b", b=128),
              xnT_hbm[t * 1024:(t + 1) * 1024, :].rearrange("(j q) b -> q j b", j=8),
              key=pfx + f"xnTl{par}", tag=f"ld.xnT{t}")
        for c in range(4):
            for j in range(8):
                p.mm(lambda e, c=c, j=j, par=par: e.matmul(
                    ps_p[c][:], xnTl[par][:, j * 128:(j + 1) * 128],
                    Wqg[:, j * 2048 + c * 512: j * 2048 + (c + 1) * 512],
                    start=(j == 0), stop=False),
                    reads=[xnTl[par], Wqg], writes=[ps_p[c]], tag=f"qg.{t}.{c}")
            p.mm(lambda e, c=c: e.matmul(
                ps_p[c][:], onecb[:], bqgrow[:, c * 512:(c + 1) * 512],
                start=False, stop=True),
                reads=[onecb, bqgrow], writes=[ps_p[c]], tag=f"qgb.{t}.{c}")
            if c < 2:
                p.op("scalar", lambda e, c=c: e.copy(qg[:, c * 512:(c + 1) * 512], ps_p[c][:]),
                     reads=[ps_p[c]], writes=[qg], tag=f"qev.{t}.{c}")
            else:
                p.op("scalar", lambda e, c=c: e.activation(
                    qg[:, c * 512:(c + 1) * 512], ps_p[c][:], AF.Sigmoid),
                     reads=[ps_p[c]], writes=[qg], tag=f"gev.{t}.{c}")
        p.dma("sync", q_hbm[t * 128:(t + 1) * 128, :], qg[:, 0:1024], key=pfx + "qst",
              tag=f"st.q{t}")
        p.dma("sync", gate_hbm[t * 128:(t + 1) * 128, :], qg[:, 1024:2048], key=pfx + "gst",
              tag=f"st.g{t}")
    p.flush(pfx + "b2a", force=False)
    p.arena_release(_m2)

    Wosb = load_weight_bf16(p, Wout_d, pfx + "Wosb", D, D)
    borow = p.abuf(pfx + "borow", [1, D], BF16)
    p.dma("gpsimd", borow[:], bout_d.rearrange("(o n) -> o n", o=1), key=pfx + "borow",
          tag="ldbo")
    ps_w = [p.psum_bank(0), p.psum_bank(1)]
    pbrep = p.abuf(pfx + "pbrep", [128, NO * 16], F32)
    pb24 = p.abuf(pfx + "pb24", [24, 16], F32)
    pbT = p.abuf(pfx + "pbT", [16, NO], F32)
    pbrow = p.abuf(pfx + "pbrow", [1, NO * 16], F32)
    idf = p.sbuf("identity_f32", [128, 128], F32)
    onec = p.sbuf("ones_col", [1, 128], F32)
    p.dma("sync", pb24[:], posb_d, key=pfx + "pb24", tag="ldpb")
    p.mm(lambda e: e.transpose(ps_w[0][0:16, 0:NO], pb24[:], idf[0:24, 0:24]),
         reads=[pb24, idf], writes=[ps_w[0]], tag="pb.t")
    p.op("scalar", lambda e: e.copy(pbT[:], ps_w[0][0:16, 0:NO]),
         reads=[ps_w[0]], writes=[pbT], tag="pb.ev")
    pb_scr = p.dscratch(pfx + "pbscr", [16, NO], F32)
    p.dma("sync", pb_scr, pbT[:], key=pfx + "pbrow", tag="pb.flat0")
    p.dma("sync", pbrow[:], pb_scr.rearrange("h o -> (h o)").rearrange("(z n) -> z n", z=1),
          key=pfx + "pbrow", tag="pb.flat1")
    p.mm(lambda e: e.matmul(ps_w[1][:, 0:NO * 16], onec[:], pbrow[:], start=True, stop=True),
         reads=[onec, pbrow], writes=[ps_w[1]], tag="pb.bc")
    p.op("scalar", lambda e: e.copy(pbrep[:], ps_w[1][:, 0:NO * 16]),
         reads=[ps_w[1]], writes=[pbrep], tag="pb.bce")
    ql = [p.abuf(pfx + f"ql{i}", [128, D], BF16) for i in range(2)]
    gl = [p.abuf(pfx + f"gl{i}", [128, D], BF16) for i in range(2)]
    xrl = [p.abuf(pfx + f"xrl{i}", [128, D], F32) for i in range(2)]
    kvsh = [p.abuf(pfx + f"kvsh{i}", [128, 2048], BF16) for i in range(4)]
    KVQ = ["sync", "scalar", "gpsimd", "sync"]
    sbf = p.abuf(pfx + "sbf", [128, NO * 16], F32)
    esb = p.abuf(pfx + "esb", [128, NO * 16], F32)
    negm = p.abuf(pfx + "negm", [128, 16], F32)
    lsum = p.abuf(pfx + "lsum", [128, 16], F32)
    rsum = p.abuf(pfx + "rsum", [128, 16], F32)
    oacc = p.abuf(pfx + "oacc", [128, D], F32)
    otmp = p.abuf(pfx + "otmp", [128, D], F32)
    ogT = p.abuf(pfx + "ogT", [128, D], BF16)
    s3 = sbf[:].rearrange("p (h o) -> p h o", o=NO)
    e3 = esb[:].rearrange("p (h o) -> p h o", o=NO)

    for t in range(T):
        par = t % 2
        p.dma("sync", ql[par][:], q_hbm[t * 128:(t + 1) * 128, :], key=pfx + f"ql{par}",
              tag=f"ld.q{t}")
        p.dma("sync", gl[par][:], gate_hbm[t * 128:(t + 1) * 128, :], key=pfx + f"gl{par}",
              tag=f"ld.g{t}")
        p.dma("scalar", xrl[par][:], x_src[t * 128:(t + 1) * 128, :], key=pfx + f"xrl{par}",
              tag=f"ld.xr{t}")
        for oi, dlt in enumerate(L1A_OFFS):
            a, r = dlt // 128, dlt % 128
            opar = oi % 4
            if r == 0:
                src_t = t - a
                kv_ap = kv[:, src_t * 2048:(src_t + 1) * 2048] if src_t >= 0 else zkv[:]
                kv_names = [kv] if src_t >= 0 else [zkv]
            else:
                hi_t, lo_t = t - a, t - a - 1
                hi_src = kv[:, hi_t * 2048:(hi_t + 1) * 2048][0:128 - r, :] if hi_t >= 0 else zkv[0:128 - r, :]
                lo_src = kv[:, lo_t * 2048:(lo_t + 1) * 2048][128 - r:128, :] if lo_t >= 0 else zkv[128 - r:128, :]
                q_ = KVQ[opar]
                p.dma(q_, kvsh[opar][r:128, :], hi_src, key=pfx + f"kvsh{opar}",
                      tag=f"sh.{t}.{oi}.h")
                p.dma(q_, kvsh[opar][0:r, :], lo_src, key=pfx + f"kvsh{opar}",
                      tag=f"sh.{t}.{oi}.l")
                kv_ap = kvsh[opar][:]
                kv_names = [kvsh[opar]]
            k3 = kv_ap[:, 0:1024].rearrange("p (h d) -> p h d", d=64)
            v3 = kv_ap[:, 1024:2048].rearrange("p (h d) -> p h d", d=64)
            q3 = ql[par][:].rearrange("p (h d) -> p h d", d=64)
            p.op("vector", lambda e, q3=q3, k3=k3: e.tensor_tensor(
                scr_bf[:].rearrange("p (h d) -> p h d", d=64), q3, k3, ALU.mult),
                 reads=[ql[par]] + kv_names, writes=[scr_bf], tag=f"sc.{t}.{oi}")
            p.op("vector", lambda e, oi=oi: e.tensor_reduce(
                s3[:, :, oi:oi + 1], scr_bf[:].rearrange("p (h d) -> p h d", d=64),
                AX.X, ALU.add),
                 reads=[scr_bf], writes=[sbf], tag=f"sr.{t}.{oi}")
        p.op("vector", lambda e: e.tensor_scalar(sbf[:], sbf[:], 0.125, None, ALU.mult),
             reads=[sbf], writes=[sbf], tag=f"ssc.{t}")
        p.op("vector", lambda e: e.tensor_tensor(sbf[:], sbf[:], pbrep[:], ALU.add),
             reads=[sbf, pbrep], writes=[sbf], tag=f"spb.{t}")
        p.op("vector", lambda e: e.tensor_reduce(negm[:], s3, AX.X, ALU.max, negate=True),
             reads=[sbf], writes=[negm], tag=f"sm.{t}")
        p.op("vector", lambda e: e.tensor_tensor(
            s3, s3, negm[:].unsqueeze(2).broadcast_to((128, 16, NO)), ALU.add),
             reads=[sbf, negm], writes=[sbf], tag=f"ssub.{t}")
        p.op("scalar", lambda e: e.activation(esb[:], sbf[:], AF.Exp),
             reads=[sbf], writes=[esb], tag=f"sexp.{t}")
        p.op("vector", lambda e: e.tensor_reduce(lsum[:], e3, AX.X, ALU.add),
             reads=[esb], writes=[lsum], tag=f"ssum.{t}")
        p.op("vector", lambda e: e.reciprocal(rsum[:], lsum[:]),
             reads=[lsum], writes=[rsum], tag=f"srec.{t}")
        p.op("vector", lambda e: e.tensor_tensor(
            e3, e3, rsum[:].unsqueeze(2).broadcast_to((128, 16, NO)), ALU.mult),
             reads=[esb, rsum], writes=[esb], tag=f"snrm.{t}")
        for oi, dlt in enumerate(L1A_OFFS):
            a, r = dlt // 128, dlt % 128
            opar = oi % 4
            if r == 0:
                src_t = t - a
                kv_ap = kv[:, src_t * 2048:(src_t + 1) * 2048] if src_t >= 0 else zkv[:]
                kv_names = [kv] if src_t >= 0 else [zkv]
            else:
                hi_t, lo_t = t - a, t - a - 1
                hi_src = kv[:, hi_t * 2048:(hi_t + 1) * 2048][0:128 - r, :] if hi_t >= 0 else zkv[0:128 - r, :]
                lo_src = kv[:, lo_t * 2048:(lo_t + 1) * 2048][128 - r:128, :] if lo_t >= 0 else zkv[128 - r:128, :]
                q_ = KVQ[opar]
                p.dma(q_, kvsh[opar][r:128, :], hi_src, key=pfx + f"kvsh{opar}",
                      tag=f"sh2.{t}.{oi}.h")
                p.dma(q_, kvsh[opar][0:r, :], lo_src, key=pfx + f"kvsh{opar}",
                      tag=f"sh2.{t}.{oi}.l")
                kv_ap = kvsh[opar][:]
                kv_names = [kvsh[opar]]
            v3 = kv_ap[:, 1024:2048].rearrange("p (h d) -> p h d", d=64)
            wv = e3[:, :, oi:oi + 1].broadcast_to((128, 16, 64))
            if oi == 0:
                p.op("vector", lambda e, v3=v3, wv=wv: e.tensor_tensor(
                    oacc[:].rearrange("p (h d) -> p h d", d=64), v3, wv, ALU.mult),
                     reads=[esb] + kv_names, writes=[oacc], tag=f"oa.{t}.{oi}")
            else:
                p.op("vector", lambda e, v3=v3, wv=wv: e.tensor_tensor(
                    otmp[:].rearrange("p (h d) -> p h d", d=64), v3, wv, ALU.mult),
                     reads=[esb] + kv_names, writes=[otmp], tag=f"om.{t}.{oi}")
                p.op("vector", lambda e: e.tensor_tensor(oacc[:], oacc[:], otmp[:], ALU.add),
                     reads=[oacc, otmp], writes=[oacc], tag=f"oa.{t}.{oi}")
        p.op("vector", lambda e, par=par: e.tensor_tensor(oacc[:], oacc[:], gl[par][:], ALU.mult),
             reads=[oacc, gl[par]], writes=[oacc], tag=f"og.{t}")
        p.op("vector", lambda e: e.tensor_copy(scr_bf[:], oacc[:]),
             reads=[oacc], writes=[scr_bf], tag=f"ogb.{t}")
        for j in range(8):
            tr128(scr_bf[:, j * 128:(j + 1) * 128], ogT[:, j * 128:(j + 1) * 128],
                  j % 2, f"ogT{t}.{j}")
        wpar = t % 2
        for c in range(2):
            for j in range(8):
                p.mm(lambda e, c=c, j=j, wpar=wpar: e.matmul(
                    ps_w[wpar][:], ogT[:, j * 128:(j + 1) * 128],
                    Wosb[:, j * D + c * 512: j * D + (c + 1) * 512],
                    start=(j == 0), stop=False),
                    reads=[ogT, Wosb], writes=[ps_w[wpar]], tag=f"wo.{t}.{c}")
            p.mm(lambda e, c=c, wpar=wpar: e.matmul(
                ps_w[wpar][:], onecb[:], borow[:, c * 512:(c + 1) * 512],
                start=False, stop=True),
                reads=[onecb, borow], writes=[ps_w[wpar]], tag=f"wob.{t}.{c}")
            p.op("vector", lambda e, c=c, par=par, wpar=wpar: e.tensor_tensor(
                xrl[par][:, c * 512:(c + 1) * 512], ps_w[wpar][:],
                xrl[par][:, c * 512:(c + 1) * 512], ALU.add),
                 reads=[ps_w[wpar], xrl[par]], writes=[xrl[par]], tag=f"woe.{t}.{c}")
        p.dma("sync", out_dram[t * 128:(t + 1) * 128, :], xrl[par][:], key=pfx + f"ost{par}",
              tag=f"st.{t}")
        if t % 4 == 3:
            p.flush(pfx + f"b2b.{t}", force=False)
    p.flush(pfx + "b2b", force=False)
    p.arena_release(_amark)


def build_l1_attn():
    nc = bass.Bass("TRN2", target_bir_lowering=False)
    p = Prog(nc)
    xin = p.din("x", [N, D], F32)
    out = p.dout("y", [N, D], F32)
    _load_shared_consts(p)
    emit_l1_attn(p, "G_", xin, out,
                 p.din("L1_attn_Wqkv", [D, 3 * D], F32), p.din("L1_attn_bqkv", [3 * D], F32),
                 p.din("L1_attn_Wout", [D, D], F32), p.din("L1_attn_bout", [D], F32),
                 p.din("L1_attn_Wgate", [D, D], F32), p.din("L1_attn_bgate", [D], F32),
                 p.din("L1_attn_pos_bias", [24, 16], F32))
    p.flush("tail", force=False)
    nc.finalize()
    return nc


def emit_l1_ffn(p, pfx, x_src, out_dram, Wqkv_d, qs_d, ks_d, Wo_d, lng_d, lnb_d,
                W1_d, b1a_d, b1g_d, W2_d, b2_d, depth=6, do_attn=True, do_ffn=True):
    DEPTH, W, NW = depth, 256, 16
    FF2, FFH = 5460, 2730
    AT, GT = 22, 22
    _amark = p.arena_mark()
    idb = p.sbuf("identity_bf", [128, 128], BF16)
    idf = p.sbuf("identity_f32", [128, 128], F32)
    onecb = p.sbuf("ones_col_bf", [1, 128], BF16)
    scr_bf = p.sbuf("scr_bf", [128, D], BF16)
    ln_scr = {k: p.sbuf("ln_" + k, [128, 1], F32)
              for k in ["ssq", "nsum", "nmean", "m2", "var", "sqv", "rstd", "nmr"]}
    ln_scr["sq"] = scr_bf
    hbufA = p.dscratch(pfx + "hbufA", [N, D], F32)
    hbufB = p.dscratch(pfx + "hbufB", [N, D], F32)
    negscr = p.dscratch(pfx + "negscr", [16, 128], BF16)
    mle = p.sbuf("mle_sb", [128, 128], BF16)
    mge = p.sbuf("mge_sb", [128, 128], BF16)
    p.dma("sync", mle[:], p.din("l1f_mle", [128, 128], BF16), key="mle")
    p.dma("sync", mge[:], p.din("l1f_mge", [128, 128], BF16), key="mge")

    _mc0 = p.arena_mark()
    xt = [p.abuf(pfx + f"c0x{i}", [128, D], F32) for i in range(2)]
    zt = [p.abuf(pfx + f"c0z{i}", [128, D], F32) for i in range(2)]
    for t in range(T):
        par = t % 2
        p.dma("sync", xt[par][:], x_src[t * 128:(t + 1) * 128, :], key=pfx + f"c0x{par}",
              tag=f"c0.ld{t}")
        ln_tile(p, xt[par][:], zt[par][:], ln_scr)
        p.dma("sync", hbufA[t * 128:(t + 1) * 128, :], zt[par][:], key=pfx + f"c0z{par}",
              tag=f"c0.st{t}")
    p.flush(pfx + "c0", force=False)
    p.arena_release(_mc0)

    hcur = [0]

    def hrd():
        return hbufA if hcur[0] == 0 else hbufB

    def hwr():
        return hbufB if hcur[0] == 0 else hbufA

    for L in range(DEPTH):
      if do_attn:
        _ml = p.arena_mark()
        Wqkv = p.abuf(pfx + "Wqkv", [128, 8 * 3072], BF16)
        p.dma("gpsimd", Wqkv[:].rearrange("p (a n) -> p a n", n=3072),
              Wqkv_d[L].rearrange("(a p) n -> p a n", p=128), key=pfx + "Wqkv",
              tag=f"a{L}.ldW")
        Wo = p.abuf(pfx + "Wo", [128, 8 * D], BF16)
        p.dma("gpsimd", Wo[:].rearrange("p (a n) -> p a n", n=D),
              Wo_d[L].rearrange("(a p) n -> p a n", p=128), key=pfx + "Wo",
              tag=f"a{L}.ldWo")
        qs_rep = p.abuf(pfx + "qs_rep", [128, 64], F32)
        ks_rep = p.abuf(pfx + "ks_rep", [128, 64], F32)
        p.dma("sync", qs_rep[:], qs_d[L].rearrange("(o n) -> o n", o=1).broadcast_to((128, 64)),
              key=pfx + "qs_rep", tag=f"a{L}.ldqs")
        p.dma("sync", ks_rep[:], ks_d[L].rearrange("(o n) -> o n", o=1).broadcast_to((128, 64)),
              key=pfx + "ks_rep", tag=f"a{L}.ldks")
        scr2 = p.abuf(pfx + "scr2", [128, 2048], F32)
        krhs = p.abuf(pfx + "krhs", [128, 16 * 768], BF16)
        qrhs2 = [p.abuf(pfx + f"qrhs{i}", [128, 16 * 256], BF16) for i in range(2)]
        vw = p.abuf(pfx + "vw", [128, 6 * 1040], BF16)
        p.op("vector", lambda e: e.memset(krhs[64:65, :], 1.0),
             writes=[krhs] + [f"krhs_s{i}" for i in range(6)], tag=f"a{L}.ones")
        p.op("vector", lambda e: e.memset(vw[:], 1.0),
             writes=[vw] + [f"vw_s{i}" for i in range(6)], tag=f"a{L}.vones")
        xsb = [p.abuf(pfx + f"xsb{i}", [128, D], BF16) for i in range(2)]
        xsT = [p.abuf(pfx + f"xsT{i}", [128, D], BF16) for i in range(2)]
        qkb = p.abuf(pfx + "qkb", [128, 2048], F32)
        qkbb2 = [p.abuf(pfx + f"qkbb{i}", [128, 2048], BF16) for i in range(2)]
        rotb = p.abuf(pfx + "rotb", [128, 2048], F32)
        t1b = p.abuf(pfx + "t1b", [128, 2048], F32)
        nrm = p.abuf(pfx + "nrm", [128, 32], F32)
        rnb = p.abuf(pfx + "rnb", [128, 32], F32)
        tb = {nm: [p.abuf(pfx + f"{nm}{i}", [128, 64], F32) for i in range(2)]
              for nm in ("tcs", "tss", "tci", "tsi")}
        tbl_d = {nm: p.din("l1f_" + nm, [N, 64], F32) for nm in ("tcs", "tss", "tci", "tsi")}
        negm0 = p.abuf(pfx + "negm0", [128, 16], F32)
        negm1 = p.abuf(pfx + "negm1", [128, 16], F32)
        negb = p.abuf(pfx + "negb", [128, 16], BF16)
        negT = p.abuf(pfx + "negT", [16, 128], BF16)
        pT2 = [p.abuf(pfx + f"pT{i}", [128, 512], BF16) for i in range(2)]
        ctx = [p.abuf(pfx + f"ctx{i}", [128, D], BF16) for i in range(2)]
        ctxT = p.abuf(pfx + "ctxT", [128, D], BF16)
        hl = [p.abuf(pfx + f"hl{i}", [128, D], F32) for i in range(2)]
        ps_tr = [p.psum_bank(6), p.psum_bank(7)]
        ps_qkv = [p.psum_bank(0), p.psum_bank(1)]
        ps_X = [p.psum_bank(2), p.psum_bank(3)]
        ps_ST = [p.psum_bank(4), p.psum_bank(5)]
        ps_PV = [p.psum_bank(2), p.psum_bank(3)]
        ps_W = [p.psum_bank(0), p.psum_bank(1)]

        def tr(src_ap, dst_ap, par, tag, dst_name=None):
            ps = ps_tr[par]
            psv = ps[:].bitcast(BF16)
            w_ = src_ap.shape[1]
            p.mm(lambda e: e.transpose(psv[0:w_, 0:128], src_ap, idb[:]),
                 reads=[_nm(src_ap), idb], writes=[ps], tag=tag + ".t")
            p.op("scalar", lambda e: e.copy(dst_ap, psv[0:w_, 0:128]),
                 reads=[ps], writes=[dst_name or _nm(dst_ap)], tag=tag + ".ev")

        def projw_a(w):
            qrhs = qrhs2[w % 2]
            for tt in range(2):
                t = 2 * w + tt
                par = t % 2
                slot = t % 6
                p.dma("gpsimd", xsb[par][:, 0:512], hrd()[t * 128:(t + 1) * 128, 0:512],
                      key=pfx + f"xsb{par}", tag=f"a{L}.sh0.{t}")
                if t == 0:
                    p.op("vector", lambda e, par=par: e.memset(xsb[par][0:1, 512:1024], 0.0),
                         writes=[xsb[par]], tag=f"a{L}.shz")
                else:
                    p.dma("gpsimd", xsb[par][0:1, 512:1024],
                          hrd()[t * 128 - 1: t * 128, 512:1024],
                          key=pfx + f"xsb{par}", tag=f"a{L}.shb.{t}")
                p.dma("gpsimd", xsb[par][1:128, 512:1024],
                      hrd()[t * 128: t * 128 + 127, 512:1024],
                      key=pfx + f"xsb{par}", tag=f"a{L}.sh1.{t}")
                for nm in ("tcs", "tss", "tci", "tsi"):
                    p.dma("sync", tb[nm][par][:], tbl_d[nm][t * 128:(t + 1) * 128, :],
                          key=pfx + f"{nm}{par}", tag=f"a{L}.tb.{t}")
                for j in range(8):
                    tr(xsb[par][:, j * 128:(j + 1) * 128], xsT[par][:, j * 128:(j + 1) * 128],
                       j % 2, f"a{L}.xT{t}.{j}")
                for c in range(6):
                    cpar = c % 2
                    for j in range(8):
                        p.mm(lambda e, c=c, j=j, par=par, cpar=cpar: e.matmul(
                            ps_qkv[cpar][:], xsT[par][:, j * 128:(j + 1) * 128],
                            Wqkv[:, j * 3072 + c * 512: j * 3072 + (c + 1) * 512],
                            start=(j == 0), stop=(j == 7)),
                            reads=[xsT[par], Wqkv], writes=[ps_qkv[cpar]], tag=f"a{L}.qkv{t}.{c}")
                    if c < 4:
                        p.op("scalar", lambda e, c=c, cpar=cpar: e.copy(
                            qkb[:, c * 512:(c + 1) * 512], ps_qkv[cpar][:]),
                             reads=[ps_qkv[cpar]], writes=[qkb], tag=f"a{L}.qkev{t}.{c}")
                    else:
                        vdst = vw[:, slot * 1040:(slot + 1) * 1040].rearrange(
                            "p (h c2) -> p h c2", c2=65)
                        p.op("scalar", lambda e, c=c, cpar=cpar, vdst=vdst: e.copy(
                            vdst[:, (c - 4) * 8:(c - 3) * 8, 0:64],
                            ps_qkv[cpar][:].rearrange("p (h d) -> p h d", d=64)),
                             reads=[ps_qkv[cpar]], writes=[f"vw_s{slot}"], tag=f"a{L}.vev{t}.{c}")
                qk3 = qkb[:].rearrange("p (s d) -> p s d", d=64)
                r3 = rotb[:].rearrange("p (s d) -> p s d", d=64)
                p.op("vector", lambda e: e.tensor_tensor(scr2[:], qkb[:], qkb[:], ALU.mult),
                     reads=[qkb], writes=[scr2], tag=f"a{L}.sq{t}")
                p.op("vector", lambda e: e.tensor_reduce(
                    nrm[:], scr2[:].rearrange("p (s d) -> p s d", d=64), AX.X, ALU.add),
                     reads=[scr2], writes=[nrm], tag=f"a{L}.ss{t}")
                p.op("scalar", lambda e: e.activation(nrm[:], nrm[:], AF.Sqrt),
                     reads=[nrm], writes=[nrm], tag=f"a{L}.nsq{t}")
                p.op("vector", lambda e: e.tensor_scalar(nrm[:], nrm[:], 1e-12, None, ALU.max),
                     reads=[nrm], writes=[nrm], tag=f"a{L}.ncl{t}")
                p.op("vector", lambda e: e.reciprocal(rnb[:], nrm[:]),
                     reads=[nrm], writes=[rnb], tag=f"a{L}.nrc{t}")
                p.op("vector", lambda e: e.tensor_tensor(
                    qk3, qk3, rnb[:].unsqueeze(2).broadcast_to((128, 32, 64)), ALU.mult),
                     reads=[qkb, rnb], writes=[qkb], tag=f"a{L}.nn{t}")
                p.op("vector", lambda e: e.tensor_tensor(
                    qkb[:, 0:1024].rearrange("p (h d) -> p h d", d=64),
                    qkb[:, 0:1024].rearrange("p (h d) -> p h d", d=64),
                    qs_rep[:].unsqueeze(1).broadcast_to((128, 16, 64)), ALU.mult),
                     reads=[qkb, qs_rep], writes=[qkb], tag=f"a{L}.qs{t}")
                p.op("vector", lambda e: e.tensor_tensor(
                    qkb[:, 1024:2048].rearrange("p (h d) -> p h d", d=64),
                    qkb[:, 1024:2048].rearrange("p (h d) -> p h d", d=64),
                    ks_rep[:].unsqueeze(1).broadcast_to((128, 16, 64)), ALU.mult),
                     reads=[qkb, ks_rep], writes=[qkb], tag=f"a{L}.ks{t}")
                p.op("vector", lambda e: e.tensor_scalar(
                    r3[:, :, 0:32], qk3[:, :, 32:64], -1.0, None, ALU.mult),
                     reads=[qkb], writes=[rotb], tag=f"a{L}.r0{t}")
                p.op("vector", lambda e: e.tensor_copy(r3[:, :, 32:64], qk3[:, :, 0:32]),
                     reads=[qkb], writes=[rotb], tag=f"a{L}.r1{t}")
                for half, (ct, st_) in (( (0, 1024), ("tcs", "tss")), ((1024, 2048), ("tci", "tsi"))):
                    h0, h1 = half
                    p.op("vector", lambda e, h0=h0, h1=h1, ct=ct, par=par: e.tensor_tensor(
                        t1b[:, h0:h1].rearrange("p (h d) -> p h d", d=64),
                        qkb[:, h0:h1].rearrange("p (h d) -> p h d", d=64),
                        tb[ct][par][:].unsqueeze(1).broadcast_to((128, 16, 64)), ALU.mult),
                         reads=[qkb, tb[ct][par]], writes=[t1b], tag=f"a{L}.c{t}.{h0}")
                    p.op("vector", lambda e, h0=h0, h1=h1, st_=st_, par=par: e.tensor_tensor(
                        rotb[:, h0:h1].rearrange("p (h d) -> p h d", d=64),
                        rotb[:, h0:h1].rearrange("p (h d) -> p h d", d=64),
                        tb[st_][par][:].unsqueeze(1).broadcast_to((128, 16, 64)), ALU.mult),
                         reads=[rotb, tb[st_][par]], writes=[rotb], tag=f"a{L}.s{t}.{h0}")
                p.op("vector", lambda e: e.tensor_tensor(qkb[:], t1b[:], rotb[:], ALU.add),
                     reads=[t1b, rotb], writes=[qkb], tag=f"a{L}.ra{t}")
                p.op("vector", lambda e, par=par: e.tensor_copy(qkbb2[par][:], qkb[:]),
                     reads=[qkb], writes=[qkbb2[par]], tag=f"a{L}.rb{t}")
        def projw_b(w):
            qrhs = qrhs2[w % 2]
            for tt in range(2):
                t = 2 * w + tt
                par = t % 2
                slot = t % 6
                qkbb_t = qkbb2[par]
                for h in range(16):
                    tr(qkbb_t[:, h * 64:(h + 1) * 64], qrhs[0:64, h * 256 + tt * 128: h * 256 + (tt + 1) * 128],
                       h % 2, f"a{L}.qT{t}.{h}")
                    tr(qkbb_t[:, 1024 + h * 64: 1024 + (h + 1) * 64],
                       krhs[0:64, h * 768 + slot * 128: h * 768 + (slot + 1) * 128],
                       h % 2, f"a{L}.kT{t}.{h}", dst_name=f"krhs_s{slot}")

        def attnw_x(w):
            qrhs = qrhs2[w % 2]
            def kslices(tq):
                ks = [(tq - 2, mge), (tq - 1, None), (tq, mle)]
                return [(kt, mk) for kt, mk in ks if kt >= 0]
            ksl_q = [kslices(2 * w), kslices(2 * w + 1)]
            for qsub in range(2):
                ksl = ksl_q[qsub]
                negm = (negm0, negm1)[qsub]
                for h in range(16):
                    psx = ps_X[h % 2]
                    for ki, (kt, mk) in enumerate(ksl):
                        p.mm(lambda e, h=h, qsub=qsub, kt=kt, ki=ki, mk=mk, psx=psx, qrhs=qrhs: e.matmul(
                            psx[:, ki * 128:(ki + 1) * 128],
                            qrhs[0:64, h * 256 + qsub * 128: h * 256 + (qsub + 1) * 128],
                            krhs[0:64, h * 768 + (kt % 6) * 128: h * 768 + (kt % 6 + 1) * 128],
                            start=True, stop=(mk is None)),
                            reads=[qrhs, f"krhs_s{kt % 6}"], writes=[psx], tag=f"a{L}.X.{w}.{h}")
                        if mk is not None:
                            p.mm(lambda e, mk=mk, ki=ki, psx=psx: e.matmul(
                                psx[:, ki * 128:(ki + 1) * 128],
                                idb[:], mk[:], start=False, stop=True),
                                reads=[idb, mk], writes=[psx], tag=f"a{L}.Xm.{w}.{h}")
                    p.op("vector", lambda e, h=h, nks=len(ksl), negm=negm, psx=psx: e.tensor_reduce(
                        negm[:, h:h + 1], psx[:, 0: nks * 128],
                        AX.X, ALU.max, negate=True),
                         reads=[psx], writes=[negm], tag=f"a{L}.Xr.{w}.{h}")
                p.op("vector", lambda e, negm=negm: e.tensor_copy(negb[:], negm[:]),
                     reads=[negm], writes=[negb], tag=f"a{L}.nb.{w}.{qsub}")
                psv = ps_tr[0][:].bitcast(BF16)
                p.mm(lambda e, psv=psv: e.transpose(psv[0:16, 0:128], negb[:], idb[:]),
                     reads=[negb, idb], writes=[ps_tr[0]], tag=f"a{L}.nt.{w}.{qsub}")
                p.op("scalar", lambda e, psv=psv: e.copy(negT[:], psv[0:16, 0:128]),
                     reads=[ps_tr[0]], writes=[negT], tag=f"a{L}.ne.{w}.{qsub}")
                p.dma("sync", negscr, negT[:], key=pfx + "negscr", tag=f"a{L}.nd1.{w}.{qsub}")
                p.dma("sync",
                      qrhs[64:65, :].rearrange("z (h b) -> z h b", b=256)[:, :, qsub * 128:(qsub + 1) * 128],
                      negscr.unsqueeze(0),
                      key=pfx + "negscr", tag=f"a{L}.nd2.{w}.{qsub}")

        def attnw_z(w):
            qrhs = qrhs2[w % 2]
            def kslices(tq):
                ks = [(tq - 2, mge), (tq - 1, None), (tq, mle)]
                return [(kt, mk) for kt, mk in ks if kt >= 0]
            ksl_q = [kslices(2 * w), kslices(2 * w + 1)]

            zitems = []
            for h in range(16):
                for qsub in range(2):
                    zitems.append((h, qsub, ksl_q[qsub]))

            def z_st(i):
                h, qsub, ksl = zitems[i]
                pst = ps_ST[i % 2]
                for ki, (kt, mk) in enumerate(ksl):
                    mkT = mle if mk is mge else (mge if mk is mle else None)
                    p.mm(lambda e, h=h, qsub=qsub, kt=kt, ki=ki, mkT=mkT, pst=pst, qrhs=qrhs: e.matmul(
                        pst[:, ki * 128:(ki + 1) * 128],
                        krhs[0:65, h * 768 + (kt % 6) * 128: h * 768 + (kt % 6 + 1) * 128],
                        qrhs[0:65, h * 256 + qsub * 128: h * 256 + (qsub + 1) * 128],
                        start=True, stop=(mkT is None)),
                        reads=[f"krhs_s{kt % 6}", qrhs], writes=[pst], tag=f"a{L}.Z.{w}.{h}")
                    if mkT is not None:
                        p.mm(lambda e, mkT=mkT, ki=ki, pst=pst: e.matmul(
                            pst[:, ki * 128:(ki + 1) * 128],
                            idb[:], mkT[:], start=False, stop=True),
                            reads=[idb, mkT], writes=[pst], tag=f"a{L}.Zm.{w}.{h}")

            def z_exp(i):
                h, qsub, ksl = zitems[i]
                pst = ps_ST[i % 2]
                nks = len(ksl)
                p.op("scalar", lambda e, nks=nks, pst=pst, i=i: e.activation(
                    pT2[i % 2][:, 0: nks * 128], pst[:, 0: nks * 128], AF.Exp, scale=8.0),
                    reads=[pst], writes=[pT2[i % 2]], tag=f"a{L}.Ze.{w}.{h}")

            def z_pv(i):
                h, qsub, ksl = zitems[i]
                ppv = ps_PV[i % 2]
                nks = len(ksl)
                for ki, (kt, mk) in enumerate(ksl):
                    p.mm(lambda e, ki=ki, kt=kt, h=h, nks=nks, ppv=ppv, i=i: e.matmul(
                        ppv[:, 0:65],
                        pT2[i % 2][:, ki * 128:(ki + 1) * 128],
                        vw[:, (kt % 6) * 1040 + h * 65: (kt % 6) * 1040 + (h + 1) * 65],
                        start=(ki == 0), stop=(ki == nks - 1)),
                        reads=[pT2[i % 2], f"vw_s{kt % 6}"], writes=[ppv], tag=f"a{L}.PV.{w}.{h}")
                p.op("vector", lambda e, qsub=qsub, ppv=ppv: e.reciprocal(
                    rnb[:, qsub:qsub + 1], ppv[:, 64:65]),
                     reads=[ppv], writes=[rnb], tag=f"a{L}.Zr.{w}.{h}")
                p.op("vector", lambda e, qsub=qsub, h=h, ppv=ppv: e.tensor_scalar(
                    ctx[qsub][:, h * 64:(h + 1) * 64], ppv[:, 0:64],
                    rnb[:, qsub:qsub + 1], None, ALU.mult),
                     reads=[ppv, rnb], writes=[ctx[qsub]], tag=f"a{L}.Zc.{w}.{h}")

            z_st(0); z_exp(0); z_st(1); z_exp(1)
            for i in range(32):
                z_pv(i)
                if i + 2 < 32:
                    z_st(i + 2)
                    z_exp(i + 2)
            for tt in range(2):
                t = 2 * w + tt
                par = t % 2
                p.dma("scalar", hl[par][:], hrd()[t * 128:(t + 1) * 128, :], key=pfx + f"hl{par}",
                      tag=f"a{L}.hres{t}")
                for j in range(8):
                    tr(ctx[tt][:, j * 128:(j + 1) * 128], ctxT[:, j * 128:(j + 1) * 128],
                       j % 2, f"a{L}.cT{t}.{j}")
                for c in range(2):
                    psw = ps_W[(2 * tt + c) % 2]
                    for j in range(8):
                        p.mm(lambda e, c=c, j=j, psw=psw: e.matmul(
                            psw[:], ctxT[:, j * 128:(j + 1) * 128],
                            Wo[:, j * D + c * 512: j * D + (c + 1) * 512],
                            start=(j == 0), stop=(j == 7)),
                            reads=[ctxT, Wo], writes=[psw], tag=f"a{L}.wo{t}.{c}")
                    p.op("vector", lambda e, c=c, par=par, psw=psw: e.tensor_tensor(
                        hl[par][:, c * 512:(c + 1) * 512], psw[:],
                        hl[par][:, c * 512:(c + 1) * 512], ALU.add),
                         reads=[psw, hl[par]], writes=[hl[par]], tag=f"a{L}.we{t}.{c}")
                p.dma("sync", hwr()[t * 128:(t + 1) * 128, :], hl[par][:], key=pfx + f"hst{par}",
                      tag=f"a{L}.hst{t}")

        projw_a(0)
        projw_b(0)
        for w in range(NW):
            if w + 1 < NW:
                projw_a(w + 1)
            attnw_x(w)
            if w + 1 < NW:
                projw_b(w + 1)
            attnw_z(w)
            p.flush(pfx + f"a{L}.w{w}", force=False)

        hcur[0] ^= 1
        p.arena_release(_ml)

      if do_ffn:
        _ml = p.arena_mark()
        W1 = p.abuf(pfx + "W1", [128, 8 * FF2], BF16)
        p.dma("gpsimd", W1[:].rearrange("p (a n) -> p a n", n=FF2),
              W1_d[L].rearrange("(a p) n -> p a n", p=128), key=pfx + "W1", tag=f"f{L}.ldW1")
        W2 = p.abuf(pfx + "W2", [128, 22 * D], BF16)
        p.dma("gpsimd", W2[:].rearrange("p (a n) -> p a n", n=D)[:, 0:21, :],
              W2_d[L][0:2688].rearrange("(a p) n -> p a n", p=128), key=pfx + "W2",
              tag=f"f{L}.ldW2a")
        p.dma("gpsimd", W2[0:42, :].rearrange("p (a n) -> p a n", n=D)[:, 21:22, :],
              W2_d[L][2688:2730].rearrange("(a o) n -> a o n", o=1), key=pfx + "W2",
              tag=f"f{L}.ldW2b")
        b2row = p.abuf(pfx + "b2row", [1, D], BF16)
        p.dma("gpsimd", b2row[:], b2_d[L].rearrange("(o n) -> o n", o=1), key=pfx + "b2row",
              tag=f"f{L}.ldb2")
        b1ar = p.abuf(pfx + "b1ar", [22, 128], F32)
        b1gr = p.abuf(pfx + "b1gr", [22, 128], F32)
        b1ac = p.abuf(pfx + "b1ac", [128, 22], F32)
        b1gc = p.abuf(pfx + "b1gc", [128, 22], F32)
        p.dma("sync", b1ar[:], b1a_d[L].rearrange("(a pp) -> a pp", pp=128), key=pfx + "b1ar",
              tag=f"f{L}.ldb1a")
        p.dma("sync", b1gr[:], b1g_d[L].rearrange("(a pp) -> a pp", pp=128), key=pfx + "b1gr",
              tag=f"f{L}.ldb1g")
        gr = bcast_vec(p, lng_d[L], pfx + "lngr", D)
        br = bcast_vec(p, lnb_d[L], pfx + "lnbr", D)
        ps_tr = [p.psum_bank(6), p.psum_bank(7)]
        for (bsrc, bdst) in ((b1ar, b1ac), (b1gr, b1gc)):
            psv = ps_tr[0]
            p.mm(lambda e, bsrc=bsrc, psv=psv: e.transpose(psv[0:128, 0:22], bsrc[:], idf[0:22, 0:22]),
                 reads=[bsrc, idf], writes=[psv], tag=f"f{L}.b1t")
            p.op("scalar", lambda e, bdst=bdst, psv=psv: e.copy(bdst[:], psv[0:128, 0:22]),
                 reads=[psv], writes=[bdst], tag=f"f{L}.b1e")
        xsf = [p.abuf(pfx + "xsf", [128, D], F32)] * 2
        hl = [p.abuf(pfx + f"fhl{i}", [128, D], F32) for i in range(4)]
        xnb2 = p.abuf(pfx + "xnb2", [128, D], BF16)
        ln_scr_f = dict(ln_scr)
        ln_scr_f["xn"] = p.abuf(pfx + "flnxn", [128, D], F32)
        xnTs = p.abuf(pfx + "xnTs", [128, 8 * 512], BF16)
        ag = p.abuf(pfx + "ag", [128, 22 * 512], BF16)
        asb = p.abuf(pfx + "asb", [128, 512], F32)
        gsb = p.abuf(pfx + "gsb", [128, 512], BF16)
        ps_a = [p.psum_bank(0), p.psum_bank(1)]
        ps_g2 = [p.psum_bank(2), p.psum_bank(3)]
        ps_y = [p.psum_bank(4), p.psum_bank(5)]

        for s in range(8):
            for tt in range(4):
                t = s * 4 + tt
                par = t % 2
                p.dma("sync", hl[tt][:], hrd()[t * 128:(t + 1) * 128, :], key=pfx + f"fhl{tt}",
                      tag=f"f{L}.ldh{t}")
                p.op("vector", lambda e, par=par, tt=tt: e.tensor_copy(xsf[par][:, 0:512], hl[tt][:, 0:512]),
                     reads=[hl[tt]], writes=[xsf[par]], tag=f"f{L}.sh0.{t}")
                if t == 0:
                    p.op("vector", lambda e, par=par: e.memset(xsf[par][0:1, 512:1024], 0.0),
                         writes=[xsf[par]], tag=f"f{L}.shz")
                else:
                    p.dma("gpsimd", xsf[par][0:1, 512:1024],
                          hrd()[t * 128 - 1: t * 128, 512:1024], key=pfx + f"xsf{par}",
                          tag=f"f{L}.shb.{t}")
                p.dma("gpsimd", xsf[par][1:128, 512:1024], hrd()[t * 128: t * 128 + 127, 512:1024],
                      key=pfx + f"xsf{par}", tag=f"f{L}.sh1.{t}")
                ln_tile(p, xsf[par][:], xnb2[:], ln_scr_f, gamma_rep=gr, beta_rep=br)
                for j in range(8):
                    tr128_ffn = tr if False else None
                    ps = ps_tr[j % 2]
                    psv = ps[:].bitcast(BF16)
                    p.mm(lambda e, j=j, psv=psv: e.transpose(
                        psv[0:128, 0:128], xnb2[:, j * 128:(j + 1) * 128], idb[:]),
                        reads=[xnb2, idb], writes=[_nm(ps)], tag=f"f{L}.xT{t}.{j}.t")
                    p.op("scalar", lambda e, j=j, tt=tt, psv=psv: e.copy(
                        xnTs[:, j * 512 + tt * 128: j * 512 + (tt + 1) * 128], psv[0:128, 0:128]),
                        reads=[_nm(ps)], writes=[xnTs], tag=f"f{L}.xT{t}.{j}.e")
            for m in range(AT):
                wdt = 128 if m < 21 else 42
                mpar = m % 2
                for j in range(8):
                    p.mm(lambda e, m=m, j=j, wdt=wdt, mpar=mpar: e.matmul(
                        ps_a[mpar][0:wdt, :],
                        W1[:, j * FF2 + m * 128: j * FF2 + m * 128 + wdt],
                        xnTs[:, j * 512:(j + 1) * 512],
                        start=(j == 0), stop=(j == 7)),
                        reads=[W1, xnTs], writes=[ps_a[mpar]], tag=f"f{L}.A.{s}.{m}")
                    p.mm(lambda e, m=m, j=j, wdt=wdt, mpar=mpar: e.matmul(
                        ps_g2[mpar][0:wdt, :],
                        W1[:, j * FF2 + FFH + m * 128: j * FF2 + FFH + m * 128 + wdt],
                        xnTs[:, j * 512:(j + 1) * 512],
                        start=(j == 0), stop=(j == 7)),
                        reads=[W1, xnTs], writes=[ps_g2[mpar]], tag=f"f{L}.G.{s}.{m}")
                p.op("vector", lambda e, m=m, wdt=wdt, mpar=mpar: e.tensor_scalar(
                    asb[0:wdt, :], ps_a[mpar][0:wdt, :], b1ac[0:wdt, m:m + 1], None, ALU.add),
                     reads=[ps_a[mpar], b1ac], writes=[asb], tag=f"f{L}.ab.{s}.{m}")
                p.op("scalar", lambda e, m=m, wdt=wdt, mpar=mpar: e.activation(
                    gsb[0:wdt, :], ps_g2[mpar][0:wdt, :], AF.Gelu, bias=b1gc[0:wdt, m:m + 1]),
                     reads=[ps_g2[mpar], b1gc], writes=[gsb], tag=f"f{L}.gg.{s}.{m}")
                p.op("vector", lambda e, m=m, wdt=wdt: e.tensor_tensor(
                    ag[0:wdt, m * 512:(m + 1) * 512], asb[0:wdt, :], gsb[0:wdt, :], ALU.mult),
                     reads=[asb, gsb], writes=[ag], tag=f"f{L}.ag.{s}.{m}")
            for tt in range(4):
                t = s * 4 + tt
                par = t % 2
                ypar = tt % 2
                for c in range(2):
                    for m in range(AT):
                        wdt = 128 if m < 21 else 42
                        p.mm(lambda e, c=c, m=m, tt=tt, wdt=wdt, ypar=ypar: e.matmul(
                            ps_y[ypar][:],
                            ag[0:wdt, m * 512 + tt * 128: m * 512 + (tt + 1) * 128],
                            W2[0:wdt, m * D + c * 512: m * D + (c + 1) * 512],
                            start=(m == 0), stop=False),
                            reads=[ag, W2], writes=[ps_y[ypar]], tag=f"f{L}.y.{t}.{c}")
                    p.mm(lambda e, c=c, ypar=ypar: e.matmul(
                        ps_y[ypar][:], onecb[:], b2row[:, c * 512:(c + 1) * 512],
                        start=False, stop=True),
                        reads=[onecb, b2row], writes=[ps_y[ypar]], tag=f"f{L}.yb.{t}.{c}")
                    p.op("vector", lambda e, c=c, tt=tt, ypar=ypar: e.tensor_tensor(
                        hl[tt][:, c * 512:(c + 1) * 512], ps_y[ypar][:],
                        hl[tt][:, c * 512:(c + 1) * 512], ALU.add),
                         reads=[ps_y[ypar], hl[tt]], writes=[hl[tt]], tag=f"f{L}.ye.{t}.{c}")
                p.dma("sync", hwr()[t * 128:(t + 1) * 128, :], hl[tt][:], key=pfx + f"fhst{tt}",
                      tag=f"f{L}.hst{t}")
            p.flush(pfx + f"f{L}.s{s}", force=False)
        hcur[0] ^= 1
        p.arena_release(_ml)

    xa = [p.abuf(pfx + f"c1a{i}", [128, D], F32) for i in range(2)]
    xb = [p.abuf(pfx + f"c1b{i}", [128, D], F32) for i in range(2)]
    for t in range(T):
        par = t % 2
        p.dma("sync", xa[par][:], x_src[t * 128:(t + 1) * 128, :], key=pfx + f"c1a{par}",
              tag=f"c1.lda{t}")
        p.dma("scalar", xb[par][:], hrd()[t * 128:(t + 1) * 128, :], key=pfx + f"c1b{par}",
              tag=f"c1.ldb{t}")
        p.op("vector", lambda e, par=par: e.tensor_tensor(xa[par][:], xa[par][:], xb[par][:], ALU.add),
             reads=[xa[par], xb[par]], writes=[xa[par]], tag=f"c1.add{t}")
        p.dma("sync", out_dram[t * 128:(t + 1) * 128, :], xa[par][:], key=pfx + f"c1st{par}",
              tag=f"c1.st{t}")
    p.flush(pfx + "c1", force=False)
    p.arena_release(_amark)


def build_l1_ffn(depth=6, do_attn=True, do_ffn=True):
    nc = bass.Bass("TRN2", target_bir_lowering=False)
    p = Prog(nc)
    xin = p.din("x", [N, D], F32)
    out = p.dout("y", [N, D], F32)
    _load_shared_consts(p)
    emit_l1_ffn(p, "H_", xin, out,
                p.din("L1_ffn_Wqkv", [6, D, 3 * D], F32),
                p.din("L1_ffn_q_scale", [6, 64], F32), p.din("L1_ffn_k_scale", [6, 64], F32),
                p.din("L1_ffn_Wo", [6, D, D], F32),
                p.din("L1_ffn_ln_g", [6, D], F32), p.din("L1_ffn_ln_b", [6, D], F32),
                p.din("L1_ffn_W1", [6, D, 5460], F32),
                p.din("L1_ffn_b1a_pad", [6, 2816], F32), p.din("L1_ffn_b1g_pad", [6, 2816], F32),
                p.din("L1_ffn_W2", [6, 2730, D], F32), p.din("L1_ffn_b2", [6, D], F32),
                depth=depth, do_attn=do_attn, do_ffn=do_ffn)
    p.flush("tail", force=False)
    nc.finalize()
    return nc


def _load_shared_consts(p):
    idb = p.sbuf("identity_bf", [128, 128], BF16)
    idf = p.sbuf("identity_f32", [128, 128], F32)
    onec = p.sbuf("ones_col", [1, 128], F32)
    p.dma("sync", idb[:], p.din("identity_bf_in", [128, 128], BF16), key="idb")
    p.dma("sync", idf[:], p.din("identity_f32_in", [128, 128], F32), key="idf")
    p.dma("sync", onec[:], p.din("ones_col_in", [1, 128], F32), key="onec")
    onecb = p.sbuf("ones_col_bf", [1, 128], BF16)
    p.dma("gpsimd", onecb[:], p.din("ones_col_in", [1, 128], F32), key="onecb")


def const_inputs():
    c = host_consts()
    return {
        "identity_bf_in": c["identity_bf"],
        "identity_f32_in": c["identity_f32"],
        "ones_col_in": c["ones_col_f32"],
        "l0a_cos": c["l0a_cos"],
        "l0a_sin": c["l0a_sin"],
        "l0f_cos": c["l0f_cos"],
        "l0f_sin": c["l0f_sin"],
        "l1f_tcs": c["l1f_tcs"],
        "l1f_tss": c["l1f_tss"],
        "l1f_tci": c["l1f_tci"],
        "l1f_tsi": c["l1f_tsi"],
        "l1f_mle": c["l1f_mle"],
        "l1f_mge": c["l1f_mge"],
    }


def build_l0_attn():
    nc = bass.Bass("TRN2", target_bir_lowering=False)
    p = Prog(nc)
    x = p.din("x", [N, D], F32)
    out = p.dout("y", [N, D], F32)
    _load_shared_consts(p)
    W = {k: p.din(f"L0_attn_W{k}", [D, D], F32) for k in "qkvo"}
    bvec = {k: p.din(f"L0_attn_b{k}", [D], F32) for k in "qkvo"}
    cosd = p.din("l0a_cos", [N, 64], F32)
    sind = p.din("l0a_sin", [N, 64], F32)
    emit_full_attn(p, "A_", x, x, out, W, bvec, cosd, sind, rope_style="interleaved")
    p.flush("tail", force=False)
    nc.finalize()
    return nc


def const_inputs_for(nc):
    import concourse.mybir as _mb
    names = {a.memorylocations[0].name for a in nc.m.functions[0].allocations
             if isinstance(a, _mb.MemoryLocationSet) and a.kind == "ExternalInput"}
    return {k: v for k, v in const_inputs().items() if k in names}


def _b1_pads(b1):
    pa = np.zeros((6, 2816), dtype=np.float32)
    pg = np.zeros((6, 2816), dtype=np.float32)
    pa[:, :2730] = b1[:, :2730]
    pg[:, :2730] = b1[:, 2730:]
    return pa, pg


KERNEL_MODE = "fused"


def _filter_inputs(nc, full):
    import concourse.mybir as _mb
    names = {a.memorylocations[0].name for a in nc.m.functions[0].allocations
             if isinstance(a, _mb.MemoryLocationSet) and a.kind == "ExternalInput"}
    return {k: v for k, v in full.items() if k in names}


def kernel(**inputs):
    consts = const_inputs()
    pa, pg = _b1_pads(inputs["L1_ffn_b1"])
    full = dict(inputs)
    full["L1_ffn_b1a_pad"] = pa
    full["L1_ffn_b1g_pad"] = pg
    full.update(consts)
    xs = np.ascontiguousarray(inputs["L0_attn_hidden_states"])

    if KERNEL_MODE == "unfused":
        cur = [np.ascontiguousarray(xs[i]) for i in range(8)]
        for build in (build_l0_attn, build_l0_ffn, build_l1_attn, build_l1_ffn):
            nc = build()
            nc.finalize()
            base = _filter_inputs(nc, full)
            in_maps = [{**base, "x": cur[i]} for i in range(8)]
            res = run_bass_kernel_spmd(nc, in_maps, core_ids=list(range(8)))
            cur = [np.asarray(res.results[i]["y"]) for i in range(8)]
        return np.stack(cur).astype(np.float32)

    nc = build_fused()
    nc.finalize()
    base = _filter_inputs(nc, full)
    in_maps = [{**base, "x": np.ascontiguousarray(xs[i])} for i in range(8)]
    res = run_bass_kernel_spmd(nc, in_maps, core_ids=list(range(8)))
    return np.stack([np.asarray(res.results[i]["y"]) for i in range(8)]).astype(np.float32)


def build_fused():
    nc = bass.Bass("TRN2", target_bir_lowering=False)
    p = Prog(nc)
    x = p.din("x", [N, D], F32)
    y = p.dout("y", [N, D], F32)
    _load_shared_consts(p)
    xr1 = p.dscratch("xr1", [N, D], F32)
    xr2 = p.dscratch("xr2", [N, D], F32)
    xr3 = p.dscratch("xr3", [N, D], F32)
    W = {k: p.din(f"L0_attn_W{k}", [D, D], F32) for k in "qkvo"}
    bvec = {k: p.din(f"L0_attn_b{k}", [D], F32) for k in "qkvo"}
    emit_full_attn(p, "A_", x, x, xr1, W, bvec,
                   p.din("l0a_cos", [N, 64], F32), p.din("l0a_sin", [N, 64], F32),
                   rope_style="interleaved")
    W = {k[0]: p.din(f"L0_ffn_W{k}", [D, D], F32) for k in ("q", "k", "v", "o")}
    bvec = {k[0]: p.din(f"L0_ffn_b{k}", [D], F32) for k in ("q", "k", "v", "o")}
    xln = p.dscratch("F_xln", [N, D], F32)
    x1 = p.dscratch("F_x1", [N, D], F32)
    x1T = p.dscratch("F_x1T", [T * 8 * 128, 128], BF16)
    emit_full_attn(p, "F_", xr1, xln, x1, W, bvec,
                   p.din("l0f_cos", [N, 64], F32), p.din("l0f_sin", [N, 64], F32),
                   rope_style="half", xln_store=xln,
                   tail_ln=dict(g=p.din("L0_ffn_g1", [D], F32),
                                b=p.din("L0_ffn_be1", [D], F32), outT_hbm=x1T))
    emit_moe(p, "M_", x1, x1T, xr1, xr2,
             p.din("L0_ffn_Wg", [D, 8], F32), p.din("L0_ffn_bg", [8], F32),
             p.din("L0_ffn_W1", [8, D, 4096], F32), p.din("L0_ffn_b1", [8, 4096], F32),
             p.din("L0_ffn_W2", [8, 4096, D], F32), p.din("L0_ffn_b2", [8, D], F32),
             p.din("L0_ffn_g2", [D], F32), p.din("L0_ffn_be2", [D], F32))
    emit_l1_attn(p, "G_", xr2, xr3,
                 p.din("L1_attn_Wqkv", [D, 3 * D], F32), p.din("L1_attn_bqkv", [3 * D], F32),
                 p.din("L1_attn_Wout", [D, D], F32), p.din("L1_attn_bout", [D], F32),
                 p.din("L1_attn_Wgate", [D, D], F32), p.din("L1_attn_bgate", [D], F32),
                 p.din("L1_attn_pos_bias", [24, 16], F32))
    emit_l1_ffn(p, "H_", xr3, y,
                p.din("L1_ffn_Wqkv", [6, D, 3 * D], F32),
                p.din("L1_ffn_q_scale", [6, 64], F32), p.din("L1_ffn_k_scale", [6, 64], F32),
                p.din("L1_ffn_Wo", [6, D, D], F32),
                p.din("L1_ffn_ln_g", [6, D], F32), p.din("L1_ffn_ln_b", [6, D], F32),
                p.din("L1_ffn_W1", [6, D, 5460], F32),
                p.din("L1_ffn_b1a_pad", [6, 2816], F32), p.din("L1_ffn_b1g_pad", [6, 2816], F32),
                p.din("L1_ffn_W2", [6, 2730, D], F32), p.din("L1_ffn_b2", [6, D], F32))
    p.flush("tail", force=False)
    return nc
